# version 13
# speedup vs baseline: 1.1049x; 1.0336x over previous
import numpy as np
import concourse.bass as bass
import concourse.mybir as mybir
from concourse.bass_utils import run_bass_kernel_spmd
from concourse.alu_op_type import AluOpType as ALU
from contextlib import ExitStack

F32 = mybir.dt.float32
BF16 = mybir.dt.bfloat16
I32 = mybir.dt.int32
AF = mybir.ActivationFunctionType
AX = mybir.AxisListType

T = 4096
D = 1024
MAGIC = 12582912.0

L0_DI = 2048
L0_NH = 32
L0_HD = 64
L0_CONVD = 2304
L0_CH = 256
L0_NCH = T // L0_CH
FFN_I = 2816
L1_DI = 2048
L1_NST = 16
L1_DTR = 64


class Prog:
    ENGINES = ("sync", "vector", "scalar", "tensor", "gpsimd")

    def __init__(self, nc, stack, sems=("ld", "so", "pe", "dv", "ac", "po", "s2", "s3"), cut=None):
        self.cut = cut
        self.total = 0
        self.nc = nc
        self.q = {k: [] for k in self.ENGINES}
        self.sem = {}
        self.cnt = {}
        self._lastwait = {k: {} for k in self.ENGINES}
        for name in sems:
            self.sem[name] = stack.enter_context(nc.semaphore(name))
            self.cnt[name] = 0

    def wait(self, eng, sem, val=None):
        if val is None:
            val = self.cnt[sem]
        if val <= 0:
            return
        lw = self._lastwait[eng]
        if lw.get(sem, -1) >= val:
            return
        lw[sem] = val
        h = self.sem[sem]
        self.q[eng].append(lambda E, h=h, v=val: E.wait_ge(h, v))

    def op(self, eng, fn, sem=None, k=1, waits=()):
        if self.cut is not None and self.total >= self.cut:
            return self.cnt.get(sem, 0)
        self.total += 1
        for w in waits:
            if isinstance(w, tuple):
                self.wait(eng, w[0], w[1])
            else:
                self.wait(eng, w)
        if sem is not None:
            h = self.sem[sem]
            self.cnt[sem] += k
            self.q[eng].append(lambda E, fn=fn, h=h, k=k: fn(E).then_inc(h, k))
        else:
            self.q[eng].append(lambda E, fn=fn: fn(E))
        return self.cnt.get(sem, 0)

    def v(self, fn, waits=(), sem="dv", k=1):
        self.wait("vector", "dv", self.cnt["dv"])
        return self.op("vector", fn, sem, k, waits)

    def s(self, fn, waits=(), sem="ac", k=1):
        self.wait("scalar", "ac", self.cnt["ac"])
        return self.op("scalar", fn, sem, k, waits)

    def t(self, fn, waits=(), sem=None, k=1, base=0):
        if base != getattr(self, "_pe_base", 0):
            self.wait("tensor", "pe", self.cnt["pe"])
            self._pe_base = base
        return self.op("tensor", fn, sem, k, waits)

    def g(self, fn, waits=(), sem="po", k=1):
        self.wait("gpsimd", "po", self.cnt["po"])
        return self.op("gpsimd", fn, sem, k, waits)

    def dma(self, fn, waits=(), q="sync", sem="ld"):
        return self.op(q, fn, sem, 16, waits)

    def barrier(self):
        for eng in self.ENGINES:
            for sem in self.sem:
                self.wait(eng, sem)

    def setup_consts(self, stack, values):
        self.cst = stack.enter_context(self.nc.sbuf_tensor("cstp", [128, max(1, len(values))], F32))
        for i, val in enumerate(values):
            self.nc.const_aps.register(F32, val, self.cst[:, i:i + 1])
            self.g(lambda g, i=i, val=val: g.memset(self.cst[:, i:i + 1], val))
        for eng in ("vector", "scalar", "tensor"):
            self.wait(eng, "po")

    def finalize(self, blk):
        for name in self.ENGINES:
            ops = self.q[name]
            if not ops:
                continue
            def runner(E, ops=ops):
                for f in ops:
                    f(E)
            getattr(blk, name)(runner)


class _StopEmit(Exception):
    pass


def _mm(out, lhsT, rhs, start, stop):
    return lambda p: p.matmul(out=out, lhsT=lhsT, rhs=rhs, start=start, stop=stop)



def build_launch1(debug=False, upto="all", cut=None):
    nc = bass.Bass("TRN2", target_bir_lowering=False, detect_race_conditions=False)
    x0_h = nc.dram_tensor("x0", [T, D], F32, kind="ExternalInput").ap()
    win_h = nc.dram_tensor("w_in", [D, 4384], F32, kind="ExternalInput").ap()
    convw_h = nc.dram_tensor("conv_w", [L0_CONVD, 4], F32, kind="ExternalInput").ap()
    convb_h = nc.dram_tensor("conv_b", [L0_CONVD], F32, kind="ExternalInput").ap()
    dtb_h = nc.dram_tensor("dt_bias", [L0_NH], F32, kind="ExternalInput").ap()
    alog_h = nc.dram_tensor("a_log", [L0_NH], F32, kind="ExternalInput").ap()
    dsk_h = nc.dram_tensor("d_skip", [1, L0_NH], F32, kind="ExternalInput").ap()
    nw_h = nc.dram_tensor("norm_w", [1, L0_DI], F32, kind="ExternalInput").ap()
    wout_h = nc.dram_tensor("w_out", [L0_DI, D], F32, kind="ExternalInput").ap()

    xn_h = nc.dram_tensor("xn_fm", [D, T], BF16, kind="ExternalOutput").ap()
    xmax_h = nc.dram_tensor("xmax", [1, 1], F32, kind="ExternalOutput").ap()
    s0_h = nc.dram_tensor("s0_fm", [D, T], BF16, kind="ExternalOutput").ap()
    dbg = {}
    if debug:
        for nm, shp, dt in (("xbcs", [L0_CONVD, T], BF16), ("dtsp", [L0_NH, T], F32),
                            ("cs", [L0_NH, T], F32), ("y3", [L0_DI, T], BF16),
                            ("xz", [L0_CONVD, T], BF16), ("zz", [T, L0_DI], BF16),
                            ("yraw", [T, L0_DI], F32), ("st0", [128, L0_NH * L0_CH], BF16),
                            ("xssp", [T, L0_DI], BF16),
                            ("cbm0", [128, 2 * L0_CH], BF16), ("xtd0", [128, 2 * L0_DI], BF16),
                            ("csc0", [128, L0_NH * 128], BF16)):
            dbg[nm] = nc.dram_tensor("dbg_" + nm, shp, dt, kind="ExternalOutput").ap()

    xbc_sp = nc.dram_tensor("xbc_sp", [L0_CONVD, T], BF16, kind="Internal").ap()
    xbcs_sp = nc.dram_tensor("xbcs_sp", [L0_CONVD, T], BF16, kind="Internal").ap() if debug else None
    z_sp = nc.dram_tensor("z_sp", [T, L0_DI], BF16, kind="Internal").ap()
    xs_sp = nc.dram_tensor("xs_sp", [T, L0_DI], BF16, kind="Internal").ap()
    y3_sp = nc.dram_tensor("y3_sp", [L0_DI, T], BF16, kind="Internal").ap()

    with ExitStack() as st:
        e = st.enter_context
        P = Prog(nc, st, cut=cut)
        P.setup_consts(st, [1e-5])
        def _fin():
            cut0, P.cut = P.cut, None
            P.op("sync", lambda q: q.wait_ge(P.sem["so"], P.cnt["so"]))
            P.op("sync", lambda q: q.wait_ge(P.sem["ld"], P.cnt["ld"]))
            P.finalize(blk)

        identb = e(nc.sbuf_tensor("identb", [128, 128], BF16))
        ones1 = e(nc.sbuf_tensor("ones1", [1, 128], F32))
        identf = e(nc.sbuf_tensor("identf", [128, 128], F32))
        convw = e(nc.sbuf_tensor("convw", [128, 18 * 4], F32))
        convb = e(nc.sbuf_tensor("convb", [128, 18], F32))
        ndtb = e(nc.sbuf_tensor("ndtb", [L0_NH, 1], F32))
        acol = e(nc.sbuf_tensor("acol", [L0_NH, 1], F32))
        dbig = e(nc.sbuf_tensor("dbig", [128, L0_NH], F32))
        nbig = e(nc.sbuf_tensor("nbig", [128, L0_DI], BF16))
        drow = e(nc.sbuf_tensor("drow", [1, L0_NH], F32))
        nwrow = e(nc.sbuf_tensor("nwrow", [1, L0_DI], F32))
        fmp = e(nc.sbuf_tensor("fmp", [128, T], F32))
        dt_raw = fmp.ap()[0:32, :]
        dt_sp = fmp.ap()[32:64, :]
        cs_fm = fmp.ap()[64:96, :]
        b_fm = e(nc.sbuf_tensor("b_fm", [128, T], BF16))
        c_fm = e(nc.sbuf_tensor("c_fm", [128, T], BF16))
        mask0 = e(nc.sbuf_tensor("mask0", [128, L0_CH], F32))
        mask1 = e(nc.sbuf_tensor("mask1", [128, L0_CH], F32))
        onesc = e(nc.sbuf_tensor("onesc", [L0_NH, L0_CH], F32))
        blk = e(nc.Block())

        P.dma(lambda q: q.dma_start(out=convw[:, :].rearrange("p (s k) -> p s k", s=18), in_=convw_h.rearrange("(s p) k -> p s k", p=128)))
        P.dma(lambda q: q.dma_start(out=convb[:, :], in_=convb_h.rearrange("(s p) -> p s", p=128), allow_slow_non_contiguous=True))
        P.dma(lambda q: q.dma_start(out=ndtb[:, :], in_=dtb_h.unsqueeze(1)))
        P.dma(lambda q: q.dma_start(out=acol[:, :], in_=alog_h.unsqueeze(1)))
        P.dma(lambda q: q.dma_start(out=drow[:, :], in_=dsk_h))
        P.dma(lambda q: q.dma_start(out=nwrow[:, :], in_=nw_h))
        P.g(lambda g: g.memset(identb[:, :], 0.0))
        P.g(lambda g: g.affine_select(out=identb[:, :], in_=identb[:, :], compare_op=ALU.not_equal,
                                      fill=1.0, base=0, pattern=[[-1, 128]], channel_multiplier=1))
        P.g(lambda g: g.memset(identf[:, :], 0.0))
        P.g(lambda g: g.affine_select(out=identf[:, :], in_=identf[:, :], compare_op=ALU.not_equal,
                                      fill=1.0, base=0, pattern=[[-1, 128]], channel_multiplier=1))
        P.g(lambda g: g.memset(ones1[:, :], 1.0))
        P.g(lambda g: g.memset(onesc[:, :], 1.0))
        P.g(lambda g: g.memset(mask0[:, :], 0.0))
        P.g(lambda g: g.affine_select(out=mask0[:, :], in_=mask0[:, :], compare_op=ALU.is_gt,
                                      fill=1.0, base=0, pattern=[[-1, L0_CH]], channel_multiplier=1))
        P.g(lambda g: g.memset(mask1[:, :], 0.0))
        P.g(lambda g: g.affine_select(out=mask1[:, :], in_=mask1[:, :], compare_op=ALU.is_gt,
                                      fill=1.0, base=128, pattern=[[-1, L0_CH]], channel_multiplier=1))
        P.s(lambda s: s.activation(out=acol[:, :], in_=acol[:, :], func=AF.Exp), waits=["ld"])
        P.v(lambda v: v.tensor_scalar_mul(out=acol[:, :], in0=acol[:, :], scalar1=-1.0), waits=["ac"])
        P.v(lambda v: v.tensor_scalar_mul(out=ndtb[:, :], in0=ndtb[:, :], scalar1=-1.0), waits=["ld"])

        if True:
            stA = ExitStack()
            p_c0 = stA.enter_context(nc.psum_tensor("p_c0", [128, 512], F32))
            p_c1 = stA.enter_context(nc.psum_tensor("p_c1", [128, L0_DI], F32))
            P.t(_mm(p_c0[:, 0:L0_NH], ones1[0:1, :], drow[0:1, :], True, True),
                waits=["ld", "po"], sem="pe")
            for j in range(4):
                P.t(_mm(p_c1[:, j * 512:(j + 1) * 512], ones1[0:1, :],
                        nwrow[0:1, j * 512:(j + 1) * 512], True, True),
                    sem="pe", k=(1 if j == 3 else 0) or 1)
            P.v(lambda v: v.tensor_copy(out=dbig[:, :], in_=p_c0[:, 0:L0_NH]), waits=["pe"])
            P.v(lambda v: v.tensor_copy(out=nbig[:, :], in_=p_c1[:, :]))
            stA.close()

            with ExitStack() as stB:
                eb = stB.enter_context
                xn_fm = eb(nc.sbuf_tensor("xn_fm_s", [128, 8 * T], BF16))
                xt0 = eb(nc.sbuf_tensor("xt0", [128, D], F32))
                xt1 = eb(nc.sbuf_tensor("xt1", [128, D], F32))
                stat = eb(nc.sbuf_tensor("stat", [128, 8], F32))
                sqs = eb(nc.sbuf_tensor("sqs", [128, D], F32))
                xnt0 = eb(nc.sbuf_tensor("xnt0", [128, D], BF16))
                xnt1 = eb(nc.sbuf_tensor("xnt1", [128, D], BF16))
                xmc = eb(nc.sbuf_tensor("xmc", [128, 2], F32))
                xm1 = eb(nc.sbuf_tensor("xm1", [1, 1], F32))
                p_tr0 = eb(nc.psum_tensor("p_tr0", [128, 1024], BF16))
                p_tr1 = eb(nc.psum_tensor("p_tr1", [128, 1024], BF16))

                P.v(lambda v: v.memset(xmc[:, 0:1], 0.0))
                evac_tok = [0, 0]
                xnt_tok = [0, 0]
                xnfm_v = xn_fm.ap().rearrange("p (k t) -> p k t", k=8)
                for tt in range(32):
                    xt = xt0 if tt % 2 == 0 else xt1
                    xnt = xnt0 if tt % 2 == 0 else xnt1
                    ptr = p_tr0 if tt % 2 == 0 else p_tr1
                    par = tt % 2
                    P.dma(lambda q, xt=xt, tt=tt: q.dma_start(out=xt[:, :], in_=x0_h[tt * 128:(tt + 1) * 128, :]),
                          waits=[("dv", 0 if tt < 2 else dv_done[par]), ("ac", 0 if tt < 2 else ac_done[par])])
                    P.v(lambda v, xt=xt: v.reduce_sum(out=stat[:, 0:1], in_=xt[:, :], axis=AX.X), waits=["ld"])
                    P.s(lambda s, xt=xt: s.activation(out=sqs[:, :], in_=xt[:, :], func=AF.Square,
                                               accum_out=stat[:, 1:2]), waits=["ld"])
                    P.v(lambda v: v.tensor_scalar_mul(out=stat[:, 2:3], in0=stat[:, 0:1], scalar1=1.0 / D))
                    P.v(lambda v: v.tensor_tensor(out=stat[:, 3:4], in0=stat[:, 2:3], in1=stat[:, 2:3],
                                                  op=ALU.mult))
                    P.v(lambda v: v.tensor_scalar_mul(out=stat[:, 4:5], in0=stat[:, 1:2], scalar1=1.0 / D),
                        waits=["ac"])
                    P.v(lambda v: v.tensor_tensor(out=stat[:, 5:6], in0=stat[:, 4:5], in1=stat[:, 3:4],
                                                  op=ALU.subtract))
                    P.s(lambda s: s.activation(out=stat[:, 6:7], in_=stat[:, 5:6], func=AF.Sqrt,
                                               bias=1e-5), waits=["dv"])
                    P.v(lambda v: v.reciprocal(out=stat[:, 7:8], in_=stat[:, 6:7]), waits=["ac"])
                    P.wait("vector", "pe", xnt_tok[par])
                    if tt >= 2:
                        pass
                    dv_xn = P.v(lambda v, xt=xt, xnt=xnt: v.tensor_scalar(out=xnt[:, :], in0=xt[:, :], scalar1=stat[:, 2:3],
                                                          scalar2=stat[:, 7:8], op0=ALU.subtract, op1=ALU.mult))
                    dv_done_cur = P.v(lambda v, xnt=xnt: v.reduce_max(out=xmc[:, 1:2], in_=xnt[:, :], axis=AX.X,
                                                             apply_absolute_value=True))
                    P.v(lambda v: v.tensor_tensor(out=xmc[:, 0:1], in0=xmc[:, 0:1], in1=xmc[:, 1:2],
                                                  op=ALU.max))
                    if tt == 0:
                        dv_done = [0, 0]
                        ac_done = [0, 0]
                    dv_done[par] = dv_done_cur
                    ac_done[par] = P.cnt["ac"]
                    P.wait("tensor", "dv", dv_xn)
                    P.wait("tensor", "dv", evac_tok[par])
                    for k in range(8):
                        xnt_tok[par] = P.t(lambda p, k=k, ptr=ptr, xnt=xnt:
                                           p.transpose(out=ptr[:, k * 128:(k + 1) * 128],
                                                       in_=xnt[:, k * 128:(k + 1) * 128],
                                                       identity=identb[:, :]),
                                           sem="pe")
                    evac_tok[par] = P.v(lambda v, ptr=ptr, tt=tt:
                                        v.tensor_copy(out=xnfm_v[:, :, tt * 128:(tt + 1) * 128],
                                                      in_=ptr[:, :].rearrange("p (k t) -> p k t", k=8)),
                                        waits=["pe"])
                P.dma(lambda q: q.dma_start(out=xn_h.rearrange("(k p) t -> p k t", p=128),
                                            in_=xn_fm[:, :].rearrange("p (k t) -> p k t", k=8)), waits=[("dv", evac_tok[0]), ("dv", evac_tok[1])],
                      q="gpsimd", sem="so")
                P.g(lambda g: g.reduce_max(xm1[0:1, :], xmc[:, 0:1], axis=AX.C), waits=["dv"])
                P.dma(lambda q: q.dma_start(out=xmax_h, in_=xm1[0:1, 0:1]), waits=["po"],
                      q="gpsimd", sem="so")

                with ExitStack() as stC:
                    ec = stC.enter_context
                    wsl_f0 = ec(nc.sbuf_tensor("wsl_f0", [128, 1024], F32))
                    wsl_f1 = ec(nc.sbuf_tensor("wsl_f1", [128, 1024], F32))
                    wsl0 = ec(nc.sbuf_tensor("wsl0", [128, 1024], BF16))
                    wsl1 = ec(nc.sbuf_tensor("wsl1", [128, 1024], BF16))
                    ev0 = ec(nc.sbuf_tensor("ev0", [128, 512], BF16))
                    ev1 = ec(nc.sbuf_tensor("ev1", [128, 512], BF16))
                    p_m0 = ec(nc.psum_tensor("p_m0", [128, 512], F32))
                    p_m1 = ec(nc.psum_tensor("p_m1", [128, 512], F32))

                    cast_tok = [0, 0]
                    mm_tok = [0, 0]
                    ev_tok = [0, 0]
                    st_tok = [0, 0]
                    for r in range(19):
                        w = 128 if r < 18 else 32
                        c0 = 2048 + r * 128
                        rp = r % 2
                        wf = wsl_f0 if rp == 0 else wsl_f1
                        wb = wsl0 if rp == 0 else wsl1
                        cast_tok[rp] = P.dma(lambda q, wb=wb, c0=c0, w=w:
                                             q.dma_start(out=wb[:, 0:8 * w].rearrange("p (k c) -> p k c", k=8),
                                                         in_=win_h[:, c0:c0 + w].rearrange("(k p) c -> p k c", p=128)),
                                             waits=[("pe", mm_tok[rp])], q="gpsimd", sem="so")
                        P.wait("tensor", "so", cast_tok[rp])
                        for tb in range(8):
                            bp = tb % 2
                            pm = p_m0 if bp == 0 else p_m1
                            ev = ev0 if bp == 0 else ev1
                            P.wait("tensor", "dv", ev_tok[bp])
                            for k in range(8):
                                mm_tok[rp] = P.t(_mm(pm[0:w, :] if w < 128 else pm[:, :],
                                                     wb[:, k * w:(k + 1) * w],
                                                     xn_fm[:, k * T + tb * 512:k * T + (tb + 1) * 512],
                                                     k == 0, k == 7), sem="pe")
                            if r < 18:
                                ev_tok[bp] = P.v(lambda v, ev=ev, pm=pm: v.tensor_copy(out=ev[:, :], in_=pm[:, :]),
                                                 waits=["pe", ("so", st_tok[bp])])
                                st_tok[bp] = P.dma(lambda q, ev=ev, r=r, tb=tb:
                                                   q.dma_start(out=xbc_sp[r * 128:(r + 1) * 128,
                                                                          tb * 512:(tb + 1) * 512],
                                                               in_=ev[:, :]),
                                                   waits=["dv"], sem="so")
                            else:
                                ev_tok[bp] = P.v(lambda v, pm=pm, tb=tb:
                                                 v.tensor_copy(out=dt_raw[:, tb * 512:(tb + 1) * 512],
                                                               in_=pm[0:32, :]), waits=["pe"])

                if upto == "ln":
                    dbgst = nc.dram_tensor("dbg_stat", [128, 8], F32, kind="ExternalOutput").ap()
                    dbgxnt = nc.dram_tensor("dbg_xnt", [128, D], BF16, kind="ExternalOutput").ap()
                    P.dma(lambda q: q.dma_start(out=dbgst, in_=stat[:, :]),
                          waits=[("dv", P.cnt["dv"]), ("ac", P.cnt["ac"])], q="gpsimd", sem="so")
                    P.dma(lambda q: q.dma_start(out=dbgxnt, in_=xnt1[:, :]), q="gpsimd", sem="so")
                    _fin()
                    return nc
                P.barrier()
                with ExitStack() as stD:
                    ed = stD.enter_context
                    wz_f = ed(nc.sbuf_tensor("wz_f", [128, 4096], F32))
                    wz = ed(nc.sbuf_tensor("wz", [128, 4096], BF16))
                    evz0 = ed(nc.sbuf_tensor("evz0", [128, 512], BF16))
                    evz1 = ed(nc.sbuf_tensor("evz1", [128, 512], BF16))
                    p_z0 = ed(nc.psum_tensor("p_z0", [128, 512], F32))
                    p_z1 = ed(nc.psum_tensor("p_z1", [128, 512], F32))

                    cast_tok = 0
                    mm_last = 0
                    ev_tok = [0, 0]
                    st_tok = [0, 0]
                    for zc in range(4):
                        cast_tok = P.dma(lambda q, zc=zc: q.dma_start(
                            out=wz[:, :].rearrange("p (k c) -> p k c", k=8),
                            in_=win_h[:, zc * 512:(zc + 1) * 512].rearrange("(k p) c -> p k c", p=128)),
                            waits=[("pe", mm_last)], q="gpsimd", sem="so")
                        P.wait("tensor", "so", cast_tok)
                        for tt in range(32):
                            bp = tt % 2
                            pm = p_z0 if bp == 0 else p_z1
                            ev = evz0 if bp == 0 else evz1
                            P.wait("tensor", "ac", ev_tok[bp])
                            for k in range(8):
                                mm_last = P.t(_mm(pm[:, :],
                                                  xn_fm[:, k * T + tt * 128:k * T + (tt + 1) * 128],
                                                  wz[:, k * 512:(k + 1) * 512], k == 0, k == 7), sem="pe")
                            P.wait("scalar", "so", st_tok[bp])
                            ev_tok[bp] = P.s(lambda s, ev=ev, pm=pm: s.activation(
                                out=ev[:, :], in_=pm[:, :], func=AF.Silu), waits=["pe"])
                            st_tok[bp] = P.dma(lambda q, ev=ev, zc=zc, tt=tt:
                                               q.dma_start(out=z_sp[tt * 128:(tt + 1) * 128,
                                                                    zc * 512:(zc + 1) * 512],
                                                           in_=ev[:, :]),
                                               waits=["ac"], sem="so")

        if debug:
            P.dma(lambda q: q.dma_start(out=dbg["xz"], in_=xbc_sp), waits=[("so", P.cnt["so"])])
            P.dma(lambda q: q.dma_start(out=dbg["zz"], in_=z_sp))

        if upto == "z":
            _fin()
            return nc
        P.barrier()
        with ExitStack() as stE:
            ee = stE.enter_context
            convx0 = ee(nc.sbuf_tensor("convx0", [128, T + 3], BF16))
            convx1 = ee(nc.sbuf_tensor("convx1", [128, T + 3], BF16))
            convacc = ee(nc.sbuf_tensor("convacc", [128, T], F32))
            convo = ee(nc.sbuf_tensor("convo", [128, T], BF16))
            trst = ee(nc.sbuf_tensor("trst", [128, 1024], BF16))
            p_trE = ee(nc.psum_tensor("p_trE", [128, 1024], BF16))

            P.g(lambda g: g.memset(convx0[:, 0:3], 0.0))
            P.g(lambda g: g.memset(convx1[:, 0:3], 0.0))
            conv_tok = [0, 0]
            silu_tok = 0
            trs_tok = 0
            dma_tok = 0
            for s_ in range(18):
                sp = s_ % 2
                cvx = convx0 if sp == 0 else convx1
                P.dma(lambda q, cvx=cvx, s_=s_: q.dma_start(out=cvx[:, 3:3 + T],
                                                            in_=xbc_sp[s_ * 128:(s_ + 1) * 128, :]),
                      waits=[("ac", conv_tok[sp]), ("po", P.cnt["po"])])
                P.v(lambda v, cvx=cvx, s_=s_: v.tensor_scalar_mul(out=convacc[:, :], in0=cvx[:, 0:T],
                                                                  scalar1=convw[:, s_ * 4:s_ * 4 + 1]),
                    waits=["ld", ("ac", silu_tok)])
                for k in range(1, 4):
                    P.v(lambda v, cvx=cvx, s_=s_, k=k:
                        v.scalar_tensor_tensor(out=convacc[:, :], in0=cvx[:, k:k + T],
                                               scalar=convw[:, s_ * 4 + k:s_ * 4 + k + 1],
                                               in1=convacc[:, :], op0=ALU.mult, op1=ALU.add))
                prev_use = trs_tok if s_ <= 16 else P.cnt["dv"]
                conv_tok[sp] = P.s(lambda s, s_=s_: s.activation(out=convo[:, :], in_=convacc[:, :],
                                                                 func=AF.Silu, bias=convb[:, s_:s_ + 1]),
                                   waits=["dv", ("dv", prev_use)])
                silu_tok = conv_tok[sp]
                if s_ < 16:
                    if debug:
                        P.dma(lambda q, s_=s_: q.dma_start(out=dbg["xbcs"][s_ * 128:(s_ + 1) * 128, :],
                                                           in_=convo[:, :]), waits=["ac"], q="gpsimd", sem="so")
                    for g4 in range(4):
                        P.wait("tensor", "ac", conv_tok[sp])
                        P.wait("tensor", "dv", trs_tok)
                        for u in range(8):
                            tt = g4 * 8 + u
                            pe_tok = P.t(lambda p, u=u, tt=tt: p.transpose(
                                out=p_trE[:, u * 128:(u + 1) * 128],
                                in_=convo[:, tt * 128:(tt + 1) * 128], identity=identb[:, :]), sem="pe")
                        trs_tok = P.v(lambda v: v.tensor_copy(out=trst[:, :], in_=p_trE[:, :]),
                                      waits=["pe", ("so", dma_tok)])
                        dma_tok = P.dma(lambda q, g4=g4, s_=s_: q.dma_start(
                            out=xs_sp[g4 * 1024:(g4 + 1) * 1024, s_ * 128:(s_ + 1) * 128]
                                .rearrange("(u p) c -> p u c", p=128),
                            in_=trst[:, :].rearrange("p (u c) -> p u c", u=8)), waits=["dv"], q="gpsimd", sem="so")
                elif s_ == 16:
                    trs_tok = P.v(lambda v: v.tensor_copy(out=b_fm[:, :], in_=convo[:, :]), waits=["ac"])
                    if debug:
                        P.dma(lambda q: q.dma_start(out=dbg["xbcs"][2048:2176, :], in_=convo[:, :]),
                              waits=["ac"], q="gpsimd", sem="so")
                else:
                    trs_tok = P.v(lambda v: v.tensor_copy(out=c_fm[:, :], in_=convo[:, :]), waits=["ac"])
                    if debug:
                        P.dma(lambda q: q.dma_start(out=dbg["xbcs"][2176:2304, :], in_=convo[:, :]),
                              waits=["ac"], q="gpsimd", sem="so")

        if upto == "conv":
            _fin()
            return nc
        P.barrier()
        P.s(lambda s: s.activation(out=dt_sp, in_=dt_raw, func=AF.Sigmoid,
                                   scale=-1.0, bias=ndtb[:, 0:1]), waits=["dv"])
        P.s(lambda s: s.activation(out=cs_fm, in_=dt_sp, func=AF.Ln))
        P.v(lambda v: v.tensor_scalar(out=dt_sp, in0=cs_fm, scalar1=-90.0, scalar2=-1.0,
                                      op0=ALU.max, op1=ALU.mult), waits=["ac"])
        if debug:
            P.dma(lambda q: q.dma_start(out=dbg["dtsp"], in_=dt_sp), waits=["dv"], q="gpsimd", sem="so")
        P.v(lambda v: v.tensor_scalar_mul(out=dt_raw, in0=dt_sp, scalar1=acol[:, 0:1]))
        for c in range(L0_NCH):
            P.v(lambda v, c=c: v.tensor_tensor_scan(out=cs_fm[:, c * L0_CH:(c + 1) * L0_CH],
                                                    data0=onesc[:, :], data1=dt_raw[:, c * L0_CH:(c + 1) * L0_CH],
                                                    initial=0.0, op0=ALU.mult, op1=ALU.add))
        if debug:
            P.dma(lambda q: q.dma_start(out=dbg["cs"], in_=cs_fm), waits=["dv"], q="gpsimd", sem="so")

        if upto == "dt":
            _fin()
            return nc
        P.barrier()
        with ExitStack() as stG:
            eg = stG.enter_context
            cs_tm0 = eg(nc.sbuf_tensor("cs_tm0", [128, L0_NH], F32))
            cs_tm1 = eg(nc.sbuf_tensor("cs_tm1", [128, L0_NH], F32))
            ncs_tm0 = eg(nc.sbuf_tensor("ncs_tm0", [128, L0_NH], F32))
            ncs_tm1 = eg(nc.sbuf_tensor("ncs_tm1", [128, L0_NH], F32))
            dt_tm0 = eg(nc.sbuf_tensor("dt_tm0", [128, L0_NH], F32))
            dt_tm1 = eg(nc.sbuf_tensor("dt_tm1", [128, L0_NH], F32))
            dec_tm0 = eg(nc.sbuf_tensor("dec_tm0", [128, L0_NH], F32))
            dec_tm1 = eg(nc.sbuf_tensor("dec_tm1", [128, L0_NH], F32))
            dec_fm = eg(nc.sbuf_tensor("dec_fm", [L0_NH, L0_CH], F32))
            dec_sc = eg(nc.sbuf_tensor("dec_sc", [L0_NH, L0_CH], F32))
            csflat = eg(nc.sbuf_tensor("csflat", [1, L0_NH * L0_CH], F32))
            ecsb = eg(nc.sbuf_tensor("ecsb", [128, 512], F32))
            btm = eg(nc.sbuf_tensor("btm", [128, 2 * 128], BF16))
            xs_tm = eg(nc.sbuf_tensor("xs_tm", [128, 2 * L0_DI], BF16))
            xtd = eg(nc.sbuf_tensor("xtd", [128, 2 * L0_DI], BF16))
            xtdd = eg(nc.sbuf_tensor("xtdd", [128, 2 * L0_DI], BF16))
            cbm = eg(nc.sbuf_tensor("cbm", [128, 2 * L0_CH], BF16))
            e_sb = eg(nc.sbuf_tensor("e_sb", [128, 8 * L0_CH], BF16))
            s_t0 = eg(nc.sbuf_tensor("s_t0", [128, L0_NH * L0_CH], BF16))
            s_t1 = eg(nc.sbuf_tensor("s_t1", [128, L0_NH * L0_CH], BF16))
            csc = eg(nc.sbuf_tensor("csc", [128, L0_NH * 128], BF16))
            prev0 = eg(nc.sbuf_tensor("prev0", [128, L0_NH * L0_HD], F32))
            prev1 = eg(nc.sbuf_tensor("prev1", [128, L0_NH * L0_HD], F32))
            cdbig = eg(nc.sbuf_tensor("cdbig", [128, L0_NH], F32))
            prevb = eg(nc.sbuf_tensor("prevb", [128, L0_NH * L0_HD], BF16))
            ytile = eg(nc.sbuf_tensor("ytile", [128, L0_DI], F32))
            ztile = eg(nc.sbuf_tensor("ztile", [128, L0_DI], BF16))
            gsil = eg(nc.sbuf_tensor("gsil", [128, L0_DI], BF16))
            y2 = eg(nc.sbuf_tensor("y2", [128, L0_DI], F32))
            rmsc = eg(nc.sbuf_tensor("rmsc", [128, 4], F32))
            y3t = eg(nc.sbuf_tensor("y3t", [128, L0_DI], BF16))
            yfm = eg(nc.sbuf_tensor("yfm", [128, 2048], BF16))
            p_y = eg(nc.psum_tensor("p_y", [128, L0_DI], F32))
            p_b0 = eg(nc.psum_tensor("p_b0", [128, 512], F32))
            p_b1 = eg(nc.psum_tensor("p_b1", [128, 512], F32))
            p_cb = eg(nc.psum_tensor("p_cb", [128, L0_CH], F32))
            p_trG = eg(nc.psum_tensor("p_trG", [128, 1024], BF16))

            P.g(lambda g: g.memset(prev0[:, :], 0.0))
            P.g(lambda g: g.memset(prevb[:, :], 0.0))
            dt_ready = P.cnt["dv"]
            tok_xs_rd = 0
            tok_st_rd = 0
            tok_z_rd = 0
            tok_flat_rd = 0

            for c in range(1 if upto.startswith("ssd1") else L0_NCH):
                c0 = c * L0_CH
                prev_c = prev0 if c % 2 == 0 else prev1
                prev_n = prev1 if c % 2 == 0 else prev0

                P.v(lambda v, c0=c0: v.tensor_scalar(out=dec_sc[:, :], in0=cs_fm[:, c0:c0 + L0_CH],
                                                     scalar1=cs_fm[:, c0 + 255:c0 + 256],
                                                     scalar2=-1.0, op0=ALU.subtract, op1=ALU.mult),
                    waits=[("dv", dt_ready)])
                P.s(lambda s: s.activation(out=dec_fm[:, :], in_=dec_sc[:, :], func=AF.Exp), waits=["dv"])
                P.dma(lambda q, c0=c0: q.dma_start(out=csflat[:, :], in_=cs_fm[:, c0:c0 + L0_CH]),
                      waits=[("pe", tok_flat_rd)], q="gpsimd", sem="so")
                for jb, (cst, dtt, dect) in enumerate(((cs_tm0, dt_tm0, dec_tm0), (cs_tm1, dt_tm1, dec_tm1))):
                    P.wait("tensor", "ac", P.cnt["ac"])
                    P.wait("tensor", "dv", P.cnt["dv"])
                    P.t(lambda p, c0=c0, jb=jb: p.transpose(out=p_b0[:, 0:32],
                                                            in_=cs_fm[:, c0 + jb * 128:c0 + (jb + 1) * 128],
                                                            identity=identf[64:96, 64:96]), sem="pe", base=64)
                    P.t(lambda p, c0=c0, jb=jb: p.transpose(out=p_b0[:, 32:64],
                                                            in_=dt_sp[:, c0 + jb * 128:c0 + (jb + 1) * 128],
                                                            identity=identf[32:64, 32:64]), sem="pe", base=32)
                    P.t(lambda p, jb=jb: p.transpose(out=p_b0[:, 64:96],
                                                     in_=dec_fm[:, jb * 128:(jb + 1) * 128],
                                                     identity=identf[0:32, 0:32]), sem="pe")
                    P.v(lambda v, cst=cst: v.tensor_copy(out=cst[:, :], in_=p_b0[:, 0:32]), waits=["pe"])
                    P.v(lambda v, dtt=dtt: v.tensor_copy(out=dtt[:, :], in_=p_b0[:, 32:64]))
                    P.v(lambda v, dect=dect: v.tensor_copy(out=dect[:, :], in_=p_b0[:, 64:96]))
                    P.v(lambda v, cst=cst, ncst=(ncs_tm0 if jb == 0 else ncs_tm1):
                        v.tensor_scalar_mul(out=ncst[:, :], in0=cst[:, :], scalar1=-1.0))
                for jb in range(2):
                    P.wait("tensor", "dv", P.cnt["dv"])
                    P.t(lambda p, c0=c0, jb=jb: p.transpose(out=p_trG[:, 0:128],
                                                            in_=b_fm[:, c0 + jb * 128:c0 + (jb + 1) * 128],
                                                            identity=identb[:, :]), sem="pe")
                    P.v(lambda v, jb=jb: v.tensor_copy(out=btm[:, jb * 128:(jb + 1) * 128], in_=p_trG[:, 0:128]),
                        waits=["pe"])
                for jb in range(2):
                    P.dma(lambda q, c0=c0, jb=jb: q.dma_start(
                        out=xs_tm[:, jb * L0_DI:(jb + 1) * L0_DI],
                        in_=xs_sp[c0 + jb * 128:c0 + (jb + 1) * 128, :]),
                        waits=[("dv", tok_xs_rd)])
                    xsv = xs_tm.ap().rearrange("p (j h q) -> p j h q", j=2, h=L0_NH)
                    dtv = (dt_tm0 if jb == 0 else dt_tm1).ap().unsqueeze(2)
                    dcv = (dec_tm0 if jb == 0 else dec_tm1).ap().unsqueeze(2)
                    P.v(lambda v, jb=jb, xsv=xsv, dtv=dtv: v.tensor_tensor(
                        out=xtd.ap().rearrange("p (j h q) -> p j h q", j=2, h=L0_NH)[:, jb, :, :],
                        in0=xsv[:, jb, :, :], in1=dtv.broadcast_to([128, L0_NH, L0_HD]), op=ALU.mult),
                        waits=["ld", "dv", ("pe", P.cnt["pe"])])
                    P.v(lambda v, jb=jb, dcv=dcv: v.tensor_tensor(
                        out=xtdd.ap().rearrange("p (j h q) -> p j h q", j=2, h=L0_NH)[:, jb, :, :],
                        in0=xtd.ap().rearrange("p (j h q) -> p j h q", j=2, h=L0_NH)[:, jb, :, :],
                        in1=dcv.broadcast_to([128, L0_NH, L0_HD]), op=ALU.mult))

                if upto == "ssd1a":
                    _fin()
                    return nc
                for jb in range(2):
                    P.wait("tensor", "dv", P.cnt["dv"])
                    P.wait("tensor", "ac", P.cnt["ac"])
                    P.t(_mm(p_cb[:, :], b_fm[:, c0 + jb * 128:c0 + (jb + 1) * 128],
                            c_fm[:, c0:c0 + L0_CH], True, True), sem="pe")
                    P.v(lambda v, jb=jb: v.tensor_tensor(out=cbm[:, jb * L0_CH:(jb + 1) * L0_CH],
                                                         in0=p_cb[:, :],
                                                         in1=(mask0 if jb == 0 else mask1)[:, :], op=ALU.mult),
                        waits=["pe"])

                if upto == "ssd1b":
                    _fin()
                    return nc
                exp_tok = [0, 0]
                tok_esb_rd = P.cnt["dv"]
                for jb, s_t in enumerate((s_t0, s_t1)):
                    ncst = ncs_tm0 if jb == 0 else ncs_tm1
                    for hg in range(4):
                        P.wait("tensor", "so", P.cnt["so"])
                        P.wait("tensor", "ac", P.cnt["ac"])
                        P.wait("tensor", "dv", P.cnt["dv"])
                        for hh in range(8):
                            h = hg * 8 + hh
                            pb = p_b0 if h % 2 == 0 else p_b1
                            P.wait("tensor", "ac", exp_tok[h % 2])
                            P.t(_mm(pb[:, 0:L0_CH], ones1[0:1, :],
                                    csflat[0:1, h * L0_CH:(h + 1) * L0_CH], True, True), sem="pe")
                            exp_tok[h % 2] = P.s(lambda s, pb=pb, hh=hh, h=h, ncst=ncst: s.activation(
                                out=e_sb[:, hh * L0_CH:(hh + 1) * L0_CH], in_=pb[:, 0:L0_CH], func=AF.Exp,
                                bias=ncst[:, h:h + 1]), waits=["pe", ("dv", tok_esb_rd)])
                        P.v(lambda v: v.tensor_scalar_min(out=e_sb[:, :], in0=e_sb[:, :], scalar1=1e30),
                            waits=["ac"])
                        cbv = cbm.ap().rearrange("p (j i) -> p j i", j=2)[:, jb, :].unsqueeze(1)
                        P.v(lambda v, s_t=s_t, cbv=cbv, hg=hg: v.tensor_tensor(
                            out=s_t.ap().rearrange("p (h i) -> p h i", h=L0_NH)[:, hg * 8:(hg + 1) * 8, :],
                            in0=e_sb.ap().rearrange("p (h i) -> p h i", h=8),
                            in1=cbv.broadcast_to([128, 8, L0_CH]), op=ALU.mult),
                            waits=["ac", ("pe", tok_st_rd if hg == 0 and jb == 0 else 0)])
                        tok_esb_rd = P.cnt["dv"]

                if debug and c == 0:
                    P.dma(lambda q: q.dma_start(out=dbg["st0"], in_=s_t0[:, :]), waits=["dv"],
                          q="gpsimd", sem="so")
                    P.dma(lambda q: q.dma_start(out=dbg["cbm0"], in_=cbm[:, :]), q="gpsimd", sem="so")
                    P.dma(lambda q: q.dma_start(out=dbg["xtd0"], in_=xtd[:, :]), q="gpsimd", sem="so")
                    P.barrier()
                if upto == "ssd1c":
                    _fin()
                    return nc
                for ib in range(2):
                    csfv = csflat.ap().rearrange("a (h i) -> a h i", h=L0_NH)
                    for u in range(8):
                        pb = p_b0 if u % 2 == 0 else p_b1
                        P.wait("tensor", "so", P.cnt["so"])
                        P.wait("tensor", "ac", P.cnt["ac"])
                        P.t(_mm(pb[:, :], ones1[0:1, :],
                                csfv[:, u * 4:(u + 1) * 4, ib * 128:(ib + 1) * 128], True, True),
                            sem="pe")
                        P.s(lambda s, pb=pb: s.activation(out=ecsb[:, :], in_=pb[:, :], func=AF.Exp),
                            waits=["pe", ("dv", P.cnt["dv"])])
                        cv = c_fm[:, c0 + ib * 128:c0 + (ib + 1) * 128].unsqueeze(1)
                        P.v(lambda v, u=u, cv=cv: v.tensor_tensor(
                            out=csc.ap().rearrange("p (h i) -> p h i", h=L0_NH)[:, u * 4:(u + 1) * 4, :],
                            in0=ecsb.ap().rearrange("p (a i) -> p a i", a=4),
                            in1=cv.broadcast_to([128, 4, 128]), op=ALU.mult),
                            waits=["ac", ("pe", tok_st_rd if u == 0 else 0)])
                    if debug and c == 0 and ib == 0:
                        P.dma(lambda q: q.dma_start(out=dbg["csc0"], in_=csc[:, :]), waits=["dv"],
                              q="gpsimd", sem="so")
                        P.barrier()
                    P.wait("tensor", "dv", P.cnt["dv"])
                    for jb, s_t in enumerate((s_t0, s_t1)):
                        for h in range(L0_NH):
                            P.t(_mm(p_y[:, h * L0_HD:(h + 1) * L0_HD],
                                    s_t[:, h * L0_CH + ib * 128:h * L0_CH + ib * 128 + 128],
                                    xtd[:, jb * L0_DI + h * L0_HD:jb * L0_DI + (h + 1) * L0_HD],
                                    jb == 0 and h % 8 == 0, False), sem="pe")
                    for h in range(L0_NH):
                        P.t(_mm(p_y[:, h * L0_HD:(h + 1) * L0_HD],
                                csc[:, h * 128:(h + 1) * 128],
                                prevb[:, h * L0_HD:(h + 1) * L0_HD],
                                False, h % 8 == 7), sem="pe")
                    tok_st_rd = P.cnt["pe"]
                    P.dma(lambda q, c0=c0, ib=ib: q.dma_start(out=ztile[:, :],
                                                              in_=z_sp[c0 + ib * 128:c0 + (ib + 1) * 128, :]),
                          waits=[("dv", tok_z_rd)])
                    xsv = xs_tm.ap().rearrange("p (j h q) -> p j h q", j=2, h=L0_NH)
                    dbv = dbig.ap().unsqueeze(2)
                    tok_xs_rd = P.v(lambda v, xsv=xsv, ib=ib, dbv=dbv: v.tensor_tensor(
                        out=y2.ap().rearrange("p (h q) -> p h q", h=L0_NH),
                        in0=xsv[:, ib, :, :], in1=dbv.broadcast_to([128, L0_NH, L0_HD]), op=ALU.mult),
                        waits=["pe"])
                    P.v(lambda v: v.tensor_tensor(out=ytile[:, :], in0=y2[:, :], in1=p_y[:, :], op=ALU.add))
                    if debug:
                        P.dma(lambda q, c0=c0, ib=ib: q.dma_start(
                            out=dbg["yraw"][c0 + ib * 128:c0 + (ib + 1) * 128, :], in_=ytile[:, :]),
                            waits=["dv"], q="gpsimd", sem="so")
                        P.barrier()
                    tok_z_rd = P.v(lambda v: v.tensor_tensor(out=y2[:, :], in0=ytile[:, :], in1=ztile[:, :],
                                                             op=ALU.mult), waits=["ld"])
                    P.s(lambda s: s.activation(out=ytile[:, :], in_=y2[:, :], func=AF.Square,
                                               accum_out=rmsc[:, 0:1]), waits=["dv"])
                    P.s(lambda s: s.activation(out=rmsc[:, 1:2], in_=rmsc[:, 0:1], func=AF.Ln,
                                               scale=1.0 / L0_DI, bias=1e-5))
                    P.s(lambda s: s.activation(out=rmsc[:, 2:3], in_=rmsc[:, 1:2], func=AF.Exp,
                                               scale=-0.5))
                    P.wait("vector", "ac", P.cnt["ac"])
                    P.v(lambda v: v.tensor_scalar_mul(out=ytile[:, :], in0=y2[:, :], scalar1=rmsc[:, 2:3]))
                    P.v(lambda v: v.tensor_tensor(out=y3t[:, :], in0=ytile[:, :], in1=nbig[:, :], op=ALU.mult))

                    for g8 in range(2):
                        P.wait("tensor", "dv", P.cnt["dv"])
                        for u in range(8):
                            kk = g8 * 8 + u
                            P.t(lambda p, u=u, kk=kk: p.transpose(out=p_trG[:, u * 128:(u + 1) * 128],
                                                                  in_=y3t[:, kk * 128:(kk + 1) * 128],
                                                                  identity=identb[:, :]), sem="pe")
                        P.v(lambda v, g8=g8: v.tensor_copy(out=yfm[:, g8 * 1024:(g8 + 1) * 1024],
                                                           in_=p_trG[:, :]), waits=["pe"])
                    P.dma(lambda q, c0=c0, ib=ib: q.dma_start(
                        out=y3_sp[:, c0 + ib * 128:c0 + (ib + 1) * 128].rearrange("(u p) t -> p u t", p=128),
                        in_=yfm[:, :].rearrange("p (u t) -> p u t", u=16)), waits=["dv"], q="gpsimd", sem="so")

                if upto == "ssd1d":
                    _fin()
                    return nc
                P.wait("tensor", "dv", P.cnt["dv"])
                P.wait("tensor", "so", P.cnt["so"])
                for jb in range(2):
                    for h in range(L0_NH):
                        P.t(_mm(p_y[:, h * L0_HD:(h + 1) * L0_HD],
                                btm[:, jb * 128:(jb + 1) * 128],
                                xtdd[:, jb * L0_DI + h * L0_HD:jb * L0_DI + (h + 1) * L0_HD],
                                jb == 0 and h % 8 == 0, jb == 1 and h % 8 == 7), sem="pe")
                tok_flat_rd = P.t(_mm(p_cb[:, 0:L0_NH], ones1[0:1, :], csflat[0:1, 255::L0_CH], True, True), sem="pe") or P.cnt["pe"]
                P.s(lambda s: s.activation(out=cdbig[:, :], in_=p_cb[:, 0:L0_NH], func=AF.Exp),
                    waits=["pe"])
                cdv = cdbig.ap().unsqueeze(2)
                P.wait("vector", "ac", P.cnt["ac"])
                P.v(lambda v, prev_c=prev_c, prev_n=prev_n, cdv=cdv: v.tensor_tensor(
                    out=prev_n.ap().rearrange("p (h q) -> p h q", h=L0_NH),
                    in0=prev_c.ap().rearrange("p (h q) -> p h q", h=L0_NH),
                    in1=cdv.broadcast_to([128, L0_NH, L0_HD]), op=ALU.mult))
                P.v(lambda v, prev_n=prev_n: v.tensor_tensor(out=prev_n[:, :], in0=prev_n[:, :],
                                                             in1=p_y[:, :], op=ALU.add))
                P.v(lambda v, prev_n=prev_n: v.tensor_copy(out=prevb[:, :], in_=prev_n[:, :]))

        if upto in ("ssd", "ssd1"):
            _fin()
            return nc
        if debug:
            P.dma(lambda q: q.dma_start(out=dbg["y3"], in_=y3_sp), waits=[("so", P.cnt["so"])],
                  q="gpsimd", sem="so")
            P.dma(lambda q: q.dma_start(out=dbg["xssp"], in_=xs_sp), q="gpsimd", sem="so")
        P.barrier()
        with ExitStack() as stH:
            eh = stH.enter_context
            wo_f = eh(nc.sbuf_tensor("wo_f", [128, 2048], F32))
            wo = eh(nc.sbuf_tensor("wo", [128, 8 * 2048], BF16))
            yin = eh(nc.sbuf_tensor("yin", [128, 16 * 512], BF16))
            evo0 = eh(nc.sbuf_tensor("evo0", [128, 512], BF16))
            evo1 = eh(nc.sbuf_tensor("evo1", [128, 512], BF16))
            p_o0 = eh(nc.psum_tensor("p_o0", [128, 512], F32))
            p_o1 = eh(nc.psum_tensor("p_o1", [128, 512], F32))
            cast_tok = 0
            for ot in range(8):
                cast_tok = P.dma(lambda q, ot=ot: q.dma_start(
                    out=wo[:, ot * 2048:(ot + 1) * 2048].rearrange("p (k c) -> p k c", k=16),
                    in_=wout_h[:, ot * 128:(ot + 1) * 128].rearrange("(k p) c -> p k c", p=128)),
                    q="gpsimd", sem="so")
            ev_tok = [0, 0]
            st_tok = [0, 0]
            for tb in range(8):
                P.dma(lambda q, tb=tb: q.dma_start(
                    out=yin[:, :].rearrange("p (k t) -> p k t", k=16),
                    in_=y3_sp[:, tb * 512:(tb + 1) * 512].rearrange("(k p) t -> p k t", p=128)),
                    waits=[("pe", P.cnt["pe"]), ("so", P.cnt["so"])])
                for ot in range(8):
                    bp = ot % 2
                    pm = p_o0 if bp == 0 else p_o1
                    ev = evo0 if bp == 0 else evo1
                    P.wait("tensor", "ld", P.cnt["ld"])
                    P.wait("tensor", "so", cast_tok)
                    P.wait("tensor", "dv", ev_tok[bp])
                    for k in range(16):
                        P.t(_mm(pm[:, :], wo[:, ot * 2048 + k * 128:ot * 2048 + (k + 1) * 128],
                                yin[:, k * 512:(k + 1) * 512], k == 0, k == 15), sem="pe")
                    ev_tok[bp] = P.v(lambda v, ev=ev, pm=pm: v.tensor_copy(out=ev[:, :], in_=pm[:, :]),
                                     waits=["pe", ("so", st_tok[bp])])
                    st_tok[bp] = P.dma(lambda q, ev=ev, ot=ot, tb=tb: q.dma_start(
                        out=s0_h[ot * 128:(ot + 1) * 128, tb * 512:(tb + 1) * 512], in_=ev[:, :]),
                        waits=["dv"], q="gpsimd", sem="so")

        _fin()
    return nc




def build_launch2():
    nc = bass.Bass("TRN2", target_bir_lowering=False, detect_race_conditions=False)
    xn_h = nc.dram_tensor("xn_fm", [D, T], BF16, kind="ExternalInput").ap()
    xmax8_h = nc.dram_tensor("xmax8", [1, 8], F32, kind="ExternalInput").ap()
    clip_h = nc.dram_tensor("clip_x", [1, 1], F32, kind="ExternalInput").ap()
    wg_h = nc.dram_tensor("w_gate", [FFN_I, D], F32, kind="ExternalInput").ap()
    wu_h = nc.dram_tensor("w_up", [FFN_I, D], F32, kind="ExternalInput").ap()
    hpre_h = nc.dram_tensor("hpre_fm", [FFN_I, T], BF16, kind="ExternalOutput").ap()
    hmax_h = nc.dram_tensor("hmax", [1, 1], F32, kind="ExternalOutput").ap()

    with ExitStack() as st:
        e = st.enter_context
        P = Prog(nc, st)
        P.setup_consts(st, [1e-5])
        identb = e(nc.sbuf_tensor("identb", [128, 128], BF16))
        ones1 = e(nc.sbuf_tensor("ones1", [1, 128], F32))
        sc = e(nc.sbuf_tensor("sc", [1, 8], F32))
        invcol = e(nc.sbuf_tensor("invcol", [128, 1], F32))
        scol = e(nc.sbuf_tensor("scol", [128, 1], F32))
        xq = e(nc.sbuf_tensor("xq", [128, 8 * T], BF16))
        xqs = e(nc.sbuf_tensor("xqs", [128, T], F32))
        wr_f = e(nc.sbuf_tensor("wr_f", [128, D], F32))
        wq_g = e(nc.sbuf_tensor("wq_g", [128, D], BF16))
        wq_u = e(nc.sbuf_tensor("wq_u", [128, D], BF16))
        wsl_g = e(nc.sbuf_tensor("wsl_g", [128, 8 * 128], BF16))
        wsl_u = e(nc.sbuf_tensor("wsl_u", [128, 8 * 128], BF16))
        wstat = e(nc.sbuf_tensor("wstat", [128, 8], F32))
        scg = e(nc.sbuf_tensor("scg", [128, 1], F32))
        scu = e(nc.sbuf_tensor("scu", [128, 1], F32))
        gs = e(nc.sbuf_tensor("gs", [128, 512], F32))
        us = e(nc.sbuf_tensor("us", [128, 512], F32))
        hp0 = e(nc.sbuf_tensor("hp0", [128, 512], BF16))
        hp1 = e(nc.sbuf_tensor("hp1", [128, 512], BF16))
        hmc = e(nc.sbuf_tensor("hmc", [128, 2], F32))
        hm1 = e(nc.sbuf_tensor("hm1", [1, 1], F32))
        p_g0 = e(nc.psum_tensor("p_g0", [128, 512], F32))
        p_g1 = e(nc.psum_tensor("p_g1", [128, 512], F32))
        p_u0 = e(nc.psum_tensor("p_u0", [128, 512], F32))
        p_u1 = e(nc.psum_tensor("p_u1", [128, 512], F32))
        p_t = e(nc.psum_tensor("p_t", [128, 1024], BF16))
        p_bc = e(nc.psum_tensor("p_bc", [128, 8], F32))
        blk = e(nc.Block())

        P.g(lambda g: g.memset(identb[:, :], 0.0))
        P.g(lambda g: g.affine_select(out=identb[:, :], in_=identb[:, :], compare_op=ALU.not_equal,
                                      fill=1.0, base=0, pattern=[[-1, 128]], channel_multiplier=1))
        P.g(lambda g: g.memset(ones1[:, :], 1.0))
        P.v(lambda v: v.memset(hmc[:, 0:1], 0.0))
        P.dma(lambda q: q.dma_start(out=sc[0:1, 0:8], in_=xmax8_h))
        P.dma(lambda q: q.dma_start(out=sc[0:1, 0:1], in_=clip_h), q="gpsimd", sem="so")
        P.v(lambda v: v.reduce_max(out=wstat[0:1, 0:1], in_=sc[0:1, 0:8], axis=AX.X), waits=["ld"])
        P.s(lambda s: s.activation(out=wstat[0:1, 1:2], in_=sc[0:1, 0:1], func=AF.Sigmoid),
            waits=["so", "dv"])
        P.v(lambda v: v.tensor_tensor(out=wstat[0:1, 2:3], in0=wstat[0:1, 0:1], in1=wstat[0:1, 1:2],
                                      op=ALU.mult), waits=["ac"])
        P.v(lambda v: v.tensor_scalar_mul(out=wstat[0:1, 3:4], in0=wstat[0:1, 2:3], scalar1=1.0 / 127.0))
        P.v(lambda v: v.reciprocal(out=wstat[0:1, 4:5], in_=wstat[0:1, 3:4]))
        P.t(_mm(p_bc[:, 0:2], ones1[0:1, :], wstat[0:1, 3:5], True, True), waits=["dv", "po"], sem="pe")
        P.v(lambda v: v.tensor_copy(out=scol[:, :], in_=p_bc[:, 0:1]), waits=["pe"])
        P.v(lambda v: v.tensor_copy(out=invcol[:, :], in_=p_bc[:, 1:2]))
        for k in range(8):
            P.dma(lambda q, k=k: q.dma_start(out=xq[:, k * T:(k + 1) * T], in_=xn_h[k * 128:(k + 1) * 128, :]))
            P.v(lambda v, k=k: v.tensor_scalar(out=xqs[:, :], in0=xq[:, k * T:(k + 1) * T],
                                               scalar1=invcol[:, 0:1], scalar2=MAGIC,
                                               op0=ALU.mult, op1=ALU.add), waits=["ld"])
            P.v(lambda v: v.tensor_scalar(out=xqs[:, :], in0=xqs[:, :], scalar1=MAGIC, scalar2=127.0,
                                          op0=ALU.subtract, op1=ALU.min))
            P.v(lambda v, k=k: v.tensor_scalar(out=xq[:, k * T:(k + 1) * T], in0=xqs[:, :],
                                               scalar1=-128.0, scalar2=None if False else 0.0,
                                               op0=ALU.max, op1=ALU.add))
        FFN_RT = FFN_I // 128
        ev_par = [0, 0]
        st_par = [0, 0]
        hp_tok = 0
        for rt in range(FFN_RT):
            for wsrc, wdst, slab, scl in ((wg_h, wq_g, wsl_g, scg), (wu_h, wq_u, wsl_u, scu)):
                P.dma(lambda q, wsrc=wsrc, rt=rt: q.dma_start(
                    out=wr_f[:, :], in_=wsrc[rt * 128:(rt + 1) * 128, :]),
                    waits=[("dv", P.cnt["dv"])])
                P.v(lambda v: v.reduce_max(out=wstat[:, 5:6], in_=wr_f[:, :], axis=AX.X,
                                           apply_absolute_value=True), waits=["ld"])
                P.v(lambda v: v.reciprocal(out=wstat[:, 6:7], in_=wstat[:, 5:6]))
                P.v(lambda v: v.tensor_scalar_mul(out=wstat[:, 6:7], in0=wstat[:, 6:7], scalar1=127.0))
                P.v(lambda v, scl=scl: v.tensor_scalar(out=scl[:, :], in0=wstat[:, 5:6],
                                                       scalar1=1.0 / 127.0, scalar2=scol[:, 0:1],
                                                       op0=ALU.mult, op1=ALU.mult))
                P.v(lambda v: v.tensor_scalar(out=xqs[:, 0:D], in0=wr_f[:, :], scalar1=wstat[:, 6:7],
                                              scalar2=MAGIC, op0=ALU.mult, op1=ALU.add))
                P.v(lambda v: v.tensor_scalar(out=xqs[:, 0:D], in0=xqs[:, 0:D], scalar1=MAGIC,
                                              scalar2=127.0, op0=ALU.subtract, op1=ALU.min))
                P.v(lambda v, wdst=wdst: v.tensor_scalar(out=wdst[:, :], in0=xqs[:, 0:D],
                                                         scalar1=-128.0, scalar2=0.0,
                                                         op0=ALU.max, op1=ALU.add))
                for g8 in range(1):
                    P.wait("tensor", "dv", P.cnt["dv"])
                    for u in range(8):
                        P.t(lambda p, u=u, wdst=wdst: p.transpose(out=p_t[:, u * 128:(u + 1) * 128],
                                                                  in_=wdst[:, u * 128:(u + 1) * 128],
                                                                  identity=identb[:, :]), sem="pe")
                    P.v(lambda v, slab=slab: v.tensor_copy(out=slab[:, :], in_=p_t[:, :]), waits=["pe"])
            for tb in range(8):
                bp = tb % 2
                pg = p_g0 if bp == 0 else p_g1
                pu = p_u0 if bp == 0 else p_u1
                hp = hp0 if bp == 0 else hp1
                P.wait("tensor", "dv", P.cnt["dv"])
                P.wait("tensor", "ac", ev_par[bp])
                for k in range(8):
                    P.t(_mm(pg[:, :], wsl_g[:, k * 128:(k + 1) * 128],
                            xq[:, k * T + tb * 512:k * T + (tb + 1) * 512], k == 0, k == 7), sem="pe")
                for k in range(8):
                    P.t(_mm(pu[:, :], wsl_u[:, k * 128:(k + 1) * 128],
                            xq[:, k * T + tb * 512:k * T + (tb + 1) * 512], k == 0, k == 7), sem="pe")
                ev_par[bp] = P.s(lambda s, pg=pg: s.activation(out=gs[:, :], in_=pg[:, :], func=AF.Silu,
                                                               scale=scg[:, 0:1]),
                                 waits=["pe", ("dv", hp_tok)])
                P.v(lambda v, pu=pu: v.tensor_scalar_mul(out=us[:, :], in0=pu[:, :], scalar1=scu[:, 0:1]),
                    waits=["pe"])
                hp_tok = P.v(lambda v, hp=hp: v.tensor_tensor(out=hp[:, :], in0=gs[:, :], in1=us[:, :],
                                                               op=ALU.mult),
                             waits=["ac", ("so", st_par[bp])])
                P.v(lambda v, hp=hp: v.reduce_max(out=hmc[:, 1:2], in_=hp[:, :], axis=AX.X,
                                                  apply_absolute_value=True))
                P.v(lambda v: v.tensor_tensor(out=hmc[:, 0:1], in0=hmc[:, 0:1], in1=hmc[:, 1:2], op=ALU.max))
                st_par[bp] = P.dma(lambda q, hp=hp, rt=rt, tb=tb: q.dma_start(
                    out=hpre_h[rt * 128:(rt + 1) * 128, tb * 512:(tb + 1) * 512], in_=hp[:, :]),
                    waits=["dv"], q="gpsimd", sem="so")
        P.g(lambda g: g.reduce_max(hm1[0:1, :], hmc[:, 0:1], axis=AX.C), waits=["dv"])
        P.dma(lambda q: q.dma_start(out=hmax_h, in_=hm1[0:1, 0:1]), waits=["po"], q="gpsimd", sem="so")
        P.op("sync", lambda q: q.wait_ge(P.sem["so"], P.cnt["so"]))
        P.op("sync", lambda q: q.wait_ge(P.sem["ld"], P.cnt["ld"]))
        P.finalize(blk)
    return nc




def build_launch3(debug=False, cut=None):
    nc = bass.Bass("TRN2", target_bir_lowering=False, detect_race_conditions=False)
    x0_h = nc.dram_tensor("x0", [T, D], F32, kind="ExternalInput").ap()
    s0_h = nc.dram_tensor("s0_fm", [D, T], BF16, kind="ExternalInput").ap()
    hpre_h = nc.dram_tensor("hpre_fm", [FFN_I, T], BF16, kind="ExternalInput").ap()
    hmax8_h = nc.dram_tensor("hmax8", [1, 8], F32, kind="ExternalInput").ap()
    clipd_h = nc.dram_tensor("clip_down", [1, 1], F32, kind="ExternalInput").ap()
    wdown_h = nc.dram_tensor("w_down", [D, FFN_I], F32, kind="ExternalInput").ap()
    lng_h = nc.dram_tensor("ln_gamma", [1, D], F32, kind="ExternalInput").ap()
    lnb_h = nc.dram_tensor("ln_beta", [1, D], F32, kind="ExternalInput").ap()
    inw_h = {d: nc.dram_tensor(f"in_w_{d}", [2 * L1_DI, D], F32, kind="ExternalInput").ap() for d in "fb"}
    cw_h = {d: nc.dram_tensor(f"conv_w_{d}", [L1_DI, 4], F32, kind="ExternalInput").ap() for d in "fb"}
    cb_h = {d: nc.dram_tensor(f"conv_b_{d}", [L1_DI], F32, kind="ExternalInput").ap() for d in "fb"}
    xpw_h = {d: nc.dram_tensor(f"xp_w_{d}", [96, L1_DI], F32, kind="ExternalInput").ap() for d in "fb"}
    dtpw_h = {d: nc.dram_tensor(f"dtp_w_{d}", [L1_DI, L1_DTR], F32, kind="ExternalInput").ap() for d in "fb"}
    dtpb_h = {d: nc.dram_tensor(f"dtp_b_{d}", [L1_DI], F32, kind="ExternalInput").ap() for d in "fb"}
    alog_h = {d: nc.dram_tensor(f"a_log_{d}", [L1_DI, L1_NST], F32, kind="ExternalInput").ap() for d in "fb"}
    dsk_h = {d: nc.dram_tensor(f"d_{d}", [L1_DI], F32, kind="ExternalInput").ap() for d in "fb"}
    outw_h = {d: nc.dram_tensor(f"out_w_{d}", [D, L1_DI], F32, kind="ExternalInput").ap() for d in "fb"}
    projw_h = nc.dram_tensor("proj_w", [D, 2 * D], F32, kind="ExternalInput").ap()
    projb_h = nc.dram_tensor("proj_b", [1, D], F32, kind="ExternalInput").ap()
    fw_h = nc.dram_tensor("ffn_w", [D, D], F32, kind="ExternalInput").ap()
    fb_h = nc.dram_tensor("ffn_b", [1, D], F32, kind="ExternalInput").ap()
    dw_h = nc.dram_tensor("delta_w", [D, 32], F32, kind="ExternalInput").ap()
    da_h = nc.dram_tensor("delta_act", [1, 1], F32, kind="ExternalInput").ap()
    x2_h = nc.dram_tensor("x2", [T, D], F32, kind="ExternalOutput").ap()

    x1_sp = nc.dram_tensor("x1_sp", [T, D], F32, kind="Internal").ap()
    xn1t_sp = nc.dram_tensor("xn1t_sp", [T, D], BF16, kind="Internal").ap()
    xn1f_sp = nc.dram_tensor("xn1f_sp", [D, T], BF16, kind="Internal").ap()
    z_sp = {d: nc.dram_tensor(f"zsp_{d}", [L1_DI, T], BF16, kind="Internal").ap() for d in "fb"}
    xcp_sp = {d: nc.dram_tensor(f"xcpsp_{d}", [L1_DI, T], BF16, kind="Internal").ap() for d in "fb"}
    xc_sp = {d: nc.dram_tensor(f"xcsp_{d}", [L1_DI, T], BF16, kind="Internal").ap() for d in "fb"}
    bc_sp = {d: nc.dram_tensor(f"bcsp_{d}", [32, T], BF16, kind="Internal").ap() for d in "fb"}
    yd_sp = {d: nc.dram_tensor(f"ydsp_{d}", [L1_DI, T], BF16, kind="Internal").ap() for d in "fb"}
    yp_sp = {d: nc.dram_tensor(f"ypsp_{d}", [D, T], BF16, kind="Internal").ap() for d in "fb"}
    yat_sp = nc.dram_tensor("yat_sp", [D, T], BF16, kind="Internal").ap()
    dbg = {}
    if debug:
        for nm, shp, dt in (("x1", [T, D], F32), ("xln", [D, T], BF16), ("xc_f", [L1_DI, T], BF16),
                            ("dtb_f", [96, T], F32), ("y_f", [L1_DI, T], BF16), ("yp_f", [D, T], BF16),
                            ("yat", [D, T], BF16), ("g1", [D, T], BF16), ("xc_b", [L1_DI, T], BF16),
                            ("y_b", [L1_DI, T], BF16)):
            dbg[nm] = nc.dram_tensor("dbg_" + nm, shp, dt, kind="ExternalOutput").ap()

    with ExitStack() as st:
        e = st.enter_context
        P = Prog(nc, st, cut=cut)
        P.setup_consts(st, [1e-5, 1.0])
        identb = e(nc.sbuf_tensor("identb", [128, 128], BF16))
        identf = e(nc.sbuf_tensor("identf", [128, 128], F32))
        ones1 = e(nc.sbuf_tensor("ones1", [1, 128], F32))
        ones1b = e(nc.sbuf_tensor("ones1b", [1, 128], BF16))
        sc = e(nc.sbuf_tensor("sc", [1, 8], F32))
        st8 = e(nc.sbuf_tensor("st8", [128, 8], F32))
        invh = e(nc.sbuf_tensor("invh", [128, 1], F32))
        sh = e(nc.sbuf_tensor("sh", [128, 1], F32))
        pbcols = e(nc.sbuf_tensor("pbcols", [128, 8], F32))
        fbcols = e(nc.sbuf_tensor("fbcols", [128, 8], F32))
        dacol = e(nc.sbuf_tensor("dacol", [128, 1], F32))
        blk = e(nc.Block())
        xst = ExitStack()
        gbig = xst.enter_context(nc.sbuf_tensor("gbig", [128, D], F32))
        bbig = xst.enter_context(nc.sbuf_tensor("bbig", [128, D], F32))

        P.g(lambda g: g.memset(identb[:, :], 0.0))
        P.g(lambda g: g.affine_select(out=identb[:, :], in_=identb[:, :], compare_op=ALU.not_equal,
                                      fill=1.0, base=0, pattern=[[-1, 128]], channel_multiplier=1))
        P.g(lambda g: g.memset(identf[:, :], 0.0))
        P.g(lambda g: g.affine_select(out=identf[:, :], in_=identf[:, :], compare_op=ALU.not_equal,
                                      fill=1.0, base=0, pattern=[[-1, 128]], channel_multiplier=1))
        P.g(lambda g: g.memset(ones1[:, :], 1.0))
        P.g(lambda g: g.memset(ones1b[:, :], 1.0))
        P.dma(lambda q: q.dma_start(out=sc[0:1, 0:8], in_=hmax8_h))
        P.dma(lambda q: q.dma_start(out=st8[0:1, 0:1], in_=clipd_h))
        P.dma(lambda q: q.dma_start(out=st8[0:1, 1:2], in_=da_h))
        P.v(lambda v: v.reduce_max(out=st8[0:1, 2:3], in_=sc[0:1, 0:8], axis=AX.X), waits=["ld"])
        P.s(lambda s: s.activation(out=st8[0:1, 3:4], in_=st8[0:1, 0:1], func=AF.Sigmoid), waits=["ld", "dv"])
        P.s(lambda s: s.activation(out=st8[0:1, 4:5], in_=st8[0:1, 1:2], func=AF.Tanh, scale=4.0))
        P.v(lambda v: v.tensor_scalar_mul(out=st8[0:1, 4:5], in0=st8[0:1, 4:5], scalar1=0.5), waits=["ac"])
        P.v(lambda v: v.tensor_tensor(out=st8[0:1, 5:6], in0=st8[0:1, 2:3], in1=st8[0:1, 3:4], op=ALU.mult))
        P.v(lambda v: v.tensor_scalar_mul(out=st8[0:1, 5:6], in0=st8[0:1, 5:6], scalar1=1.0 / 127.0))
        P.v(lambda v: v.reciprocal(out=st8[0:1, 6:7], in_=st8[0:1, 5:6]))
        with ExitStack() as s0s:
            p_b = s0s.enter_context(nc.psum_tensor("p_bc3", [128, 16], F32))
            P.t(_mm(p_b[:, 0:3], ones1[0:1, :], st8[0:1, 4:7], True, True), waits=["dv", "po"], sem="pe")
            P.v(lambda v: v.tensor_copy(out=dacol[:, :], in_=p_b[:, 0:1]), waits=["pe"])
            P.v(lambda v: v.tensor_copy(out=sh[:, :], in_=p_b[:, 1:2]))
            P.v(lambda v: v.tensor_copy(out=invh[:, :], in_=p_b[:, 2:3]))
            gb = s0s.enter_context(nc.sbuf_tensor("gbrow", [1, D], F32))
            bb = s0s.enter_context(nc.sbuf_tensor("bbrow", [1, D], F32))
            P.dma(lambda q: q.dma_start(out=gb[0:1, :], in_=lng_h))
            P.dma(lambda q: q.dma_start(out=bb[0:1, :], in_=lnb_h))
            with nc.psum_tensor("p_gb", [128, 512], F32) as p_gb:
                for j in range(2):
                    P.t(_mm(p_gb[:, :], ones1[0:1, :], gb[0:1, j * 512:(j + 1) * 512], True, True),
                        waits=["ld", ("dv", P.cnt["dv"])], sem="pe")
                    P.v(lambda v, j=j: v.tensor_copy(out=gbig[:, j * 512:(j + 1) * 512], in_=p_gb[:, :]),
                        waits=["pe"])
                    P.t(_mm(p_gb[:, :], ones1[0:1, :], bb[0:1, j * 512:(j + 1) * 512], True, True),
                        waits=[("dv", P.cnt["dv"])], sem="pe")
                    P.v(lambda v, j=j: v.tensor_copy(out=bbig[:, j * 512:(j + 1) * 512], in_=p_gb[:, :]),
                        waits=["pe"])
                P.dma(lambda q: q.dma_start(out=pbcols[:, :], in_=projb_h.rearrange("a (o p) -> p (a o)", p=128),
                                            allow_slow_non_contiguous=True))
                P.dma(lambda q: q.dma_start(out=fbcols[:, :], in_=fb_h.rearrange("a (o p) -> p (a o)", p=128),
                                            allow_slow_non_contiguous=True))

        xln_fm = xst.enter_context(nc.sbuf_tensor("xln_fm", [128, 8 * T], BF16))
        with ExitStack() as s1s:
            e1 = s1s.enter_context
            wd_f0 = e1(nc.sbuf_tensor("wd_f0", [128, FFN_I], F32))
            wd_f1 = wd_f0
            wdq = e1(nc.sbuf_tensor("wdq", [128, FFN_I], BF16))
            wdall = e1(nc.sbuf_tensor("wdall", [128, 8 * FFN_I], BF16))
            scd = e1(nc.sbuf_tensor("scd", [128, 8], F32))
            wst = e1(nc.sbuf_tensor("wst", [128, 4], F32))
            hq = e1(nc.sbuf_tensor("hq", [128, 22 * 512], BF16))
            hqf = e1(nc.sbuf_tensor("hqf", [128, 2 * 512], F32))
            g0tb = e1(nc.sbuf_tensor("g0tb", [128, 8 * 512], BF16))
            s0tb = e1(nc.sbuf_tensor("s0tb", [128, 8 * 512], BF16))
            inctm = e1(nc.sbuf_tensor("inctm", [128, 4 * D], BF16))
            x0t = e1(nc.sbuf_tensor("x0t", [128, 4 * D], F32))
            x1t = x0t
            lst = e1(nc.sbuf_tensor("lst", [128, 8], F32))
            xn1t = e1(nc.sbuf_tensor("xn1t", [128, D], BF16))
            xrw = hqf
            sqs3 = g0tb.ap().bitcast(F32)[:, 0:D]
            xlnt = e1(nc.sbuf_tensor("xlnt", [128, D], BF16))
            p_m0 = e1(nc.psum_tensor("p3_m0", [128, 512], F32))
            p_m1 = e1(nc.psum_tensor("p3_m1", [128, 512], F32))
            p_t3 = e1(nc.psum_tensor("p3_t", [128, 1024], BF16))

            wdrd = [0, 0]
            for rt in range(8):
                wd_f = wd_f0 if rt % 2 == 0 else wd_f1
                P.dma(lambda q, rt=rt, wd_f=wd_f: q.dma_start(
                    out=wd_f[:, :], in_=wdown_h[rt * 128:(rt + 1) * 128, :]),
                    waits=[("dv", wdrd[(rt + 1) % 2])])
                P.v(lambda v, wd_f=wd_f: v.reduce_max(out=wst[:, 0:1], in_=wd_f[:, :], axis=AX.X,
                                                      apply_absolute_value=True), waits=["ld"])
                P.v(lambda v: v.reciprocal(out=wst[:, 1:2], in_=wst[:, 0:1]))
                P.v(lambda v: v.tensor_scalar_mul(out=wst[:, 1:2], in0=wst[:, 1:2], scalar1=127.0))
                P.v(lambda v, rt=rt: v.tensor_scalar(out=scd[:, rt:rt + 1], in0=wst[:, 0:1],
                                                     scalar1=1.0 / 127.0, scalar2=sh[:, 0:1],
                                                     op0=ALU.mult, op1=ALU.mult))
                P.v(lambda v, wd_f=wd_f: v.tensor_scalar(out=wd_f[:, :], in0=wd_f[:, :], scalar1=wst[:, 1:2],
                                              scalar2=MAGIC, op0=ALU.mult, op1=ALU.add))
                P.v(lambda v, wd_f=wd_f: v.tensor_scalar(out=wd_f[:, :], in0=wd_f[:, :], scalar1=MAGIC, scalar2=127.0,
                                              op0=ALU.subtract, op1=ALU.min))
                wdrd[rt % 2] = P.v(lambda v, wd_f=wd_f: v.tensor_scalar(out=wdq[:, :], in0=wd_f[:, :], scalar1=-128.0, scalar2=0.0,
                                              op0=ALU.max, op1=ALU.add),
                                   waits=[("pe", P.cnt["pe"])])
                for g8 in range(3):
                    nu = 8 if g8 < 2 else 6
                    P.wait("tensor", "dv", P.cnt["dv"])
                    for u in range(nu):
                        kk = g8 * 8 + u
                        P.t(lambda p, u=u, kk=kk: p.transpose(out=p_t3[:, u * 128:(u + 1) * 128],
                                                              in_=wdq[:, kk * 128:(kk + 1) * 128],
                                                              identity=identb[:, :]), sem="pe")
                    P.v(lambda v, rt=rt, g8=g8, nu=nu: v.tensor_copy(
                        out=wdall[:, rt * FFN_I + g8 * 1024:rt * FFN_I + g8 * 1024 + nu * 128],
                        in_=p_t3[:, 0:nu * 128]), waits=["pe"])

            for tb in range(8):
                pe_war = P.cnt["pe"]
                for kc in range(22):
                    P.dma(lambda q, kc=kc, tb=tb: q.dma_start(
                        out=hq[:, kc * 512:(kc + 1) * 512],
                        in_=hpre_h[kc * 128:(kc + 1) * 128, tb * 512:(tb + 1) * 512]),
                        waits=[("pe", pe_war)])
                    if kc % 2 == 1 or kc == 21:
                        g0 = (kc // 2) * 2
                        w = (kc - g0 + 1) * 512
                        P.v(lambda v, g0=g0, w=w: v.tensor_scalar(
                            out=hqf[:, 0:w], in0=hq[:, g0 * 512:g0 * 512 + w],
                            scalar1=invh[:, 0:1], scalar2=MAGIC,
                            op0=ALU.mult, op1=ALU.add), waits=["ld"])
                        P.v(lambda v, w=w: v.tensor_scalar(out=hqf[:, 0:w], in0=hqf[:, 0:w],
                                                           scalar1=MAGIC, scalar2=127.0,
                                                           op0=ALU.subtract, op1=ALU.min))
                        P.v(lambda v, g0=g0, w=w: v.tensor_scalar(
                            out=hq[:, g0 * 512:g0 * 512 + w], in0=hqf[:, 0:w],
                            scalar1=-128.0, scalar2=0.0, op0=ALU.max, op1=ALU.add))
                for ot in range(8):
                    pm = p_m0 if ot % 2 == 0 else p_m1
                    P.wait("tensor", "dv", P.cnt["dv"])
                    P.wait("tensor", "ac", P.cnt["ac"])
                    for k in range(22):
                        P.t(_mm(pm[:, :], wdall[:, ot * FFN_I + k * 128:ot * FFN_I + (k + 1) * 128],
                                hq[:, k * 512:(k + 1) * 512], k == 0, k == 21), sem="pe")
                    P.s(lambda s, pm=pm, ot=ot: s.activation(out=g0tb[:, ot * 512:(ot + 1) * 512],
                                                             in_=pm[:, :], func=AF.Silu,
                                                             scale=scd[:, ot:ot + 1]),
                        waits=["pe", ("dv", P.cnt["dv"])])
                P.dma(lambda q, tb=tb: q.dma_start(
                    out=s0tb[:, :].rearrange("p (k t) -> p k t", k=8),
                    in_=s0_h[:, tb * 512:(tb + 1) * 512].rearrange("(k p) t -> p k t", p=128)),
                    waits=[("pe", P.cnt["pe"])])
                P.v(lambda v: v.tensor_tensor(out=s0tb[:, :], in0=s0tb[:, :], in1=g0tb[:, :], op=ALU.mult),
                    waits=["ld", "ac"])
                for g8 in range(4):
                    P.wait("tensor", "dv", P.cnt["dv"])
                    for u in range(8):
                        kk = g8 + u * 4
                        pass
                    for k in range(8):
                        P.t(lambda p, k=k, g8=g8: p.transpose(
                            out=p_t3[:, k * 128:(k + 1) * 128],
                            in_=s0tb[:, k * 512 + g8 * 128:k * 512 + (g8 + 1) * 128],
                            identity=identb[:, :]), sem="pe")
                    P.v(lambda v, g8=g8: v.tensor_copy(out=inctm[:, g8 * D:(g8 + 1) * D], in_=p_t3[:, :]),
                        waits=["pe"])
                P.dma(lambda q, tb=tb: q.dma_start(out=x0t[:, :].rearrange("p (s d) -> p s d", s=4),
                                                   in_=x0_h[tb * 512:(tb + 1) * 512, :]
                                                   .rearrange("(s p) d -> p s d", p=128)),
                      waits=[("ac", P.cnt["ac"]), ("so", P.cnt["so"])])
                P.v(lambda v: v.tensor_tensor(out=x1t[:, :], in0=x0t[:, :], in1=inctm[:, :], op=ALU.add),
                    waits=["ld", "dv"])
                P.dma(lambda q, tb=tb: q.dma_start(out=x1_sp[tb * 512:(tb + 1) * 512, :]
                                                   .rearrange("(s p) d -> p s d", p=128),
                                                   in_=x1t[:, :].rearrange("p (s d) -> p s d", s=4)),
                      waits=["dv"], q="gpsimd", sem="so")
                for sb in range(4):
                    xs_ = x1t.ap()[:, sb * D:(sb + 1) * D]
                    P.v(lambda v, xs_=xs_: v.reduce_sum(out=lst[:, 0:1], in_=xs_, axis=AX.X))
                    P.s(lambda s, xs_=xs_: s.activation(out=sqs3[:, :], in_=xs_, func=AF.Square,
                                                        accum_out=lst[:, 1:2]),
                        waits=[("dv", P.cnt["dv"])])
                    P.v(lambda v: v.tensor_scalar_mul(out=lst[:, 2:3], in0=lst[:, 0:1], scalar1=1.0 / D))
                    P.v(lambda v: v.tensor_tensor(out=lst[:, 3:4], in0=lst[:, 2:3], in1=lst[:, 2:3],
                                                  op=ALU.mult))
                    P.v(lambda v: v.tensor_scalar_mul(out=lst[:, 4:5], in0=lst[:, 1:2], scalar1=1.0 / D),
                        waits=["ac"])
                    P.v(lambda v: v.tensor_tensor(out=lst[:, 5:6], in0=lst[:, 4:5], in1=lst[:, 3:4],
                                                  op=ALU.subtract))
                    P.s(lambda s: s.activation(out=lst[:, 6:7], in_=lst[:, 5:6], func=AF.Sqrt, bias=1e-5),
                        waits=["dv"])
                    P.v(lambda v: v.reciprocal(out=lst[:, 7:8], in_=lst[:, 6:7]), waits=["ac"])
                    P.v(lambda v, xs_=xs_: v.tensor_scalar(out=xn1t[:, :], in0=xs_, scalar1=lst[:, 2:3],
                                                           scalar2=lst[:, 7:8],
                                                           op0=ALU.subtract, op1=ALU.mult))
                    P.dma(lambda q, tb=tb, sb=sb: q.dma_start(
                        out=xn1t_sp[tb * 512 + sb * 128:tb * 512 + (sb + 1) * 128, :], in_=xn1t[:, :]),
                        waits=["dv"], q="gpsimd", sem="so")
                    P.v(lambda v: v.reduce_sum(out=lst[:, 0:1], in_=xn1t[:, :], axis=AX.X))
                    P.s(lambda s: s.activation(out=sqs3[:, :], in_=xn1t[:, :], func=AF.Square,
                                               accum_out=lst[:, 1:2]), waits=[("dv", P.cnt["dv"])])
                    P.v(lambda v: v.tensor_scalar_mul(out=lst[:, 2:3], in0=lst[:, 0:1], scalar1=1.0 / D))
                    P.v(lambda v: v.tensor_tensor(out=lst[:, 3:4], in0=lst[:, 2:3], in1=lst[:, 2:3],
                                                  op=ALU.mult))
                    P.v(lambda v: v.tensor_scalar_mul(out=lst[:, 4:5], in0=lst[:, 1:2], scalar1=1.0 / D),
                        waits=["ac"])
                    P.v(lambda v: v.tensor_tensor(out=lst[:, 5:6], in0=lst[:, 4:5], in1=lst[:, 3:4],
                                                  op=ALU.subtract))
                    P.s(lambda s: s.activation(out=lst[:, 6:7], in_=lst[:, 5:6], func=AF.Sqrt, bias=1e-5),
                        waits=["dv"])
                    P.v(lambda v: v.reciprocal(out=lst[:, 7:8], in_=lst[:, 6:7]), waits=["ac"])
                    P.v(lambda v: v.tensor_scalar(out=xrw[:, :], in0=xn1t[:, :], scalar1=lst[:, 2:3],
                                                  scalar2=lst[:, 7:8], op0=ALU.subtract, op1=ALU.mult))
                    P.v(lambda v: v.tensor_tensor(out=xrw[:, :], in0=xrw[:, :], in1=gbig[:, :], op=ALU.mult))
                    P.v(lambda v: v.tensor_tensor(out=xlnt[:, :], in0=xrw[:, :], in1=bbig[:, :], op=ALU.add))
                    for which, tile in (("xn1", xn1t), ("xln", xlnt)):
                        P.wait("tensor", "dv", P.cnt["dv"])
                        for u in range(8):
                            P.t(lambda p, u=u, tile=tile: p.transpose(out=p_t3[:, u * 128:(u + 1) * 128],
                                                                      in_=tile[:, u * 128:(u + 1) * 128],
                                                                      identity=identb[:, :]), sem="pe")
                        if which == "xn1":
                            P.v(lambda v: v.tensor_copy(out=inctm[:, 0:1024], in_=p_t3[:, :]),
                                waits=["pe", ("so", P.cnt["so"])])
                            P.dma(lambda q, tb=tb, sb=sb: q.dma_start(
                                out=xn1f_sp[:, tb * 512 + sb * 128:tb * 512 + (sb + 1) * 128]
                                .rearrange("(k p) t -> p k t", p=128),
                                in_=inctm[:, 0:1024].rearrange("p (k t) -> p k t", k=8)),
                                waits=["dv"], q="gpsimd", sem="so")
                        else:
                            tt_ = tb * 512 + sb * 128
                            xlv = xln_fm.ap().rearrange("p (k t) -> p k t", k=8)
                            P.v(lambda v, xlv=xlv, tt_=tt_: v.tensor_copy(
                                out=xlv[:, :, tt_:tt_ + 128],
                                in_=p_t3[:, :].rearrange("p (k t) -> p k t", k=8)), waits=["pe"])
        if debug:
            P.dma(lambda q: q.dma_start(out=dbg["x1"], in_=x1_sp), waits=[("so", P.cnt["so"])],
                  q="gpsimd", sem="so")
            P.dma(lambda q: q.dma_start(out=dbg["xln"].rearrange("(k p) t -> p k t", p=128),
                                        in_=xln_fm[:, :].rearrange("p (k t) -> p k t", k=8)),
                  waits=[("dv", P.cnt["dv"])], q="gpsimd", sem="so")
        P.barrier()
        nc_dirs = _emit_l1_dirs(nc, st, e, P, dict(
            inw_h=inw_h, cw_h=cw_h, cb_h=cb_h, xpw_h=xpw_h, dtpw_h=dtpw_h, dtpb_h=dtpb_h,
            alog_h=alog_h, dsk_h=dsk_h, outw_h=outw_h, z_sp=z_sp, xcp_sp=xcp_sp, xc_sp=xc_sp,
            bc_sp=bc_sp, yd_sp=yd_sp, yp_sp=yp_sp, xln_fm=xln_fm, xln_stack=xst,
            identb=identb, identf=identf, ones1=ones1, ones1b=ones1b, dbg=dbg, debug=debug))
        P.barrier()
        _emit_l3_tail(nc, st, e, P, dict(
            projw_h=projw_h, fw_h=fw_h, dw_h=dw_h, yp_sp=yp_sp, yat_sp=yat_sp, xn1t_sp=xn1t_sp,
            xn1f_sp=xn1f_sp, x1_sp=x1_sp, x2_h=x2_h, pbcols=pbcols, fbcols=fbcols, dacol=dacol,
            identb=identb, identf=identf, ones1=ones1, dbg=dbg, debug=debug))
        P.op("sync", lambda q: q.wait_ge(P.sem["so"], P.cnt["so"]))
        P.op("sync", lambda q: q.wait_ge(P.sem["ld"], P.cnt["ld"]))
        P.finalize(blk)
    return nc



def _emit_l1_dirs(nc, st, e, P, env):
    g = type("E", (), env)
    xln_fm = env["xln_fm"]
    identb, identf, ones1 = env["identb"], env["identf"], env["ones1"]
    ones1b = env["ones1b"]
    debug, dbg = env["debug"], env["dbg"]

    with ExitStack() as s2:
        e2 = s2.enter_context
        wsl_f = e2(nc.sbuf_tensor("l1w_f", [128, D], F32))
        wslq = e2(nc.sbuf_tensor("l1w_q", [128, D], BF16))
        slab = e2(nc.sbuf_tensor("l1slab", [128, 8 * 128], BF16))
        ev0 = e2(nc.sbuf_tensor("l1ev0", [128, 512], BF16))
        ev1 = e2(nc.sbuf_tensor("l1ev1", [128, 512], BF16))
        p_a = e2(nc.psum_tensor("l1p_a", [128, 512], F32))
        p_b = e2(nc.psum_tensor("l1p_b", [128, 512], F32))
        p_t = e2(nc.psum_tensor("l1p_t", [128, 1024], BF16))
        for d in "fb":
            ev_tok = [("dv", 0), ("dv", 0)]
            st_tok = [0, 0]
            for rt in range(32):
                P.dma(lambda q, d=d, rt=rt: q.dma_start(out=wslq[:, :],
                                                        in_=env["inw_h"][d][rt * 128:(rt + 1) * 128, :]),
                      waits=[("pe", P.cnt["pe"])], q="gpsimd", sem="so")
                P.wait("tensor", "so", P.cnt["so"])
                P.wait("tensor", "dv", P.cnt["dv"])
                for u in range(8):
                    P.t(lambda p, u=u: p.transpose(out=p_t[:, u * 128:(u + 1) * 128],
                                                   in_=wslq[:, u * 128:(u + 1) * 128],
                                                   identity=identb[:, :]), sem="pe")
                P.v(lambda v: v.tensor_copy(out=slab[:, :], in_=p_t[:, :]), waits=["pe"])
                P.wait("tensor", "dv", P.cnt["dv"])
                for tb in range(8):
                    bp = tb % 2
                    pm = p_a if bp == 0 else p_b
                    ev = ev0 if bp == 0 else ev1
                    P.wait("tensor", ev_tok[bp][0], ev_tok[bp][1])
                    for k in range(8):
                        P.t(_mm(pm[:, :], slab[:, k * 128:(k + 1) * 128],
                                xln_fm[:, k * T + tb * 512:k * T + (tb + 1) * 512], k == 0, k == 7),
                            sem="pe")
                    if rt < 16:
                        ev_tok[bp] = ("dv", P.v(lambda v, ev=ev, pm=pm: v.tensor_copy(out=ev[:, :], in_=pm[:, :]),
                                                waits=["pe", ("so", st_tok[bp])]))
                        wsem = "dv"
                    else:
                        P.wait("scalar", "so", st_tok[bp])
                        ev_tok[bp] = ("ac", P.s(lambda s, ev=ev, pm=pm: s.activation(out=ev[:, :], in_=pm[:, :],
                                                                                     func=AF.Silu), waits=["pe"]))
                        wsem = "ac"
                    dst = env["xcp_sp"][d] if rt < 16 else env["z_sp"][d]
                    row = rt * 128 if rt < 16 else (rt - 16) * 128
                    st_tok[bp] = P.dma(lambda q, ev=ev, dst=dst, row=row, tb=tb: q.dma_start(
                        out=dst[row:row + 128, tb * 512:(tb + 1) * 512], in_=ev[:, :]),
                        waits=[wsem], q="gpsimd", sem="so")
    env["xln_stack"].close()
    P.barrier()

    def _one_dir(d):
        rev = d == "b"
        with ExitStack() as s3:
            e3 = s3.enter_context
            convw1 = e3(nc.sbuf_tensor("cw1" + d, [128, 16 * 4], F32))
            convb1 = e3(nc.sbuf_tensor("cb1" + d, [128, 16], F32))
            dtb = e3(nc.sbuf_tensor("dtb" + d, [64, T], BF16))
            dtpT = e3(nc.sbuf_tensor("dtpT" + d, [64, 16 * 128], BF16))
            dtp_q = e3(nc.sbuf_tensor("dtp_q" + d, [128, 64], BF16))
            ndtpb = e3(nc.sbuf_tensor("ndtpb" + d, [128, 16], F32))
            aneg = e3(nc.sbuf_tensor("aneg" + d, [128, 256], F32))
            dcol = e3(nc.sbuf_tensor("dcol" + d, [128, 16], F32))
            p_c = e3(nc.psum_tensor("p_c" + d, [128, 512], F32))
            p_d = e3(nc.psum_tensor("p_d" + d, [128, 512], F32))
            p_t2 = e3(nc.psum_tensor("p_t2" + d, [128, 1024], BF16))
            s3a = ExitStack()
            e3a = s3a.enter_context
            convx = e3a(nc.sbuf_tensor("cvx" + d, [128, T + 3], BF16))
            convacc = e3a(nc.sbuf_tensor("cva" + d, [128, T], F32))
            convo = e3a(nc.sbuf_tensor("cvo" + d, [128, T], BF16))
            xpT = e3a(nc.sbuf_tensor("xpT" + d, [128, 16 * 96], BF16))
            xp_f = e3a(nc.sbuf_tensor("xp_f" + d, [96, L1_DI], F32))
            xp_q = e3a(nc.sbuf_tensor("xp_q" + d, [96, L1_DI], BF16))
            xdbl = e3a(nc.sbuf_tensor("xdbl" + d, [96, T], F32))

            P.dma(lambda q, d=d: q.dma_start(out=convw1[:, :].rearrange("p (s k) -> p s k", s=16),
                                             in_=env["cw_h"][d].rearrange("(s p) k -> p s k", p=128)))
            P.dma(lambda q, d=d: q.dma_start(out=convb1[:, :], in_=env["cb_h"][d]
                                             .rearrange("(s p) -> p s", p=128),
                                             allow_slow_non_contiguous=True))
            P.dma(lambda q, d=d: q.dma_start(out=ndtpb[:, :], in_=env["dtpb_h"][d]
                                             .rearrange("(s p) -> p s", p=128),
                                             allow_slow_non_contiguous=True))
            P.dma(lambda q, d=d: q.dma_start(out=dcol[:, :], in_=env["dsk_h"][d]
                                             .rearrange("(s p) -> p s", p=128),
                                             allow_slow_non_contiguous=True))
            P.dma(lambda q, d=d: q.dma_start(out=aneg[:, :].rearrange("p (s n) -> p s n", s=16),
                                             in_=env["alog_h"][d].rearrange("(s p) n -> p s n", p=128)))
            P.s(lambda s: s.activation(out=aneg[:, :], in_=aneg[:, :], func=AF.Exp), waits=["ld"])
            P.v(lambda v: v.tensor_scalar_mul(out=aneg[:, :], in0=aneg[:, :], scalar1=-1.0), waits=["ac"])
            P.v(lambda v: v.tensor_scalar_mul(out=ndtpb[:, :], in0=ndtpb[:, :], scalar1=-1.0), waits=["ld"])

            P.g(lambda g_: g_.memset(convx[:, :], 0.0))
            off = 3 if not rev else 0
            cv_tok = 0
            for ds in range(16):
                P.dma(lambda q, d=d, ds=ds, off=off: q.dma_start(
                    out=convx[:, off:off + T], in_=env["xcp_sp"][d][ds * 128:(ds + 1) * 128, :]),
                    waits=[("ac", cv_tok), ("po", P.cnt["po"])])
                for k in range(4):
                    tap = k if not rev else 3 - k
                    if k == 0:
                        P.v(lambda v, ds=ds, tap=tap, k=k: v.tensor_scalar_mul(
                            out=convacc[:, :], in0=convx[:, k:k + T],
                            scalar1=convw1[:, ds * 4 + tap:ds * 4 + tap + 1]), waits=["ld", ("ac", cv_tok)])
                    else:
                        P.v(lambda v, ds=ds, tap=tap, k=k: v.scalar_tensor_tensor(
                            out=convacc[:, :], in0=convx[:, k:k + T],
                            scalar=convw1[:, ds * 4 + tap:ds * 4 + tap + 1],
                            in1=convacc[:, :], op0=ALU.mult, op1=ALU.add))
                cv_tok = P.s(lambda s, ds=ds: s.activation(out=convo[:, :], in_=convacc[:, :],
                                                           func=AF.Silu, bias=convb1[:, ds:ds + 1]),
                             waits=["dv", ("so", P.cnt["so"])])
                P.dma(lambda q, d=d, ds=ds: q.dma_start(out=env["xc_sp"][d][ds * 128:(ds + 1) * 128, :],
                                                        in_=convo[:, :]), waits=["ac"], q="gpsimd", sem="so")
            if debug and ("xc_" + d) in dbg:
                P.dma(lambda q, d=d: q.dma_start(out=dbg["xc_" + d], in_=env["xc_sp"][d]),
                      waits=[("so", P.cnt["so"])], q="gpsimd", sem="so")

            P.dma(lambda q, d=d: q.dma_start(out=xp_q[:, :], in_=env["xpw_h"][d]),
                  waits=[("pe", P.cnt["pe"])], q="gpsimd", sem="so")
            P.wait("tensor", "so", P.cnt["so"])
            for u in range(16):
                P.wait("tensor", "ac", P.cnt["ac"])
                P.wait("tensor", "dv", P.cnt["dv"])
                P.t(lambda p, u=u: p.transpose(out=p_t2[:, 0:96], in_=xp_q[:, u * 128:(u + 1) * 128],
                                               identity=identb[0:96, 0:96]), sem="pe")
                P.v(lambda v, u=u: v.tensor_copy(out=xpT[:, u * 96:(u + 1) * 96], in_=p_t2[:, 0:96]),
                    waits=["pe"])
            for ds in range(16):
                P.dma(lambda q, d=d, ds=ds: q.dma_start(out=convo[:, :],
                                                        in_=env["xc_sp"][d][ds * 128:(ds + 1) * 128, :]),
                      waits=[("pe", P.cnt["pe"]), ("ac", P.cnt["ac"]), ("so", P.cnt["so"])])
                for tb in range(8):
                    pm = p_c if tb % 2 == 0 else p_d
                    P.wait("tensor", "ld", P.cnt["ld"])
                    P.wait("tensor", "dv", P.cnt["dv"])
                    P.t(_mm(pm[0:96, :], xpT[:, ds * 96:(ds + 1) * 96],
                            convo[:, tb * 512:(tb + 1) * 512], True, True), sem="pe")
                    if ds == 0:
                        P.v(lambda v, pm=pm, tb=tb: v.tensor_copy(out=xdbl[:, tb * 512:(tb + 1) * 512],
                                                                  in_=pm[0:96, :]), waits=["pe"])
                    else:
                        P.v(lambda v, pm=pm, tb=tb: v.tensor_tensor(out=xdbl[:, tb * 512:(tb + 1) * 512],
                                                                    in0=xdbl[:, tb * 512:(tb + 1) * 512],
                                                                    in1=pm[0:96, :], op=ALU.add),
                            waits=["pe"])
            P.dma(lambda q, d=d: q.dma_start(out=env["bc_sp"][d], in_=xdbl[64:96, :]),
                  waits=["dv"], q="gpsimd", sem="so")
            P.s(lambda s: s.activation(out=dtb[:, :], in_=xdbl[0:64, :], func=AF.Copy), waits=["dv"])
            if debug and ("dtb_" + d) in dbg:
                P.dma(lambda q: q.dma_start(out=dbg["dtb_" + d], in_=xdbl[:, :]),
                      waits=["dv"], q="gpsimd", sem="so")
            s3a.close()
            P.barrier()
            for u in range(16):
                P.dma(lambda q, d=d, u=u: q.dma_start(out=dtp_q[:, :],
                                                      in_=env["dtpw_h"][d][u * 128:(u + 1) * 128, :]),
                      waits=[("pe", P.cnt["pe"])], q="gpsimd", sem="so")
                P.wait("tensor", "so", P.cnt["so"])
                P.wait("tensor", "dv", P.cnt["dv"])
                P.t(lambda p: p.transpose(out=p_t2[0:64, 0:128], in_=dtp_q[:, :],
                                          identity=identb[:, :]), sem="pe")
                P.v(lambda v, u=u: v.tensor_copy(out=dtpT[:, u * 128:(u + 1) * 128], in_=p_t2[0:64, 0:128]),
                    waits=["pe"])

            with ExitStack() as s4:
                e4 = s4.enter_context
                Ball = e4(nc.sbuf_tensor("Ball" + d, [128, 16 * 1024], BF16))
                Call = e4(nc.sbuf_tensor("Call" + d, [128, 16 * 1024], BF16))
                brow = e4(nc.sbuf_tensor("brow" + d, [1, 512], BF16))
                brow2 = e4(nc.sbuf_tensor("brow2" + d, [1, 512], BF16))
                delta = e4(nc.sbuf_tensor("delta" + d, [128, 1024], F32))
                delta2 = e4(nc.sbuf_tensor("delta2" + d, [128, 1024], F32))
                dbufs = [delta, delta2]

                xcc0 = e4(nc.sbuf_tensor("xcc0" + d, [128, 1024], BF16))
                xcc1 = e4(nc.sbuf_tensor("xcc1" + d, [128, 1024], BF16))
                zcc = e4(nc.sbuf_tensor("zcc" + d, [128, 1024], BF16))
                Ebuf0 = e4(nc.sbuf_tensor("Ebuf0" + d, [128, 1024], F32))
                Ebuf1 = e4(nc.sbuf_tensor("Ebuf1" + d, [128, 1024], F32))
                hball = e4(nc.sbuf_tensor("hball" + d, [128, 16 * 1024], BF16))
                w_all = e4(nc.sbuf_tensor("w_all" + d, [128, 16 * 1024], BF16))
                wacc = e4(nc.sbuf_tensor("wacc" + d, [128, 16 * 1024], BF16))
                states = e4(nc.sbuf_tensor("states" + d, [128, 256], F32))
                ytq = e4(nc.sbuf_tensor("ytq" + d, [128, 1024], BF16))
                y2q = e4(nc.sbuf_tensor("y2q" + d, [128, 1024], BF16))
                y3q = e4(nc.sbuf_tensor("y3q" + d, [128, 1024], BF16))
                db = y2q

                P.v(lambda v: v.memset(states[:, :], 0.0))
                tree_tok = 0
                y3_tok = 0
                stt_tok = 0
                sp_tok = 0
                xcc_rd = [0, 0]
                row_mm = [0, 0]
                rp_i = 0
                tsmin_tok = 0
                ln_tok = 0
                btoks = [0] * 16
                nxt_b_tok = 0
                nxt_c_tok = 0
                tqs = list(range(4)) if not rev else list(range(3, -1, -1))
                for tqi, tq in enumerate(tqs):
                    t0 = tq * 1024
                    def _bcast_one(bc, n, t0b, rd_tok):
                        nonlocal row_mm, rp_i
                        dest = Ball if bc == 0 else Call
                        for half in range(2):
                            rp = rp_i % 2
                            rp_i += 1
                            row = brow if rp == 0 else brow2
                            P.dma(lambda q, d=d, n=n, t0b=t0b, bc=bc, row=row, half=half: q.dma_start(
                                out=row[0:1, :],
                                in_=env["bc_sp"][d][bc * 16 + n:bc * 16 + n + 1,
                                                    t0b + half * 512:t0b + (half + 1) * 512]),
                                waits=[("pe", row_mm[rp])])
                            pm = p_c if half == 0 else p_d
                            P.wait("tensor", "ld", P.cnt["ld"])
                            P.wait("tensor", "ac", P.cnt["ac"])
                            row_mm[rp] = P.t(_mm(pm[:, :], ones1b[0:1, :], row[0:1, :],
                                                 True, True), sem="pe")
                            P.s(lambda s, n=n, half=half, pm=pm, dest=dest: s.activation(
                                out=dest[:, n * 1024 + half * 512:n * 1024 + (half + 1) * 512],
                                in_=pm[:, :], func=AF.Copy),
                                waits=["pe"] + ([("dv", rd_tok)] if half == 0 else []))

                    if tqi == 0:
                        rd0 = P.cnt["dv"]
                        for bc in range(2):
                            for n in range(16):
                                _bcast_one(bc, n, t0, rd0 if (bc == 0 and n == 0) else 0)
                            if bc == 0:
                                bc_tok = P.cnt["ac"]
                        cc_tok = P.cnt["ac"]
                    else:
                        bc_tok, cc_tok = nxt_b_tok, nxt_c_tok

                    def _gate(j, t0=t0):
                        nonlocal stt_tok, y3_tok, sp_tok
                        xcj = xcc0 if j % 2 == 0 else xcc1
                        P.dma(lambda q, d=d, j=j, t0=t0: q.dma_start(
                            out=zcc[:, :], in_=env["z_sp"][d][j * 128:(j + 1) * 128, t0:t0 + 1024]),
                            waits=[("dv", y3_tok)])
                        stt_tok = P.v(lambda v, j=j, xcj=xcj: v.scalar_tensor_tensor(
                            out=y2q[:, :], in0=xcj[:, :], scalar=dcol[:, j:j + 1],
                            in1=ytq[:, :], op0=ALU.mult, op1=ALU.add))
                        xcc_rd[j % 2] = stt_tok
                        y3_tok = P.v(lambda v: v.tensor_tensor(out=y3q[:, :], in0=y2q[:, :],
                                                               in1=zcc[:, :], op=ALU.mult),
                                     waits=["ld", ("so", sp_tok)])
                        sp_tok = P.dma(lambda q, d=d, j=j, t0=t0: q.dma_start(
                            out=env["yd_sp"][d][j * 128:(j + 1) * 128, t0:t0 + 1024], in_=y3q[:, :]),
                            waits=["dv"], q="gpsimd", sem="so")

                    def _delta(j, inline):
                        nonlocal tsmin_tok
                        dj = dbufs[j % 2]
                        for half in range(2):
                            pm = p_c if half == 0 else p_d
                            if inline:
                                P.wait("tensor", "dv", P.cnt["dv"])
                                P.wait("tensor", "ac", P.cnt["ac"])
                            else:
                                P.wait("tensor", "dv", tsmin_tok)
                            P.t(_mm(pm[:, :], dtpT[:, j * 128:(j + 1) * 128],
                                    dtb[:, t0 + half * 512:t0 + (half + 1) * 512], True, True), sem="pe")
                            tsmin_tok = P.v(lambda v, pm=pm, j=j, half=half, dj=dj: v.tensor_scalar(
                                out=dj[:, half * 512:(half + 1) * 512], in0=pm[:, :],
                                scalar1=ndtpb[:, j:j + 1], scalar2=88.0,
                                op0=ALU.subtract, op1=ALU.min), waits=["pe"])
                        P.s(lambda s, dj=dj: s.activation(out=dj[:, :], in_=dj[:, :], func=AF.Exp),
                            waits=[("dv", tsmin_tok)])
                        return P.s(lambda s, dj=dj: s.activation(out=dj[:, :], in_=dj[:, :],
                                                                 func=AF.Ln, bias=1.0))

                    scan_tok = [0, 0]
                    exp_tok2 = [0, 0]
                    for ds in range(16):
                        xcd = xcc0 if ds % 2 == 0 else xcc1
                        delta = dbufs[ds % 2]
                        if ds == 0:
                            ln_tok = _delta(0, True)
                        P.dma(lambda q, d=d, ds=ds, t0=t0, xcd=xcd: q.dma_start(
                            out=xcd[:, :], in_=env["xc_sp"][d][ds * 128:(ds + 1) * 128, t0:t0 + 1024]),
                            waits=[("dv", xcc_rd[ds % 2])])
                        P.v(lambda v, xcd=xcd, delta=delta: v.tensor_tensor(
                            out=db[:, :], in0=delta[:, :], in1=xcd[:, :], op=ALU.mult),
                            waits=["ld", ("ac", ln_tok)])
                        for n in range(16):
                            par = n % 2
                            Eb = Ebuf0 if par == 0 else Ebuf1
                            scol_ = states.ap()[:, ds * 16 + n:ds * 16 + n + 1]
                            exp_tok2[par] = P.s(lambda s, ds=ds, n=n, Eb=Eb, delta=delta: s.activation(
                                out=Eb[:, :], in_=delta[:, :], func=AF.Exp,
                                scale=aneg[:, ds * 16 + n:ds * 16 + n + 1]),
                                waits=[("dv", scan_tok[par])])
                            btoks[n] = P.v(lambda v, n=n: v.tensor_tensor(
                                out=w_all[:, n * 1024:(n + 1) * 1024], in0=db[:, :],
                                in1=Ball[:, n * 1024:(n + 1) * 1024], op=ALU.mult),
                                waits=([("ac", bc_tok)] if ds == 0 and n == 0 else []))
                            wv = w_all.ap()[:, n * 1024:(n + 1) * 1024]
                            hv = hball.ap()[:, n * 1024:(n + 1) * 1024]
                            if not rev:
                                ein, bin_, hout = Eb[:, :], wv, hv
                            else:
                                ein = Eb.ap()[:, ::-1]
                                bin_ = wv[:, ::-1]
                                hout = hv[:, ::-1]
                            init = 0.0 if tqi == 0 else scol_
                            scan_tok[par] = P.v(
                                lambda v, ein=ein, bin_=bin_, hout=hout, init=init:
                                v.tensor_tensor_scan(out=hout, data0=ein, data1=bin_, initial=init,
                                                     op0=ALU.mult, op1=ALU.add),
                                waits=[("ac", exp_tok2[par])])
                        if ds < 15:
                            ln_tok = _delta(ds + 1, False)
                        if ds == 15 and tqi < 3:
                            t0n = tqs[tqi + 1] * 1024
                            for n in range(16):
                                _bcast_one(0, n, t0n, btoks[n])
                            nxt_b_tok = P.cnt["ac"]
                        if ds > 0:
                            _gate(ds - 1)
                        lastc = (1023 if not rev else 0)
                        P.v(lambda v, ds=ds, lastc=lastc: v.tensor_copy(
                            out=states[:, ds * 16:(ds + 1) * 16].unsqueeze(2),
                            in_=hball[:, :].rearrange("p (n t) -> p n t", n=16)[:, :, lastc:lastc + 1]))
                        wtok = P.v(lambda v: v.tensor_tensor(out=wacc[:, :], in0=hball[:, :],
                                                             in1=Call[:, :], op=ALU.mult),
                                   waits=([("ac", cc_tok)] if ds == 0 else []))
                        if ds == 15 and tqi < 3:
                            t0n = tqs[tqi + 1] * 1024
                            for n in range(16):
                                _bcast_one(1, n, t0n, wtok if n == 0 else 0)
                            nxt_c_tok = P.cnt["ac"]
                        for width in (8, 4, 2):
                            P.v(lambda v, width=width: v.tensor_tensor(
                                out=wacc[:, 0:width * 1024], in0=wacc[:, 0:width * 1024],
                                in1=wacc[:, width * 1024:2 * width * 1024], op=ALU.add))
                        tree_tok = P.v(lambda v: v.tensor_tensor(out=ytq[:, :], in0=wacc[:, 0:1024],
                                                                 in1=wacc[:, 1024:2048], op=ALU.add))
                    _gate(15)
            P.barrier()
            if debug and ("y_" + d) in dbg:
                P.dma(lambda q, d=d: q.dma_start(out=dbg["y_" + d], in_=env["yd_sp"][d]),
                      waits=[("so", P.cnt["so"])], q="gpsimd", sem="so")

            with ExitStack() as s5:
                e5 = s5.enter_context
                woT = e5(nc.sbuf_tensor("woT" + d, [128, 8 * L1_DI], BF16))
                wrow = e5(nc.sbuf_tensor("wrow" + d, [128, L1_DI], F32))
                wrq = e5(nc.sbuf_tensor("wrq" + d, [128, L1_DI], BF16))
                yin = e5(nc.sbuf_tensor("yin" + d, [128, 16 * 512], BF16))
                evo0 = e5(nc.sbuf_tensor("evo0" + d, [128, 512], BF16))
                evo1 = e5(nc.sbuf_tensor("evo1" + d, [128, 512], BF16))
                for rt in range(8):
                    P.dma(lambda q, d=d, rt=rt: q.dma_start(out=wrq[:, :],
                                                            in_=env["outw_h"][d][rt * 128:(rt + 1) * 128, :]),
                          waits=[("pe", P.cnt["pe"])], q="gpsimd", sem="so")
                    for g8 in range(2):
                        P.wait("tensor", "so", P.cnt["so"])
                        P.wait("tensor", "dv", P.cnt["dv"])
                        for u in range(8):
                            kk = g8 * 8 + u
                            P.t(lambda p, u=u, kk=kk: p.transpose(out=p_t2[:, u * 128:(u + 1) * 128],
                                                                  in_=wrq[:, kk * 128:(kk + 1) * 128],
                                                                  identity=identb[:, :]), sem="pe")
                        P.v(lambda v, rt=rt, g8=g8: v.tensor_copy(
                            out=woT[:, rt * L1_DI + g8 * 1024:rt * L1_DI + (g8 + 1) * 1024],
                            in_=p_t2[:, :]), waits=["pe"])
                ev_tok = [0, 0]
                st_tok = [0, 0]
                for tb in range(8):
                    P.dma(lambda q, d=d, tb=tb: q.dma_start(
                        out=yin[:, :].rearrange("p (k t) -> p k t", k=16),
                        in_=env["yd_sp"][d][:, tb * 512:(tb + 1) * 512].rearrange("(k p) t -> p k t", p=128)),
                        waits=[("pe", P.cnt["pe"]), ("so", P.cnt["so"])])
                    for ot in range(8):
                        bp = ot % 2
                        pm = p_c if bp == 0 else p_d
                        ev = evo0 if bp == 0 else evo1
                        P.wait("tensor", "ld", P.cnt["ld"])
                        P.wait("tensor", "dv", ev_tok[bp])
                        for k in range(16):
                            P.t(_mm(pm[:, :], woT[:, ot * L1_DI + k * 128:ot * L1_DI + (k + 1) * 128],
                                    yin[:, k * 512:(k + 1) * 512], k == 0, k == 15), sem="pe")
                        ev_tok[bp] = P.v(lambda v, ev=ev, pm=pm: v.tensor_copy(out=ev[:, :], in_=pm[:, :]),
                                         waits=["pe", ("so", st_tok[bp])])
                        st_tok[bp] = P.dma(lambda q, d=d, ev=ev, ot=ot, tb=tb: q.dma_start(
                            out=env["yp_sp"][d][ot * 128:(ot + 1) * 128, tb * 512:(tb + 1) * 512],
                            in_=ev[:, :]), waits=["dv"], q="gpsimd", sem="so")
            if debug and ("yp_" + d) in dbg:
                P.dma(lambda q, d=d: q.dma_start(out=dbg["yp_" + d], in_=env["yp_sp"][d]),
                      waits=[("so", P.cnt["so"])], q="gpsimd", sem="so")
        P.barrier()

    for d in "fb":
        _one_dir(d)


def _emit_l3_tail(nc, st, e, P, env):
    identb, ones1 = env["identb"], env["ones1"]
    debug, dbg = env["debug"], env["dbg"]
    pbcols, fbcols, dacol = env["pbcols"], env["fbcols"], env["dacol"]

    def _proj():
      with ExitStack() as s6:
        e6 = s6.enter_context
        pwT = e6(nc.sbuf_tensor("pwT", [128, 8 * 2048], BF16))
        wrow = e6(nc.sbuf_tensor("pw_row", [128, 2048], F32))
        wrq = e6(nc.sbuf_tensor("pw_rq", [128, 2048], BF16))
        yin = e6(nc.sbuf_tensor("pyin", [128, 16 * 512], BF16))
        evo0 = e6(nc.sbuf_tensor("pevo0", [128, 512], BF16))
        evo1 = e6(nc.sbuf_tensor("pevo1", [128, 512], BF16))
        p_a = e6(nc.psum_tensor("p6_a", [128, 512], F32))
        p_b = e6(nc.psum_tensor("p6_b", [128, 512], F32))
        p_t = e6(nc.psum_tensor("p6_t", [128, 1024], BF16))
        for rt in range(8):
            P.dma(lambda q, rt=rt: q.dma_start(out=wrq[:, :],
                                               in_=env["projw_h"][rt * 128:(rt + 1) * 128, :]),
                  waits=[("pe", P.cnt["pe"])], q="gpsimd", sem="so")
            for g8 in range(2):
                P.wait("tensor", "so", P.cnt["so"])
                P.wait("tensor", "dv", P.cnt["dv"])
                for u in range(8):
                    kk = g8 * 8 + u
                    P.t(lambda p, u=u, kk=kk: p.transpose(out=p_t[:, u * 128:(u + 1) * 128],
                                                          in_=wrq[:, kk * 128:(kk + 1) * 128],
                                                          identity=identb[:, :]), sem="pe")
                P.v(lambda v, rt=rt, g8=g8: v.tensor_copy(
                    out=pwT[:, rt * 2048 + g8 * 1024:rt * 2048 + (g8 + 1) * 1024], in_=p_t[:, :]),
                    waits=["pe"])
        ev_tok = [0, 0]
        st_tok = [0, 0]
        for tb in range(8):
            P.dma(lambda q, tb=tb: q.dma_start(
                out=yin[:, 0:8 * 512].rearrange("p (k t) -> p k t", k=8),
                in_=env["yp_sp"]["f"][:, tb * 512:(tb + 1) * 512].rearrange("(k p) t -> p k t", p=128)),
                waits=[("pe", P.cnt["pe"])])
            P.dma(lambda q, tb=tb: q.dma_start(
                out=yin[:, 8 * 512:16 * 512].rearrange("p (k t) -> p k t", k=8),
                in_=env["yp_sp"]["b"][:, tb * 512:(tb + 1) * 512].rearrange("(k p) t -> p k t", p=128)))
            for ot in range(8):
                bp = ot % 2
                pm = p_a if bp == 0 else p_b
                ev = evo0 if bp == 0 else evo1
                P.wait("tensor", "ld", P.cnt["ld"])
                P.wait("tensor", "dv", ev_tok[bp])
                for k in range(16):
                    P.t(_mm(pm[:, :], pwT[:, ot * 2048 + k * 128:ot * 2048 + (k + 1) * 128],
                            yin[:, k * 512:(k + 1) * 512], k == 0, k == 15), sem="pe")
                ev_tok[bp] = P.v(lambda v, pm=pm, ev=ev, ot=ot: v.tensor_scalar(
                    out=ev[:, :], in0=pm[:, :], scalar1=pbcols[:, ot:ot + 1], scalar2=0.0,
                    op0=ALU.add, op1=ALU.add), waits=["pe", ("so", st_tok[bp])], sem="dv")
                st_tok[bp] = P.dma(lambda q, ev=ev, ot=ot, tb=tb: q.dma_start(
                    out=env["yat_sp"][ot * 128:(ot + 1) * 128, tb * 512:(tb + 1) * 512], in_=ev[:, :]),
                    waits=["dv"], q="gpsimd", sem="so")
    _proj()
    if debug:
        P.dma(lambda q: q.dma_start(out=dbg["yat"], in_=env["yat_sp"]),
              waits=[("so", P.cnt["so"])], q="gpsimd", sem="so")
    P.barrier()

    def _ffn():
      with ExitStack() as s7:
        e7 = s7.enter_context
        fwT = e7(nc.sbuf_tensor("fwT", [128, 8 * 1024], BF16))
        xq1 = e7(nc.sbuf_tensor("xq1", [128, 8 * T], BF16))
        wrow = e7(nc.sbuf_tensor("fw_row", [128, 1024], F32))
        dwrow = e7(nc.sbuf_tensor("dw_row", [128, 32], F32))
        q_bf = e7(nc.sbuf_tensor("q_bf", [128, 1024], BF16))
        wa = e7(nc.sbuf_tensor("wa", [128, 1024], F32))
        xsb = e7(nc.sbuf_tensor("xsb", [128, 1024], F32))
        stp = e7(nc.sbuf_tensor("stp", [128, 1024], I32))
        stpf = e7(nc.sbuf_tensor("stpf", [128, 1024], F32))
        amax = e7(nc.sbuf_tensor("amax", [128, 32], F32))
        ascl = e7(nc.sbuf_tensor("ascl", [128, 32], F32))
        asci = e7(nc.sbuf_tensor("asci", [128, 32], F32))
        ei32 = e7(nc.sbuf_tensor("ei32", [128, 32], I32))
        xnin = e7(nc.sbuf_tensor("xnin", [128, 1024], BF16))
        g1tb = e7(nc.sbuf_tensor("g1tb", [128, 8 * 512], BF16))
        yatb = e7(nc.sbuf_tensor("yatb", [128, 8 * 512], BF16))
        xnfb = e7(nc.sbuf_tensor("xnfb", [128, 8 * 512], BF16))
        s1b = e7(nc.sbuf_tensor("s1b", [128, 8 * 512], F32))
        s1g = e7(nc.sbuf_tensor("s1g", [128, 8 * 512], BF16))
        s1gt = e7(nc.sbuf_tensor("s1gt", [128, 4 * D], BF16))
        x1b = e7(nc.sbuf_tensor("x1b", [128, 4 * D], F32))
        x2b = e7(nc.sbuf_tensor("x2b", [128, 4 * D], F32))
        p_a = e7(nc.psum_tensor("p7_a", [128, 512], F32))
        p_b = e7(nc.psum_tensor("p7_b", [128, 512], F32))
        p_t = e7(nc.psum_tensor("p7_t", [128, 1024], BF16))

        def mxfp4(src_ap, dcol_ap, out_ap, blocks):
            P.s(lambda s: s.activation(out=wa[:, :], in_=src_ap, func=AF.Abs),
                waits=["ld", ("dv", P.cnt["dv"])])
            wv = wa.ap().rearrange("p (b i) -> p b i", b=32)
            for wdt in (16, 8, 4, 2, 1):
                P.v(lambda v, wv=wv, wdt=wdt: v.tensor_tensor(
                    out=wv[:, :, 0:wdt], in0=wv[:, :, 0:wdt], in1=wv[:, :, wdt:2 * wdt], op=ALU.max),
                    waits=["ac"])
            P.v(lambda v, wv=wv: v.tensor_copy(out=amax[:, :], in_=wv[:, :, 0].squeeze(2)
                                               if False else wv[:, :, 0:1].squeeze(2)))
            P.v(lambda v: v.tensor_scalar(out=ei32[:, :], in0=amax[:, :].bitcast(I32), scalar1=1,
                                          scalar2=24, op0=ALU.logical_shift_left,
                                          op1=ALU.logical_shift_right))
            P.v(lambda v: v.tensor_scalar(out=ascl[:, :], in0=ei32[:, :], scalar1=129.0, scalar2=-124.0,
                                          op0=ALU.subtract, op1=ALU.max))
            if dcol_ap is not None:
                P.v(lambda v: v.tensor_scalar(out=ascl[:, :], in0=ascl[:, :], scalar1=dcol_ap,
                                              scalar2=None if False else 0.0, op0=ALU.add, op1=ALU.add))
            else:
                P.v(lambda v: v.tensor_tensor(out=ascl[:, :], in0=ascl[:, :], in1=dwrow[:, :], op=ALU.add))
            P.s(lambda s: s.activation(out=ascl[:, :], in_=ascl[:, :], func=AF.Exp,
                                       scale=0.6931471805599453), waits=["dv"])
            P.v(lambda v: v.reciprocal(out=asci[:, :], in_=ascl[:, :]), waits=["ac"])
            av = asci.ap().unsqueeze(2).broadcast_to([128, 32, 32])
            P.v(lambda v, src_ap=src_ap, av=av: v.tensor_tensor(
                out=xsb[:, :].rearrange("p (b i) -> p b i", b=32),
                in0=src_ap.rearrange("p (b i) -> p b i", b=32), in1=av, op=ALU.mult))
            P.v(lambda v: v.tensor_scalar(out=xsb[:, :], in0=xsb[:, :], scalar1=6.0, scalar2=-6.0,
                                          op0=ALU.min, op1=ALU.max))
            P.v(lambda v: v.tensor_scalar(out=stp[:, :], in0=xsb[:, :].bitcast(I32), scalar1=1,
                                          scalar2=24, op0=ALU.logical_shift_left,
                                          op1=ALU.logical_shift_right))
            P.v(lambda v: v.tensor_scalar(out=stp[:, :], in0=stp[:, :], scalar1=127, scalar2=0,
                                          op0=ALU.subtract, op1=ALU.max))
            P.v(lambda v: v.tensor_scalar(out=stp[:, :], in0=stp[:, :], scalar1=2, scalar2=0,
                                          op0=ALU.min, op1=ALU.max))
            P.v(lambda v: v.tensor_scalar(out=ei32[:, 0:32], in0=stp[:, 0:32], scalar1=0, scalar2=0,
                                          op0=ALU.add, op1=ALU.add)) if False else None
            P.v(lambda v: v.tensor_scalar(out=stpf[:, :].bitcast(I32), in0=stp[:, :], scalar1=-1,
                                          scalar2=128, op0=ALU.mult, op1=ALU.add))
            P.v(lambda v: v.tensor_scalar(out=stpf[:, :].bitcast(I32), in0=stpf[:, :].bitcast(I32),
                                          scalar1=23, scalar2=0, op0=ALU.logical_shift_left,
                                          op1=ALU.logical_shift_left))
            P.v(lambda v: v.tensor_tensor(out=xsb[:, :], in0=xsb[:, :], in1=stpf[:, :], op=ALU.mult))
            P.v(lambda v: v.tensor_scalar(out=xsb[:, :], in0=xsb[:, :], scalar1=MAGIC, scalar2=MAGIC,
                                          op0=ALU.add, op1=ALU.subtract))
            P.v(lambda v: v.tensor_scalar(out=stp[:, :], in0=stp[:, :], scalar1=126, scalar2=23,
                                          op0=ALU.add, op1=ALU.arith_shift_left)
                if False else v.tensor_scalar(out=stp[:, :], in0=stp[:, :], scalar1=126, scalar2=0,
                                              op0=ALU.add, op1=ALU.add))
            P.v(lambda v: v.tensor_scalar(out=stp[:, :], in0=stp[:, :], scalar1=23, scalar2=0,
                                          op0=ALU.logical_shift_left, op1=ALU.logical_shift_left))
            P.v(lambda v: v.tensor_tensor(out=xsb[:, :], in0=xsb[:, :], in1=stp[:, :].bitcast(F32),
                                          op=ALU.mult))
            sv = ascl.ap().unsqueeze(2).broadcast_to([128, 32, 32])
            P.v(lambda v, sv=sv, out_ap=out_ap: v.tensor_tensor(
                out=out_ap.rearrange("p (b i) -> p b i", b=32),
                in0=xsb[:, :].rearrange("p (b i) -> p b i", b=32), in1=sv, op=ALU.mult))

        for rt in range(8):
            P.dma(lambda q, rt=rt: q.dma_start(out=wrow[:, :], in_=env["fw_h"][rt * 128:(rt + 1) * 128, :]),
                  waits=[("dv", P.cnt["dv"]), ("pe", P.cnt["pe"])])
            P.dma(lambda q, rt=rt: q.dma_start(out=dwrow[:, :], in_=env["dw_h"][rt * 128:(rt + 1) * 128, :]))
            P.s(lambda s: s.activation(out=dwrow[:, :], in_=dwrow[:, :], func=AF.Tanh, scale=4.0),
                waits=["ld"])
            P.v(lambda v: v.tensor_scalar(out=dwrow[:, :], in0=dwrow[:, :], scalar1=0.5, scalar2=0.0,
                                          op0=ALU.mult, op1=ALU.add), waits=["ac"])
            mxfp4(wrow.ap()[:, :], None, q_bf.ap()[:, :], 32)
            P.wait("tensor", "dv", P.cnt["dv"])
            for u in range(8):
                P.t(lambda p, u=u: p.transpose(out=p_t[:, u * 128:(u + 1) * 128],
                                               in_=q_bf[:, u * 128:(u + 1) * 128],
                                               identity=identb[:, :]), sem="pe")
            P.v(lambda v, rt=rt: v.tensor_copy(out=fwT[:, rt * 1024:(rt + 1) * 1024], in_=p_t[:, :]),
                waits=["pe"])
        for tt in range(32):
            P.dma(lambda q, tt=tt: q.dma_start(out=xnin[:, :],
                                               in_=env["xn1t_sp"][tt * 128:(tt + 1) * 128, :]),
                  waits=[("dv", P.cnt["dv"]), ("pe", P.cnt["pe"])])
            mxfp4(xnin.ap()[:, :], dacol.ap()[:, 0:1], q_bf.ap()[:, :], 32)
            P.wait("tensor", "dv", P.cnt["dv"])
            for u in range(8):
                P.t(lambda p, u=u: p.transpose(out=p_t[:, u * 128:(u + 1) * 128],
                                               in_=q_bf[:, u * 128:(u + 1) * 128],
                                               identity=identb[:, :]), sem="pe")
            xqv = xq1.ap().rearrange("p (k t) -> p k t", k=8)
            P.v(lambda v, tt=tt, xqv=xqv: v.tensor_copy(
                out=xqv[:, :, tt * 128:(tt + 1) * 128],
                in_=p_t[:, :].rearrange("p (k t) -> p k t", k=8)), waits=["pe"])
        s1_tok = 0
        x2_tok = 0
        x2dma_tok = 0
        for tb in range(8):
            for ot in range(8):
                pm = p_a if ot % 2 == 0 else p_b
                P.wait("tensor", "dv", P.cnt["dv"])
                P.wait("tensor", "ac", P.cnt["ac"])
                for k in range(8):
                    P.t(_mm(pm[:, :], fwT[:, ot * 1024 + k * 128:ot * 1024 + (k + 1) * 128],
                            xq1[:, k * T + tb * 512:k * T + (tb + 1) * 512], k == 0, k == 7), sem="pe")
                P.s(lambda s, pm=pm, ot=ot: s.activation(out=g1tb[:, ot * 512:(ot + 1) * 512],
                                                         in_=pm[:, :], func=AF.Silu,
                                                         bias=fbcols[:, ot:ot + 1]), waits=["pe"])
            if debug:
                P.dma(lambda q, tb=tb: q.dma_start(out=dbg["g1"][:, tb * 512:(tb + 1) * 512]
                                                   .rearrange("(k p) t -> p k t", p=128),
                                                   in_=g1tb[:, :].rearrange("p (k t) -> p k t", k=8)),
                      waits=["ac"], q="gpsimd", sem="so")
            P.dma(lambda q, tb=tb: q.dma_start(
                out=yatb[:, :].rearrange("p (k t) -> p k t", k=8),
                in_=env["yat_sp"][:, tb * 512:(tb + 1) * 512].rearrange("(k p) t -> p k t", p=128)),
                waits=[("dv", s1_tok)])
            P.dma(lambda q, tb=tb: q.dma_start(
                out=xnfb[:, :].rearrange("p (k t) -> p k t", k=8),
                in_=env["xn1f_sp"][:, tb * 512:(tb + 1) * 512].rearrange("(k p) t -> p k t", p=128)))
            s1_tok = P.v(lambda v: v.tensor_tensor(out=s1b[:, :], in0=xnfb[:, :], in1=yatb[:, :],
                                                   op=ALU.add), waits=["ld", "ac"])
            P.v(lambda v: v.tensor_tensor(out=s1g[:, :], in0=s1b[:, :], in1=g1tb[:, :], op=ALU.mult))
            for g8 in range(4):
                P.wait("tensor", "dv", P.cnt["dv"])
                for k in range(8):
                    P.t(lambda p, k=k, g8=g8: p.transpose(
                        out=p_t[:, k * 128:(k + 1) * 128],
                        in_=s1g[:, k * 512 + g8 * 128:k * 512 + (g8 + 1) * 128],
                        identity=identb[:, :]), sem="pe")
                P.v(lambda v, g8=g8: v.tensor_copy(out=s1gt[:, g8 * D:(g8 + 1) * D], in_=p_t[:, :]),
                    waits=["pe"])
            P.dma(lambda q, tb=tb: q.dma_start(out=x1b[:, :].rearrange("p (s d) -> p s d", s=4),
                                               in_=env["x1_sp"][tb * 512:(tb + 1) * 512, :]
                                               .rearrange("(s p) d -> p s d", p=128)),
                  waits=[("dv", x2_tok)])
            x2_tok = P.v(lambda v: v.tensor_tensor(out=x2b[:, :], in0=x1b[:, :], in1=s1gt[:, :],
                                                   op=ALU.add),
                         waits=["ld", "dv", ("so", x2dma_tok)])
            x2dma_tok = P.dma(lambda q, tb=tb: q.dma_start(out=env["x2_h"][tb * 512:(tb + 1) * 512, :]
                                               .rearrange("(s p) d -> p s d", p=128),
                                               in_=x2b[:, :].rearrange("p (s d) -> p s d", s=4)),
                  waits=["dv"], q="gpsimd", sem="so")
    _ffn()



_NC_CACHE = {}


def _get_ncs():
    if "ncs" not in _NC_CACHE:
        _NC_CACHE["ncs"] = (build_launch1(), build_launch2(), build_launch3())
    return _NC_CACHE["ncs"]


def kernel(**inp):
    B = 8
    nc1, nc2, nc3 = _get_ncs()
    f32 = lambda a: np.ascontiguousarray(np.asarray(a), dtype=np.float32)

    x0 = f32(inp["L0_attn_hidden_states"])
    im1 = []
    for b in range(B):
        im1.append(dict(
            x0=x0[b],
            w_in=f32(inp["L0_attn_W_in"]),
            conv_w=f32(inp["L0_attn_conv_w"]),
            conv_b=f32(inp["L0_attn_conv_b"]),
            dt_bias=f32(inp["L0_attn_dt_bias"]),
            a_log=f32(inp["L0_attn_A_log"]),
            d_skip=f32(inp["L0_attn_D"]).reshape(1, -1),
            norm_w=f32(inp["L0_attn_norm_w"]).reshape(1, -1),
            w_out=f32(inp["L0_attn_W_out"]),
        ))
    r1 = run_bass_kernel_spmd(nc1, im1, core_ids=list(range(B))).results

    xmax8 = np.concatenate([np.asarray(r["xmax"]).reshape(1, 1) for r in r1], axis=1)
    im2 = []
    for b in range(B):
        im2.append(dict(
            xn_fm=np.asarray(r1[b]["xn_fm"]),
            xmax8=f32(xmax8),
            clip_x=f32(inp["L0_ffn_clip_x"]).reshape(1, 1),
            w_gate=f32(inp["L0_ffn_w_gate"]),
            w_up=f32(inp["L0_ffn_w_up"]),
        ))
    r2 = run_bass_kernel_spmd(nc2, im2, core_ids=list(range(B))).results

    hmax8 = np.concatenate([np.asarray(r["hmax"]).reshape(1, 1) for r in r2], axis=1)
    im3 = []
    for b in range(B):
        m = dict(
            x0=x0[b],
            s0_fm=np.asarray(r1[b]["s0_fm"]),
            hpre_fm=np.asarray(r2[b]["hpre_fm"]),
            hmax8=f32(hmax8),
            clip_down=f32(inp["L0_ffn_clip_down"]).reshape(1, 1),
            w_down=f32(inp["L0_ffn_w_down"]),
            ln_gamma=f32(inp["L1_attn_ln_gamma"]).reshape(1, -1),
            ln_beta=f32(inp["L1_attn_ln_beta"]).reshape(1, -1),
            proj_w=f32(inp["L1_attn_proj_w"]),
            proj_b=f32(inp["L1_attn_proj_b"]).reshape(1, -1),
            ffn_w=f32(inp["L1_ffn_weight"]),
            ffn_b=f32(inp["L1_ffn_bias"]).reshape(1, -1),
            delta_w=f32(inp["L1_ffn_delta_w"]),
            delta_act=f32(inp["L1_ffn_delta_act"]).reshape(1, 1),
        )
        for d in "fb":
            m[f"in_w_{d}"] = f32(inp[f"L1_attn_in_w_{d}"])
            m[f"conv_w_{d}"] = f32(inp[f"L1_attn_conv_w_{d}"]).reshape(L1_DI, 4)
            m[f"conv_b_{d}"] = f32(inp[f"L1_attn_conv_b_{d}"])
            m[f"xp_w_{d}"] = f32(inp[f"L1_attn_xp_w_{d}"])
            m[f"dtp_w_{d}"] = f32(inp[f"L1_attn_dtp_w_{d}"])
            m[f"dtp_b_{d}"] = f32(inp[f"L1_attn_dtp_b_{d}"])
            m[f"a_log_{d}"] = f32(inp[f"L1_attn_A_log_{d}"])
            m[f"d_{d}"] = f32(inp[f"L1_attn_D_{d}"])
            m[f"out_w_{d}"] = f32(inp[f"L1_attn_out_w_{d}"])
        im3.append(m)
    r3 = run_bass_kernel_spmd(nc3, im3, core_ids=list(range(B))).results

    out = np.stack([np.asarray(r["x2"], np.float32) for r in r3], axis=0)
    return out


# version 16
# speedup vs baseline: 1.1140x; 1.0082x over previous
import numpy as np
import concourse.bass as bass
import concourse.mybir as mybir
from concourse.bass_utils import run_bass_kernel_spmd
from concourse.alu_op_type import AluOpType as ALU
from contextlib import ExitStack

F32 = mybir.dt.float32
BF16 = mybir.dt.bfloat16
I32 = mybir.dt.int32
AF = mybir.ActivationFunctionType
AX = mybir.AxisListType

T = 4096
D = 1024
MAGIC = 12582912.0

L0_DI = 2048
L0_NH = 32
L0_HD = 64
L0_CONVD = 2304
L0_CH = 256
L0_NCH = T // L0_CH
FFN_I = 2816
L1_DI = 2048
L1_NST = 16
L1_DTR = 64


class Prog:
    ENGINES = ("sync", "vector", "scalar", "tensor", "gpsimd")

    def __init__(self, nc, stack, sems=("ld", "so", "pe", "dv", "ac", "po", "s2", "s3"), cut=None):
        self.cut = cut
        self.total = 0
        self.nc = nc
        self.q = {k: [] for k in self.ENGINES}
        self.sem = {}
        self.cnt = {}
        self._lastwait = {k: {} for k in self.ENGINES}
        for name in sems:
            self.sem[name] = stack.enter_context(nc.semaphore(name))
            self.cnt[name] = 0

    def wait(self, eng, sem, val=None):
        if val is None:
            val = self.cnt[sem]
        if val <= 0:
            return
        lw = self._lastwait[eng]
        if lw.get(sem, -1) >= val:
            return
        lw[sem] = val
        h = self.sem[sem]
        self.q[eng].append(lambda E, h=h, v=val: E.wait_ge(h, v))

    def op(self, eng, fn, sem=None, k=1, waits=()):
        if self.cut is not None and self.total >= self.cut:
            return self.cnt.get(sem, 0)
        self.total += 1
        for w in waits:
            if isinstance(w, tuple):
                self.wait(eng, w[0], w[1])
            else:
                self.wait(eng, w)
        if sem is not None:
            h = self.sem[sem]
            self.cnt[sem] += k
            self.q[eng].append(lambda E, fn=fn, h=h, k=k: fn(E).then_inc(h, k))
        else:
            self.q[eng].append(lambda E, fn=fn: fn(E))
        return self.cnt.get(sem, 0)

    def v(self, fn, waits=(), sem="dv", k=1):
        self.wait("vector", "dv", self.cnt["dv"])
        return self.op("vector", fn, sem, k, waits)

    def s(self, fn, waits=(), sem="ac", k=1):
        self.wait("scalar", "ac", self.cnt["ac"])
        return self.op("scalar", fn, sem, k, waits)

    def t(self, fn, waits=(), sem=None, k=1, base=0):
        if base != getattr(self, "_pe_base", 0):
            self.wait("tensor", "pe", self.cnt["pe"])
            self._pe_base = base
        return self.op("tensor", fn, sem, k, waits)

    def g(self, fn, waits=(), sem="po", k=1):
        self.wait("gpsimd", "po", self.cnt["po"])
        return self.op("gpsimd", fn, sem, k, waits)

    def dma(self, fn, waits=(), q="sync", sem="ld"):
        return self.op(q, fn, sem, 16, waits)

    def barrier(self):
        for eng in self.ENGINES:
            for sem in self.sem:
                self.wait(eng, sem)

    def setup_consts(self, stack, values):
        self.cst = stack.enter_context(self.nc.sbuf_tensor("cstp", [128, max(1, len(values))], F32))
        for i, val in enumerate(values):
            self.nc.const_aps.register(F32, val, self.cst[:, i:i + 1])
            self.g(lambda g, i=i, val=val: g.memset(self.cst[:, i:i + 1], val))
        for eng in ("vector", "scalar", "tensor"):
            self.wait(eng, "po")

    def finalize(self, blk):
        for name in self.ENGINES:
            ops = self.q[name]
            if not ops:
                continue
            def runner(E, ops=ops):
                for f in ops:
                    f(E)
            getattr(blk, name)(runner)


class _StopEmit(Exception):
    pass


def _mm(out, lhsT, rhs, start, stop):
    return lambda p: p.matmul(out=out, lhsT=lhsT, rhs=rhs, start=start, stop=stop)



def build_launch1(debug=False, upto="all", cut=None):
    nc = bass.Bass("TRN2", target_bir_lowering=False, detect_race_conditions=False)
    x0_h = nc.dram_tensor("x0", [T, D], F32, kind="ExternalInput").ap()
    win_h = nc.dram_tensor("w_in", [D, 4384], F32, kind="ExternalInput").ap()
    convw_h = nc.dram_tensor("conv_w", [L0_CONVD, 4], F32, kind="ExternalInput").ap()
    convb_h = nc.dram_tensor("conv_b", [L0_CONVD], F32, kind="ExternalInput").ap()
    dtb_h = nc.dram_tensor("dt_bias", [L0_NH], F32, kind="ExternalInput").ap()
    alog_h = nc.dram_tensor("a_log", [L0_NH], F32, kind="ExternalInput").ap()
    dsk_h = nc.dram_tensor("d_skip", [1, L0_NH], F32, kind="ExternalInput").ap()
    nw_h = nc.dram_tensor("norm_w", [1, L0_DI], F32, kind="ExternalInput").ap()
    wout_h = nc.dram_tensor("w_out", [L0_DI, D], F32, kind="ExternalInput").ap()

    xn_h = nc.dram_tensor("xn_fm", [D, T], BF16, kind="ExternalOutput").ap()
    xmax_h = nc.dram_tensor("xmax", [1, 1], F32, kind="ExternalOutput").ap()
    s0_h = nc.dram_tensor("s0_fm", [D, T], BF16, kind="ExternalOutput").ap()
    dbg = {}
    if debug:
        for nm, shp, dt in (("xbcs", [L0_CONVD, T], BF16), ("dtsp", [L0_NH, T], F32),
                            ("cs", [L0_NH, T], F32), ("y3", [L0_DI, T], BF16),
                            ("xz", [L0_CONVD, T], BF16), ("zz", [T, L0_DI], BF16),
                            ("yraw", [T, L0_DI], F32), ("st0", [128, L0_NH * L0_CH], BF16),
                            ("xssp", [T, L0_DI], BF16),
                            ("cbm0", [128, 2 * L0_CH], BF16), ("xtd0", [128, 2 * L0_DI], BF16),
                            ("csc0", [128, L0_NH * 128], BF16)):
            dbg[nm] = nc.dram_tensor("dbg_" + nm, shp, dt, kind="ExternalOutput").ap()

    xbc_sp = nc.dram_tensor("xbc_sp", [L0_CONVD, T], BF16, kind="Internal").ap()
    xbcs_sp = nc.dram_tensor("xbcs_sp", [L0_CONVD, T], BF16, kind="Internal").ap() if debug else None
    z_sp = nc.dram_tensor("z_sp", [T, L0_DI], BF16, kind="Internal").ap()
    xs_sp = nc.dram_tensor("xs_sp", [T, L0_DI], BF16, kind="Internal").ap()
    y3_sp = nc.dram_tensor("y3_sp", [L0_DI, T], BF16, kind="Internal").ap()

    with ExitStack() as st:
        e = st.enter_context
        P = Prog(nc, st, cut=cut)
        P.setup_consts(st, [1e-5])
        def _fin():
            cut0, P.cut = P.cut, None
            P.op("sync", lambda q: q.wait_ge(P.sem["so"], P.cnt["so"]))
            P.op("sync", lambda q: q.wait_ge(P.sem["ld"], P.cnt["ld"]))
            P.finalize(blk)

        identb = e(nc.sbuf_tensor("identb", [128, 128], BF16))
        ones1 = e(nc.sbuf_tensor("ones1", [1, 128], F32))
        identf = e(nc.sbuf_tensor("identf", [128, 128], F32))
        convw = e(nc.sbuf_tensor("convw", [128, 18 * 4], F32))
        convb = e(nc.sbuf_tensor("convb", [128, 18], F32))
        ndtb = e(nc.sbuf_tensor("ndtb", [L0_NH, 1], F32))
        acol = e(nc.sbuf_tensor("acol", [L0_NH, 1], F32))
        dbig = e(nc.sbuf_tensor("dbig", [128, L0_NH], F32))
        nbig = e(nc.sbuf_tensor("nbig", [128, L0_DI], BF16))
        drow = e(nc.sbuf_tensor("drow", [1, L0_NH], F32))
        nwrow = e(nc.sbuf_tensor("nwrow", [1, L0_DI], F32))
        fmp = e(nc.sbuf_tensor("fmp", [128, T], F32))
        dt_raw = fmp.ap()[0:32, :]
        dt_sp = fmp.ap()[32:64, :]
        cs_fm = fmp.ap()[64:96, :]
        b_fm = e(nc.sbuf_tensor("b_fm", [128, T], BF16))
        c_fm = e(nc.sbuf_tensor("c_fm", [128, T], BF16))
        mask0 = e(nc.sbuf_tensor("mask0", [128, L0_CH], F32))
        mask1 = e(nc.sbuf_tensor("mask1", [128, L0_CH], F32))
        onesc = e(nc.sbuf_tensor("onesc", [L0_NH, L0_CH], F32))
        blk = e(nc.Block())

        P.dma(lambda q: q.dma_start(out=convw[:, :].rearrange("p (s k) -> p s k", s=18), in_=convw_h.rearrange("(s p) k -> p s k", p=128)))
        P.dma(lambda q: q.dma_start(out=convb[:, :], in_=convb_h.rearrange("(s p) -> p s", p=128), allow_slow_non_contiguous=True))
        P.dma(lambda q: q.dma_start(out=ndtb[:, :], in_=dtb_h.unsqueeze(1)))
        P.dma(lambda q: q.dma_start(out=acol[:, :], in_=alog_h.unsqueeze(1)))
        P.dma(lambda q: q.dma_start(out=drow[:, :], in_=dsk_h))
        P.dma(lambda q: q.dma_start(out=nwrow[:, :], in_=nw_h))
        P.g(lambda g: g.memset(identb[:, :], 0.0))
        P.g(lambda g: g.affine_select(out=identb[:, :], in_=identb[:, :], compare_op=ALU.not_equal,
                                      fill=1.0, base=0, pattern=[[-1, 128]], channel_multiplier=1))
        P.g(lambda g: g.memset(identf[:, :], 0.0))
        P.g(lambda g: g.affine_select(out=identf[:, :], in_=identf[:, :], compare_op=ALU.not_equal,
                                      fill=1.0, base=0, pattern=[[-1, 128]], channel_multiplier=1))
        P.g(lambda g: g.memset(ones1[:, :], 1.0))
        P.g(lambda g: g.memset(onesc[:, :], 1.0))
        P.g(lambda g: g.memset(mask0[:, :], 0.0))
        P.g(lambda g: g.affine_select(out=mask0[:, :], in_=mask0[:, :], compare_op=ALU.is_gt,
                                      fill=1.0, base=0, pattern=[[-1, L0_CH]], channel_multiplier=1))
        P.g(lambda g: g.memset(mask1[:, :], 0.0))
        P.g(lambda g: g.affine_select(out=mask1[:, :], in_=mask1[:, :], compare_op=ALU.is_gt,
                                      fill=1.0, base=128, pattern=[[-1, L0_CH]], channel_multiplier=1))
        P.s(lambda s: s.activation(out=acol[:, :], in_=acol[:, :], func=AF.Exp), waits=["ld"])
        P.v(lambda v: v.tensor_scalar_mul(out=acol[:, :], in0=acol[:, :], scalar1=-1.0), waits=["ac"])
        P.v(lambda v: v.tensor_scalar_mul(out=ndtb[:, :], in0=ndtb[:, :], scalar1=-1.0), waits=["ld"])

        if True:
            stA = ExitStack()
            p_c0 = stA.enter_context(nc.psum_tensor("p_c0", [128, 512], F32))
            p_c1 = stA.enter_context(nc.psum_tensor("p_c1", [128, L0_DI], F32))
            P.t(_mm(p_c0[:, 0:L0_NH], ones1[0:1, :], drow[0:1, :], True, True),
                waits=["ld", "po"], sem="pe")
            for j in range(4):
                P.t(_mm(p_c1[:, j * 512:(j + 1) * 512], ones1[0:1, :],
                        nwrow[0:1, j * 512:(j + 1) * 512], True, True),
                    sem="pe", k=(1 if j == 3 else 0) or 1)
            P.v(lambda v: v.tensor_copy(out=dbig[:, :], in_=p_c0[:, 0:L0_NH]), waits=["pe"])
            P.v(lambda v: v.tensor_copy(out=nbig[:, :], in_=p_c1[:, :]))
            stA.close()

            with ExitStack() as stB:
                eb = stB.enter_context
                xn_fm = eb(nc.sbuf_tensor("xn_fm_s", [128, 8 * T], BF16))
                xt0 = eb(nc.sbuf_tensor("xt0", [128, D], F32))
                xt1 = eb(nc.sbuf_tensor("xt1", [128, D], F32))
                stat = eb(nc.sbuf_tensor("stat", [128, 8], F32))
                sqs = eb(nc.sbuf_tensor("sqs", [128, D], F32))
                xnt0 = eb(nc.sbuf_tensor("xnt0", [128, D], BF16))
                xnt1 = eb(nc.sbuf_tensor("xnt1", [128, D], BF16))
                xmc = eb(nc.sbuf_tensor("xmc", [128, 2], F32))
                xm1 = eb(nc.sbuf_tensor("xm1", [1, 1], F32))
                p_tr0 = eb(nc.psum_tensor("p_tr0", [128, 1024], BF16))
                p_tr1 = eb(nc.psum_tensor("p_tr1", [128, 1024], BF16))

                P.v(lambda v: v.memset(xmc[:, 0:1], 0.0))
                evac_tok = [0, 0]
                xnt_tok = [0, 0]
                xnfm_v = xn_fm.ap().rearrange("p (k t) -> p k t", k=8)
                for tt in range(32):
                    xt = xt0 if tt % 2 == 0 else xt1
                    xnt = xnt0 if tt % 2 == 0 else xnt1
                    ptr = p_tr0 if tt % 2 == 0 else p_tr1
                    par = tt % 2
                    P.dma(lambda q, xt=xt, tt=tt: q.dma_start(out=xt[:, :], in_=x0_h[tt * 128:(tt + 1) * 128, :]),
                          waits=[("dv", 0 if tt < 2 else dv_done[par]), ("ac", 0 if tt < 2 else ac_done[par])])
                    P.v(lambda v, xt=xt: v.reduce_sum(out=stat[:, 0:1], in_=xt[:, :], axis=AX.X), waits=["ld"])
                    P.s(lambda s, xt=xt: s.activation(out=sqs[:, :], in_=xt[:, :], func=AF.Square,
                                               accum_out=stat[:, 1:2]), waits=["ld"])
                    P.v(lambda v: v.tensor_scalar_mul(out=stat[:, 2:3], in0=stat[:, 0:1], scalar1=1.0 / D))
                    P.v(lambda v: v.tensor_tensor(out=stat[:, 3:4], in0=stat[:, 2:3], in1=stat[:, 2:3],
                                                  op=ALU.mult))
                    P.v(lambda v: v.tensor_scalar_mul(out=stat[:, 4:5], in0=stat[:, 1:2], scalar1=1.0 / D),
                        waits=["ac"])
                    P.v(lambda v: v.tensor_tensor(out=stat[:, 5:6], in0=stat[:, 4:5], in1=stat[:, 3:4],
                                                  op=ALU.subtract))
                    P.s(lambda s: s.activation(out=stat[:, 6:7], in_=stat[:, 5:6], func=AF.Sqrt,
                                               bias=1e-5), waits=["dv"])
                    P.v(lambda v: v.reciprocal(out=stat[:, 7:8], in_=stat[:, 6:7]), waits=["ac"])
                    P.wait("vector", "pe", xnt_tok[par])
                    if tt >= 2:
                        pass
                    dv_xn = P.v(lambda v, xt=xt, xnt=xnt: v.tensor_scalar(out=xnt[:, :], in0=xt[:, :], scalar1=stat[:, 2:3],
                                                          scalar2=stat[:, 7:8], op0=ALU.subtract, op1=ALU.mult))
                    dv_done_cur = P.v(lambda v, xnt=xnt: v.reduce_max(out=xmc[:, 1:2], in_=xnt[:, :], axis=AX.X,
                                                             apply_absolute_value=True))
                    P.v(lambda v: v.tensor_tensor(out=xmc[:, 0:1], in0=xmc[:, 0:1], in1=xmc[:, 1:2],
                                                  op=ALU.max))
                    if tt == 0:
                        dv_done = [0, 0]
                        ac_done = [0, 0]
                    dv_done[par] = dv_done_cur
                    ac_done[par] = P.cnt["ac"]
                    P.wait("tensor", "dv", dv_xn)
                    P.wait("tensor", "dv", evac_tok[par])
                    for k in range(8):
                        xnt_tok[par] = P.t(lambda p, k=k, ptr=ptr, xnt=xnt:
                                           p.transpose(out=ptr[:, k * 128:(k + 1) * 128],
                                                       in_=xnt[:, k * 128:(k + 1) * 128],
                                                       identity=identb[:, :]),
                                           sem="pe")
                    evac_tok[par] = P.v(lambda v, ptr=ptr, tt=tt:
                                        v.tensor_copy(out=xnfm_v[:, :, tt * 128:(tt + 1) * 128],
                                                      in_=ptr[:, :].rearrange("p (k t) -> p k t", k=8)),
                                        waits=["pe"])
                P.dma(lambda q: q.dma_start(out=xn_h.rearrange("(k p) t -> p k t", p=128),
                                            in_=xn_fm[:, :].rearrange("p (k t) -> p k t", k=8)), waits=[("dv", evac_tok[0]), ("dv", evac_tok[1])],
                      q="gpsimd", sem="so")
                P.g(lambda g: g.reduce_max(xm1[0:1, :], xmc[:, 0:1], axis=AX.C), waits=["dv"])
                P.dma(lambda q: q.dma_start(out=xmax_h, in_=xm1[0:1, 0:1]), waits=["po"],
                      q="gpsimd", sem="so")

                with ExitStack() as stC:
                    ec = stC.enter_context
                    wsl_f0 = ec(nc.sbuf_tensor("wsl_f0", [128, 1024], F32))
                    wsl_f1 = ec(nc.sbuf_tensor("wsl_f1", [128, 1024], F32))
                    wsl0 = ec(nc.sbuf_tensor("wsl0", [128, 1024], BF16))
                    wsl1 = ec(nc.sbuf_tensor("wsl1", [128, 1024], BF16))
                    ev0 = ec(nc.sbuf_tensor("ev0", [128, 512], BF16))
                    ev1 = ec(nc.sbuf_tensor("ev1", [128, 512], BF16))
                    p_m0 = ec(nc.psum_tensor("p_m0", [128, 512], F32))
                    p_m1 = ec(nc.psum_tensor("p_m1", [128, 512], F32))

                    cast_tok = [0, 0]
                    mm_tok = [0, 0]
                    ev_tok = [0, 0]
                    st_tok = [0, 0]
                    for r in range(19):
                        w = 128 if r < 18 else 32
                        c0 = 2048 + r * 128
                        rp = r % 2
                        wf = wsl_f0 if rp == 0 else wsl_f1
                        wb = wsl0 if rp == 0 else wsl1
                        cast_tok[rp] = P.dma(lambda q, wb=wb, c0=c0, w=w:
                                             q.dma_start(out=wb[:, 0:8 * w].rearrange("p (k c) -> p k c", k=8),
                                                         in_=win_h[:, c0:c0 + w].rearrange("(k p) c -> p k c", p=128)),
                                             waits=[("pe", mm_tok[rp])], q="gpsimd", sem="so")
                        P.wait("tensor", "so", cast_tok[rp])
                        for tb in range(8):
                            bp = tb % 2
                            pm = p_m0 if bp == 0 else p_m1
                            ev = ev0 if bp == 0 else ev1
                            P.wait("tensor", "dv", ev_tok[bp])
                            for k in range(8):
                                mm_tok[rp] = P.t(_mm(pm[0:w, :] if w < 128 else pm[:, :],
                                                     wb[:, k * w:(k + 1) * w],
                                                     xn_fm[:, k * T + tb * 512:k * T + (tb + 1) * 512],
                                                     k == 0, k == 7), sem="pe")
                            if r < 18:
                                ev_tok[bp] = P.v(lambda v, ev=ev, pm=pm: v.tensor_copy(out=ev[:, :], in_=pm[:, :]),
                                                 waits=["pe", ("s3", st_tok[bp])])
                                st_tok[bp] = P.dma(lambda q, ev=ev, r=r, tb=tb:
                                                   q.dma_start(out=xbc_sp[r * 128:(r + 1) * 128,
                                                                          tb * 512:(tb + 1) * 512],
                                                               in_=ev[:, :]),
                                                   waits=["dv"], sem="s3")
                            else:
                                ev_tok[bp] = P.v(lambda v, pm=pm, tb=tb:
                                                 v.tensor_copy(out=dt_raw[:, tb * 512:(tb + 1) * 512],
                                                               in_=pm[0:32, :]), waits=["pe"])

                if upto == "ln":
                    dbgst = nc.dram_tensor("dbg_stat", [128, 8], F32, kind="ExternalOutput").ap()
                    dbgxnt = nc.dram_tensor("dbg_xnt", [128, D], BF16, kind="ExternalOutput").ap()
                    P.dma(lambda q: q.dma_start(out=dbgst, in_=stat[:, :]),
                          waits=[("dv", P.cnt["dv"]), ("ac", P.cnt["ac"])], q="gpsimd", sem="so")
                    P.dma(lambda q: q.dma_start(out=dbgxnt, in_=xnt1[:, :]), q="gpsimd", sem="so")
                    _fin()
                    return nc
                P.barrier()
                with ExitStack() as stD:
                    ed = stD.enter_context
                    wz_f = ed(nc.sbuf_tensor("wz_f", [128, 4096], F32))
                    wz = ed(nc.sbuf_tensor("wz", [128, 4096], BF16))
                    evz0 = ed(nc.sbuf_tensor("evz0", [128, 512], BF16))
                    evz1 = ed(nc.sbuf_tensor("evz1", [128, 512], BF16))
                    p_z0 = ed(nc.psum_tensor("p_z0", [128, 512], F32))
                    p_z1 = ed(nc.psum_tensor("p_z1", [128, 512], F32))

                    cast_tok = 0
                    mm_last = 0
                    ev_tok = [0, 0]
                    st_tok = [0, 0]
                    for zc in range(4):
                        cast_tok = P.dma(lambda q, zc=zc: q.dma_start(
                            out=wz[:, :].rearrange("p (k c) -> p k c", k=8),
                            in_=win_h[:, zc * 512:(zc + 1) * 512].rearrange("(k p) c -> p k c", p=128)),
                            waits=[("pe", mm_last)], q="gpsimd", sem="so")
                        P.wait("tensor", "so", cast_tok)
                        for tt in range(32):
                            bp = tt % 2
                            pm = p_z0 if bp == 0 else p_z1
                            ev = evz0 if bp == 0 else evz1
                            P.wait("tensor", "ac", ev_tok[bp])
                            for k in range(8):
                                mm_last = P.t(_mm(pm[:, :],
                                                  xn_fm[:, k * T + tt * 128:k * T + (tt + 1) * 128],
                                                  wz[:, k * 512:(k + 1) * 512], k == 0, k == 7), sem="pe")
                            P.wait("scalar", "s3", st_tok[bp])
                            ev_tok[bp] = P.s(lambda s, ev=ev, pm=pm: s.activation(
                                out=ev[:, :], in_=pm[:, :], func=AF.Silu), waits=["pe"])
                            st_tok[bp] = P.dma(lambda q, ev=ev, zc=zc, tt=tt:
                                               q.dma_start(out=z_sp[tt * 128:(tt + 1) * 128,
                                                                    zc * 512:(zc + 1) * 512],
                                                           in_=ev[:, :]),
                                               waits=["ac"], sem="s3")

        if debug:
            P.dma(lambda q: q.dma_start(out=dbg["xz"], in_=xbc_sp), waits=[("so", P.cnt["so"])])
            P.dma(lambda q: q.dma_start(out=dbg["zz"], in_=z_sp))

        if upto == "z":
            _fin()
            return nc
        P.barrier()
        with ExitStack() as stE:
            ee = stE.enter_context
            convx0 = ee(nc.sbuf_tensor("convx0", [128, T + 3], BF16))
            convx1 = ee(nc.sbuf_tensor("convx1", [128, T + 3], BF16))
            convacc = ee(nc.sbuf_tensor("convacc", [128, T], F32))
            convo = ee(nc.sbuf_tensor("convo", [128, T], BF16))
            trst = ee(nc.sbuf_tensor("trst", [128, 1024], BF16))
            p_trE = ee(nc.psum_tensor("p_trE", [128, 1024], BF16))

            P.g(lambda g: g.memset(convx0[:, 0:3], 0.0))
            P.g(lambda g: g.memset(convx1[:, 0:3], 0.0))
            conv_tok = [0, 0]
            silu_tok = 0
            trs_tok = 0
            dma_tok = 0
            for s_ in range(18):
                sp = s_ % 2
                cvx = convx0 if sp == 0 else convx1
                P.dma(lambda q, cvx=cvx, s_=s_: q.dma_start(out=cvx[:, 3:3 + T],
                                                            in_=xbc_sp[s_ * 128:(s_ + 1) * 128, :]),
                      waits=[("ac", conv_tok[sp]), ("po", P.cnt["po"])])
                P.v(lambda v, cvx=cvx, s_=s_: v.tensor_scalar_mul(out=convacc[:, :], in0=cvx[:, 0:T],
                                                                  scalar1=convw[:, s_ * 4:s_ * 4 + 1]),
                    waits=["ld", ("ac", silu_tok)])
                for k in range(1, 4):
                    P.v(lambda v, cvx=cvx, s_=s_, k=k:
                        v.scalar_tensor_tensor(out=convacc[:, :], in0=cvx[:, k:k + T],
                                               scalar=convw[:, s_ * 4 + k:s_ * 4 + k + 1],
                                               in1=convacc[:, :], op0=ALU.mult, op1=ALU.add))
                prev_use = trs_tok if s_ <= 16 else P.cnt["dv"]
                conv_tok[sp] = P.s(lambda s, s_=s_: s.activation(out=convo[:, :], in_=convacc[:, :],
                                                                 func=AF.Silu, bias=convb[:, s_:s_ + 1]),
                                   waits=["dv", ("dv", prev_use)])
                silu_tok = conv_tok[sp]
                if s_ < 16:
                    if debug:
                        P.dma(lambda q, s_=s_: q.dma_start(out=dbg["xbcs"][s_ * 128:(s_ + 1) * 128, :],
                                                           in_=convo[:, :]), waits=["ac"], q="gpsimd", sem="so")
                    for g4 in range(4):
                        P.wait("tensor", "ac", conv_tok[sp])
                        P.wait("tensor", "dv", trs_tok)
                        for u in range(8):
                            tt = g4 * 8 + u
                            pe_tok = P.t(lambda p, u=u, tt=tt: p.transpose(
                                out=p_trE[:, u * 128:(u + 1) * 128],
                                in_=convo[:, tt * 128:(tt + 1) * 128], identity=identb[:, :]), sem="pe")
                        trs_tok = P.v(lambda v: v.tensor_copy(out=trst[:, :], in_=p_trE[:, :]),
                                      waits=["pe", ("so", dma_tok)])
                        dma_tok = P.dma(lambda q, g4=g4, s_=s_: q.dma_start(
                            out=xs_sp[g4 * 1024:(g4 + 1) * 1024, s_ * 128:(s_ + 1) * 128]
                                .rearrange("(u p) c -> p u c", p=128),
                            in_=trst[:, :].rearrange("p (u c) -> p u c", u=8)), waits=["dv"], q="gpsimd", sem="so")
                elif s_ == 16:
                    trs_tok = P.v(lambda v: v.tensor_copy(out=b_fm[:, :], in_=convo[:, :]), waits=["ac"])
                    if debug:
                        P.dma(lambda q: q.dma_start(out=dbg["xbcs"][2048:2176, :], in_=convo[:, :]),
                              waits=["ac"], q="gpsimd", sem="so")
                else:
                    trs_tok = P.v(lambda v: v.tensor_copy(out=c_fm[:, :], in_=convo[:, :]), waits=["ac"])
                    if debug:
                        P.dma(lambda q: q.dma_start(out=dbg["xbcs"][2176:2304, :], in_=convo[:, :]),
                              waits=["ac"], q="gpsimd", sem="so")

        if upto == "conv":
            _fin()
            return nc
        P.barrier()
        P.s(lambda s: s.activation(out=dt_sp, in_=dt_raw, func=AF.Sigmoid,
                                   scale=-1.0, bias=ndtb[:, 0:1]), waits=["dv"])
        P.s(lambda s: s.activation(out=cs_fm, in_=dt_sp, func=AF.Ln))
        P.v(lambda v: v.tensor_scalar(out=dt_sp, in0=cs_fm, scalar1=-90.0, scalar2=-1.0,
                                      op0=ALU.max, op1=ALU.mult), waits=["ac"])
        if debug:
            P.dma(lambda q: q.dma_start(out=dbg["dtsp"], in_=dt_sp), waits=["dv"], q="gpsimd", sem="so")
        P.v(lambda v: v.tensor_scalar_mul(out=dt_raw, in0=dt_sp, scalar1=acol[:, 0:1]))
        for c in range(L0_NCH):
            P.v(lambda v, c=c: v.tensor_tensor_scan(out=cs_fm[:, c * L0_CH:(c + 1) * L0_CH],
                                                    data0=onesc[:, :], data1=dt_raw[:, c * L0_CH:(c + 1) * L0_CH],
                                                    initial=0.0, op0=ALU.mult, op1=ALU.add))
        if debug:
            P.dma(lambda q: q.dma_start(out=dbg["cs"], in_=cs_fm), waits=["dv"], q="gpsimd", sem="so")

        if upto == "dt":
            _fin()
            return nc
        P.barrier()
        with ExitStack() as stG:
            eg = stG.enter_context
            cs_tm0 = eg(nc.sbuf_tensor("cs_tm0", [128, L0_NH], F32))
            cs_tm1 = eg(nc.sbuf_tensor("cs_tm1", [128, L0_NH], F32))
            ncs_tm0 = eg(nc.sbuf_tensor("ncs_tm0", [128, L0_NH], F32))
            ncs_tm1 = eg(nc.sbuf_tensor("ncs_tm1", [128, L0_NH], F32))
            dt_tm0 = eg(nc.sbuf_tensor("dt_tm0", [128, L0_NH], F32))
            dt_tm1 = eg(nc.sbuf_tensor("dt_tm1", [128, L0_NH], F32))
            dec_tm0 = eg(nc.sbuf_tensor("dec_tm0", [128, L0_NH], F32))
            dec_tm1 = eg(nc.sbuf_tensor("dec_tm1", [128, L0_NH], F32))
            dec_fm = eg(nc.sbuf_tensor("dec_fm", [L0_NH, L0_CH], F32))
            dec_sc = eg(nc.sbuf_tensor("dec_sc", [L0_NH, L0_CH], F32))
            csflat = eg(nc.sbuf_tensor("csflat", [1, L0_NH * L0_CH], F32))
            ecsb = eg(nc.sbuf_tensor("ecsb", [128, 512], F32))
            btm = eg(nc.sbuf_tensor("btm", [128, 2 * 128], BF16))
            xs_tm = eg(nc.sbuf_tensor("xs_tm", [128, 2 * L0_DI], BF16))
            xtd = eg(nc.sbuf_tensor("xtd", [128, 2 * L0_DI], BF16))
            xtdd = eg(nc.sbuf_tensor("xtdd", [128, 2 * L0_DI], BF16))
            cbm = eg(nc.sbuf_tensor("cbm", [128, 2 * L0_CH], BF16))
            e_sb = eg(nc.sbuf_tensor("e_sb", [128, 8 * L0_CH], BF16))
            s_t0 = eg(nc.sbuf_tensor("s_t0", [128, L0_NH * L0_CH], BF16))
            s_t1 = eg(nc.sbuf_tensor("s_t1", [128, L0_NH * L0_CH], BF16))
            csc = eg(nc.sbuf_tensor("csc", [128, L0_NH * 128], BF16))
            prev0 = eg(nc.sbuf_tensor("prev0", [128, L0_NH * L0_HD], F32))
            prev1 = eg(nc.sbuf_tensor("prev1", [128, L0_NH * L0_HD], F32))
            cdbig = eg(nc.sbuf_tensor("cdbig", [128, L0_NH], F32))
            prevb = eg(nc.sbuf_tensor("prevb", [128, L0_NH * L0_HD], BF16))
            ytile = eg(nc.sbuf_tensor("ytile", [128, L0_DI], F32))
            ztile = eg(nc.sbuf_tensor("ztile", [128, L0_DI], BF16))
            gsil = eg(nc.sbuf_tensor("gsil", [128, L0_DI], BF16))
            y2 = eg(nc.sbuf_tensor("y2", [128, L0_DI], F32))
            rmsc = eg(nc.sbuf_tensor("rmsc", [128, 4], F32))
            y3t = eg(nc.sbuf_tensor("y3t", [128, L0_DI], BF16))
            yfm = eg(nc.sbuf_tensor("yfm", [128, 2048], BF16))
            p_y = eg(nc.psum_tensor("p_y", [128, L0_DI], F32))
            p_b0 = eg(nc.psum_tensor("p_b0", [128, 512], F32))
            p_b1 = eg(nc.psum_tensor("p_b1", [128, 512], F32))
            p_cb = eg(nc.psum_tensor("p_cb", [128, L0_CH], F32))
            p_trG = eg(nc.psum_tensor("p_trG", [128, 1024], BF16))

            P.g(lambda g: g.memset(prev0[:, :], 0.0))
            P.g(lambda g: g.memset(prevb[:, :], 0.0))
            dt_ready = P.cnt["dv"]
            tok_xs_rd = 0
            tok_st_rd = 0
            tok_z_rd = 0
            tok_flat_rd = 0

            for c in range(1 if upto.startswith("ssd1") else L0_NCH):
                c0 = c * L0_CH
                prev_c = prev0 if c % 2 == 0 else prev1
                prev_n = prev1 if c % 2 == 0 else prev0

                P.v(lambda v, c0=c0: v.tensor_scalar(out=dec_sc[:, :], in0=cs_fm[:, c0:c0 + L0_CH],
                                                     scalar1=cs_fm[:, c0 + 255:c0 + 256],
                                                     scalar2=-1.0, op0=ALU.subtract, op1=ALU.mult),
                    waits=[("dv", dt_ready)])
                P.s(lambda s: s.activation(out=dec_fm[:, :], in_=dec_sc[:, :], func=AF.Exp), waits=["dv"])
                P.dma(lambda q, c0=c0: q.dma_start(out=csflat[:, :], in_=cs_fm[:, c0:c0 + L0_CH]),
                      waits=[("pe", tok_flat_rd)], q="gpsimd", sem="s2")
                for jb, (cst, dtt, dect) in enumerate(((cs_tm0, dt_tm0, dec_tm0), (cs_tm1, dt_tm1, dec_tm1))):
                    P.wait("tensor", "ac", P.cnt["ac"])
                    P.wait("tensor", "dv", P.cnt["dv"])
                    P.t(lambda p, c0=c0, jb=jb: p.transpose(out=p_b0[:, 0:32],
                                                            in_=cs_fm[:, c0 + jb * 128:c0 + (jb + 1) * 128],
                                                            identity=identf[64:96, 64:96]), sem="pe", base=64)
                    P.t(lambda p, c0=c0, jb=jb: p.transpose(out=p_b0[:, 32:64],
                                                            in_=dt_sp[:, c0 + jb * 128:c0 + (jb + 1) * 128],
                                                            identity=identf[32:64, 32:64]), sem="pe", base=32)
                    P.t(lambda p, jb=jb: p.transpose(out=p_b0[:, 64:96],
                                                     in_=dec_fm[:, jb * 128:(jb + 1) * 128],
                                                     identity=identf[0:32, 0:32]), sem="pe")
                    P.v(lambda v, cst=cst: v.tensor_copy(out=cst[:, :], in_=p_b0[:, 0:32]), waits=["pe"])
                    P.v(lambda v, dtt=dtt: v.tensor_copy(out=dtt[:, :], in_=p_b0[:, 32:64]))
                    P.v(lambda v, dect=dect: v.tensor_copy(out=dect[:, :], in_=p_b0[:, 64:96]))
                    P.v(lambda v, cst=cst, ncst=(ncs_tm0 if jb == 0 else ncs_tm1):
                        v.tensor_scalar_mul(out=ncst[:, :], in0=cst[:, :], scalar1=-1.0))
                for jb in range(2):
                    P.wait("tensor", "dv", P.cnt["dv"])
                    P.t(lambda p, c0=c0, jb=jb: p.transpose(out=p_trG[:, 0:128],
                                                            in_=b_fm[:, c0 + jb * 128:c0 + (jb + 1) * 128],
                                                            identity=identb[:, :]), sem="pe")
                    P.v(lambda v, jb=jb: v.tensor_copy(out=btm[:, jb * 128:(jb + 1) * 128], in_=p_trG[:, 0:128]),
                        waits=["pe"])
                for jb in range(2):
                    P.dma(lambda q, c0=c0, jb=jb: q.dma_start(
                        out=xs_tm[:, jb * L0_DI:(jb + 1) * L0_DI],
                        in_=xs_sp[c0 + jb * 128:c0 + (jb + 1) * 128, :]),
                        waits=[("dv", tok_xs_rd)])
                    xsv = xs_tm.ap().rearrange("p (j h q) -> p j h q", j=2, h=L0_NH)
                    dtv = (dt_tm0 if jb == 0 else dt_tm1).ap().unsqueeze(2)
                    dcv = (dec_tm0 if jb == 0 else dec_tm1).ap().unsqueeze(2)
                    P.v(lambda v, jb=jb, xsv=xsv, dtv=dtv: v.tensor_tensor(
                        out=xtd.ap().rearrange("p (j h q) -> p j h q", j=2, h=L0_NH)[:, jb, :, :],
                        in0=xsv[:, jb, :, :], in1=dtv.broadcast_to([128, L0_NH, L0_HD]), op=ALU.mult),
                        waits=["ld", "dv", ("pe", P.cnt["pe"])])
                    P.v(lambda v, jb=jb, dcv=dcv: v.tensor_tensor(
                        out=xtdd.ap().rearrange("p (j h q) -> p j h q", j=2, h=L0_NH)[:, jb, :, :],
                        in0=xtd.ap().rearrange("p (j h q) -> p j h q", j=2, h=L0_NH)[:, jb, :, :],
                        in1=dcv.broadcast_to([128, L0_NH, L0_HD]), op=ALU.mult))

                if upto == "ssd1a":
                    _fin()
                    return nc
                for jb in range(2):
                    P.wait("tensor", "dv", P.cnt["dv"])
                    P.wait("tensor", "ac", P.cnt["ac"])
                    P.t(_mm(p_cb[:, :], b_fm[:, c0 + jb * 128:c0 + (jb + 1) * 128],
                            c_fm[:, c0:c0 + L0_CH], True, True), sem="pe")
                    P.v(lambda v, jb=jb: v.tensor_tensor(out=cbm[:, jb * L0_CH:(jb + 1) * L0_CH],
                                                         in0=p_cb[:, :],
                                                         in1=(mask0 if jb == 0 else mask1)[:, :], op=ALU.mult),
                        waits=["pe"])

                if upto == "ssd1b":
                    _fin()
                    return nc
                exp_tok = [0, 0]
                tok_esb_rd = P.cnt["dv"]
                for jb, s_t in enumerate((s_t0, s_t1)):
                    ncst = ncs_tm0 if jb == 0 else ncs_tm1
                    for hg in range(4):
                        P.wait("tensor", "so", P.cnt["so"])
                        P.wait("tensor", "s2", P.cnt["s2"])
                        P.wait("tensor", "ac", P.cnt["ac"])
                        P.wait("tensor", "dv", P.cnt["dv"])
                        for hh in range(8):
                            h = hg * 8 + hh
                            pb = p_b0 if h % 2 == 0 else p_b1
                            P.wait("tensor", "ac", exp_tok[h % 2])
                            P.t(_mm(pb[:, 0:L0_CH], ones1[0:1, :],
                                    csflat[0:1, h * L0_CH:(h + 1) * L0_CH], True, True), sem="pe")
                            exp_tok[h % 2] = P.s(lambda s, pb=pb, hh=hh, h=h, ncst=ncst: s.activation(
                                out=e_sb[:, hh * L0_CH:(hh + 1) * L0_CH], in_=pb[:, 0:L0_CH], func=AF.Exp,
                                bias=ncst[:, h:h + 1]), waits=["pe", ("dv", tok_esb_rd)])
                        P.v(lambda v: v.tensor_scalar_min(out=e_sb[:, :], in0=e_sb[:, :], scalar1=1e30),
                            waits=["ac"])
                        cbv = cbm.ap().rearrange("p (j i) -> p j i", j=2)[:, jb, :].unsqueeze(1)
                        P.v(lambda v, s_t=s_t, cbv=cbv, hg=hg: v.tensor_tensor(
                            out=s_t.ap().rearrange("p (h i) -> p h i", h=L0_NH)[:, hg * 8:(hg + 1) * 8, :],
                            in0=e_sb.ap().rearrange("p (h i) -> p h i", h=8),
                            in1=cbv.broadcast_to([128, 8, L0_CH]), op=ALU.mult),
                            waits=["ac", ("pe", tok_st_rd if hg == 0 and jb == 0 else 0)])
                        tok_esb_rd = P.cnt["dv"]

                if debug and c == 0:
                    P.dma(lambda q: q.dma_start(out=dbg["st0"], in_=s_t0[:, :]), waits=["dv"],
                          q="gpsimd", sem="so")
                    P.dma(lambda q: q.dma_start(out=dbg["cbm0"], in_=cbm[:, :]), q="gpsimd", sem="so")
                    P.dma(lambda q: q.dma_start(out=dbg["xtd0"], in_=xtd[:, :]), q="gpsimd", sem="so")
                    P.barrier()
                if upto == "ssd1c":
                    _fin()
                    return nc
                for ib in range(2):
                    csfv = csflat.ap().rearrange("a (h i) -> a h i", h=L0_NH)
                    for u in range(8):
                        pb = p_b0 if u % 2 == 0 else p_b1
                        P.wait("tensor", "so", P.cnt["so"])
                        P.wait("tensor", "ac", P.cnt["ac"])
                        P.t(_mm(pb[:, :], ones1[0:1, :],
                                csfv[:, u * 4:(u + 1) * 4, ib * 128:(ib + 1) * 128], True, True),
                            sem="pe")
                        P.s(lambda s, pb=pb: s.activation(out=ecsb[:, :], in_=pb[:, :], func=AF.Exp),
                            waits=["pe", ("dv", P.cnt["dv"])])
                        cv = c_fm[:, c0 + ib * 128:c0 + (ib + 1) * 128].unsqueeze(1)
                        P.v(lambda v, u=u, cv=cv: v.tensor_tensor(
                            out=csc.ap().rearrange("p (h i) -> p h i", h=L0_NH)[:, u * 4:(u + 1) * 4, :],
                            in0=ecsb.ap().rearrange("p (a i) -> p a i", a=4),
                            in1=cv.broadcast_to([128, 4, 128]), op=ALU.mult),
                            waits=["ac", ("pe", tok_st_rd if u == 0 else 0)])
                    if debug and c == 0 and ib == 0:
                        P.dma(lambda q: q.dma_start(out=dbg["csc0"], in_=csc[:, :]), waits=["dv"],
                              q="gpsimd", sem="so")
                        P.barrier()
                    P.wait("tensor", "dv", P.cnt["dv"])
                    for jb, s_t in enumerate((s_t0, s_t1)):
                        for h in range(L0_NH):
                            P.t(_mm(p_y[:, h * L0_HD:(h + 1) * L0_HD],
                                    s_t[:, h * L0_CH + ib * 128:h * L0_CH + ib * 128 + 128],
                                    xtd[:, jb * L0_DI + h * L0_HD:jb * L0_DI + (h + 1) * L0_HD],
                                    jb == 0 and h % 8 == 0, False), sem="pe")
                    for h in range(L0_NH):
                        P.t(_mm(p_y[:, h * L0_HD:(h + 1) * L0_HD],
                                csc[:, h * 128:(h + 1) * 128],
                                prevb[:, h * L0_HD:(h + 1) * L0_HD],
                                False, h % 8 == 7), sem="pe")
                    tok_st_rd = P.cnt["pe"]
                    P.dma(lambda q, c0=c0, ib=ib: q.dma_start(out=ztile[:, :],
                                                              in_=z_sp[c0 + ib * 128:c0 + (ib + 1) * 128, :]),
                          waits=[("dv", tok_z_rd)])
                    xsv = xs_tm.ap().rearrange("p (j h q) -> p j h q", j=2, h=L0_NH)
                    dbv = dbig.ap().unsqueeze(2)
                    tok_xs_rd = P.v(lambda v, xsv=xsv, ib=ib, dbv=dbv: v.tensor_tensor(
                        out=y2.ap().rearrange("p (h q) -> p h q", h=L0_NH),
                        in0=xsv[:, ib, :, :], in1=dbv.broadcast_to([128, L0_NH, L0_HD]), op=ALU.mult),
                        waits=["pe"])
                    P.v(lambda v: v.tensor_tensor(out=ytile[:, :], in0=y2[:, :], in1=p_y[:, :], op=ALU.add))
                    if debug:
                        P.dma(lambda q, c0=c0, ib=ib: q.dma_start(
                            out=dbg["yraw"][c0 + ib * 128:c0 + (ib + 1) * 128, :], in_=ytile[:, :]),
                            waits=["dv"], q="gpsimd", sem="so")
                        P.barrier()
                    tok_z_rd = P.v(lambda v: v.tensor_tensor(out=y2[:, :], in0=ytile[:, :], in1=ztile[:, :],
                                                             op=ALU.mult), waits=["ld"])
                    P.s(lambda s: s.activation(out=ytile[:, :], in_=y2[:, :], func=AF.Square,
                                               accum_out=rmsc[:, 0:1]), waits=["dv"])
                    P.s(lambda s: s.activation(out=rmsc[:, 1:2], in_=rmsc[:, 0:1], func=AF.Ln,
                                               scale=1.0 / L0_DI, bias=1e-5))
                    P.s(lambda s: s.activation(out=rmsc[:, 2:3], in_=rmsc[:, 1:2], func=AF.Exp,
                                               scale=-0.5))
                    P.wait("vector", "ac", P.cnt["ac"])
                    P.v(lambda v: v.tensor_scalar_mul(out=ytile[:, :], in0=y2[:, :], scalar1=rmsc[:, 2:3]))
                    P.v(lambda v: v.tensor_tensor(out=y3t[:, :], in0=ytile[:, :], in1=nbig[:, :], op=ALU.mult))

                    for g8 in range(2):
                        P.wait("tensor", "dv", P.cnt["dv"])
                        for u in range(8):
                            kk = g8 * 8 + u
                            P.t(lambda p, u=u, kk=kk: p.transpose(out=p_trG[:, u * 128:(u + 1) * 128],
                                                                  in_=y3t[:, kk * 128:(kk + 1) * 128],
                                                                  identity=identb[:, :]), sem="pe")
                        P.v(lambda v, g8=g8: v.tensor_copy(out=yfm[:, g8 * 1024:(g8 + 1) * 1024],
                                                           in_=p_trG[:, :]), waits=["pe"])
                    P.dma(lambda q, c0=c0, ib=ib: q.dma_start(
                        out=y3_sp[:, c0 + ib * 128:c0 + (ib + 1) * 128].rearrange("(u p) t -> p u t", p=128),
                        in_=yfm[:, :].rearrange("p (u t) -> p u t", u=16)), waits=["dv"], q="gpsimd", sem="so")

                if upto == "ssd1d":
                    _fin()
                    return nc
                P.wait("tensor", "dv", P.cnt["dv"])
                P.wait("tensor", "so", P.cnt["so"])
                for jb in range(2):
                    for h in range(L0_NH):
                        P.t(_mm(p_y[:, h * L0_HD:(h + 1) * L0_HD],
                                btm[:, jb * 128:(jb + 1) * 128],
                                xtdd[:, jb * L0_DI + h * L0_HD:jb * L0_DI + (h + 1) * L0_HD],
                                jb == 0 and h % 8 == 0, jb == 1 and h % 8 == 7), sem="pe")
                tok_flat_rd = P.t(_mm(p_cb[:, 0:L0_NH], ones1[0:1, :], csflat[0:1, 255::L0_CH], True, True), sem="pe") or P.cnt["pe"]
                P.s(lambda s: s.activation(out=cdbig[:, :], in_=p_cb[:, 0:L0_NH], func=AF.Exp),
                    waits=["pe"])
                cdv = cdbig.ap().unsqueeze(2)
                P.wait("vector", "ac", P.cnt["ac"])
                P.v(lambda v, prev_c=prev_c, prev_n=prev_n, cdv=cdv: v.tensor_tensor(
                    out=prev_n.ap().rearrange("p (h q) -> p h q", h=L0_NH),
                    in0=prev_c.ap().rearrange("p (h q) -> p h q", h=L0_NH),
                    in1=cdv.broadcast_to([128, L0_NH, L0_HD]), op=ALU.mult))
                P.v(lambda v, prev_n=prev_n: v.tensor_tensor(out=prev_n[:, :], in0=prev_n[:, :],
                                                             in1=p_y[:, :], op=ALU.add))
                P.v(lambda v, prev_n=prev_n: v.tensor_copy(out=prevb[:, :], in_=prev_n[:, :]))

        if upto in ("ssd", "ssd1"):
            _fin()
            return nc
        if debug:
            P.dma(lambda q: q.dma_start(out=dbg["y3"], in_=y3_sp), waits=[("so", P.cnt["so"])],
                  q="gpsimd", sem="so")
            P.dma(lambda q: q.dma_start(out=dbg["xssp"], in_=xs_sp), q="gpsimd", sem="so")
        P.barrier()
        with ExitStack() as stH:
            eh = stH.enter_context
            wo_f = eh(nc.sbuf_tensor("wo_f", [128, 2048], F32))
            wo = eh(nc.sbuf_tensor("wo", [128, 8 * 2048], BF16))
            yin = eh(nc.sbuf_tensor("yin", [128, 16 * 512], BF16))
            evo0 = eh(nc.sbuf_tensor("evo0", [128, 512], BF16))
            evo1 = eh(nc.sbuf_tensor("evo1", [128, 512], BF16))
            p_o0 = eh(nc.psum_tensor("p_o0", [128, 512], F32))
            p_o1 = eh(nc.psum_tensor("p_o1", [128, 512], F32))
            cast_tok = 0
            for ot in range(8):
                cast_tok = P.dma(lambda q, ot=ot: q.dma_start(
                    out=wo[:, ot * 2048:(ot + 1) * 2048].rearrange("p (k c) -> p k c", k=16),
                    in_=wout_h[:, ot * 128:(ot + 1) * 128].rearrange("(k p) c -> p k c", p=128)),
                    q="gpsimd", sem="so")
            ev_tok = [0, 0]
            st_tok = [0, 0]
            for tb in range(8):
                P.dma(lambda q, tb=tb: q.dma_start(
                    out=yin[:, :].rearrange("p (k t) -> p k t", k=16),
                    in_=y3_sp[:, tb * 512:(tb + 1) * 512].rearrange("(k p) t -> p k t", p=128)),
                    waits=[("pe", P.cnt["pe"]), ("so", P.cnt["so"])])
                for ot in range(8):
                    bp = ot % 2
                    pm = p_o0 if bp == 0 else p_o1
                    ev = evo0 if bp == 0 else evo1
                    P.wait("tensor", "ld", P.cnt["ld"])
                    P.wait("tensor", "so", cast_tok)
                    P.wait("tensor", "dv", ev_tok[bp])
                    for k in range(16):
                        P.t(_mm(pm[:, :], wo[:, ot * 2048 + k * 128:ot * 2048 + (k + 1) * 128],
                                yin[:, k * 512:(k + 1) * 512], k == 0, k == 15), sem="pe")
                    ev_tok[bp] = P.v(lambda v, ev=ev, pm=pm: v.tensor_copy(out=ev[:, :], in_=pm[:, :]),
                                     waits=["pe", ("so", st_tok[bp])])
                    st_tok[bp] = P.dma(lambda q, ev=ev, ot=ot, tb=tb: q.dma_start(
                        out=s0_h[ot * 128:(ot + 1) * 128, tb * 512:(tb + 1) * 512], in_=ev[:, :]),
                        waits=["dv"], q="gpsimd", sem="so")

        _fin()
    return nc




def build_launch2():
    nc = bass.Bass("TRN2", target_bir_lowering=False, detect_race_conditions=False)
    xn_h = nc.dram_tensor("xn_fm", [D, T], BF16, kind="ExternalInput").ap()
    xmax8_h = nc.dram_tensor("xmax8", [1, 8], F32, kind="ExternalInput").ap()
    clip_h = nc.dram_tensor("clip_x", [1, 1], F32, kind="ExternalInput").ap()
    wg_h = nc.dram_tensor("w_gate", [FFN_I, D], F32, kind="ExternalInput").ap()
    wu_h = nc.dram_tensor("w_up", [FFN_I, D], F32, kind="ExternalInput").ap()
    hpre_h = nc.dram_tensor("hpre_fm", [FFN_I, T], BF16, kind="ExternalOutput").ap()
    hmax_h = nc.dram_tensor("hmax", [1, 1], F32, kind="ExternalOutput").ap()

    with ExitStack() as st:
        e = st.enter_context
        P = Prog(nc, st)
        P.setup_consts(st, [1e-5])
        identb = e(nc.sbuf_tensor("identb", [128, 128], BF16))
        ones1 = e(nc.sbuf_tensor("ones1", [1, 128], F32))
        sc = e(nc.sbuf_tensor("sc", [1, 8], F32))
        invcol = e(nc.sbuf_tensor("invcol", [128, 1], F32))
        scol = e(nc.sbuf_tensor("scol", [128, 1], F32))
        xq = e(nc.sbuf_tensor("xq", [128, 8 * T], BF16))
        xqs = e(nc.sbuf_tensor("xqs", [128, T], F32))
        wr_f = e(nc.sbuf_tensor("wr_f", [128, D], F32))
        wq_g = e(nc.sbuf_tensor("wq_g", [128, D], BF16))
        wq_u = e(nc.sbuf_tensor("wq_u", [128, D], BF16))
        wsl_g = e(nc.sbuf_tensor("wsl_g", [128, 8 * 128], BF16))
        wsl_u = e(nc.sbuf_tensor("wsl_u", [128, 8 * 128], BF16))
        wstat = e(nc.sbuf_tensor("wstat", [128, 8], F32))
        scg = e(nc.sbuf_tensor("scg", [128, 1], F32))
        scu = e(nc.sbuf_tensor("scu", [128, 1], F32))
        gs = e(nc.sbuf_tensor("gs", [128, 512], F32))
        us = e(nc.sbuf_tensor("us", [128, 512], F32))
        hp0 = e(nc.sbuf_tensor("hp0", [128, 512], BF16))
        hp1 = e(nc.sbuf_tensor("hp1", [128, 512], BF16))
        hmc = e(nc.sbuf_tensor("hmc", [128, 2], F32))
        hm1 = e(nc.sbuf_tensor("hm1", [1, 1], F32))
        p_g0 = e(nc.psum_tensor("p_g0", [128, 512], F32))
        p_g1 = e(nc.psum_tensor("p_g1", [128, 512], F32))
        p_u0 = e(nc.psum_tensor("p_u0", [128, 512], F32))
        p_u1 = e(nc.psum_tensor("p_u1", [128, 512], F32))
        p_t = e(nc.psum_tensor("p_t", [128, 1024], BF16))
        p_bc = e(nc.psum_tensor("p_bc", [128, 8], F32))
        blk = e(nc.Block())

        P.g(lambda g: g.memset(identb[:, :], 0.0))
        P.g(lambda g: g.affine_select(out=identb[:, :], in_=identb[:, :], compare_op=ALU.not_equal,
                                      fill=1.0, base=0, pattern=[[-1, 128]], channel_multiplier=1))
        P.g(lambda g: g.memset(ones1[:, :], 1.0))
        P.v(lambda v: v.memset(hmc[:, 0:1], 0.0))
        P.dma(lambda q: q.dma_start(out=sc[0:1, 0:8], in_=xmax8_h))
        P.dma(lambda q: q.dma_start(out=sc[0:1, 0:1], in_=clip_h), q="gpsimd", sem="so")
        P.v(lambda v: v.reduce_max(out=wstat[0:1, 0:1], in_=sc[0:1, 0:8], axis=AX.X), waits=["ld"])
        P.s(lambda s: s.activation(out=wstat[0:1, 1:2], in_=sc[0:1, 0:1], func=AF.Sigmoid),
            waits=["so", "dv"])
        P.v(lambda v: v.tensor_tensor(out=wstat[0:1, 2:3], in0=wstat[0:1, 0:1], in1=wstat[0:1, 1:2],
                                      op=ALU.mult), waits=["ac"])
        P.v(lambda v: v.tensor_scalar_mul(out=wstat[0:1, 3:4], in0=wstat[0:1, 2:3], scalar1=1.0 / 127.0))
        P.v(lambda v: v.reciprocal(out=wstat[0:1, 4:5], in_=wstat[0:1, 3:4]))
        P.t(_mm(p_bc[:, 0:2], ones1[0:1, :], wstat[0:1, 3:5], True, True), waits=["dv", "po"], sem="pe")
        P.v(lambda v: v.tensor_copy(out=scol[:, :], in_=p_bc[:, 0:1]), waits=["pe"])
        P.v(lambda v: v.tensor_copy(out=invcol[:, :], in_=p_bc[:, 1:2]))
        for k in range(8):
            P.dma(lambda q, k=k: q.dma_start(out=xq[:, k * T:(k + 1) * T], in_=xn_h[k * 128:(k + 1) * 128, :]))
            P.v(lambda v, k=k: v.tensor_scalar(out=xqs[:, :], in0=xq[:, k * T:(k + 1) * T],
                                               scalar1=invcol[:, 0:1], scalar2=MAGIC,
                                               op0=ALU.mult, op1=ALU.add), waits=["ld"])
            P.v(lambda v: v.tensor_scalar(out=xqs[:, :], in0=xqs[:, :], scalar1=MAGIC, scalar2=127.0,
                                          op0=ALU.subtract, op1=ALU.min))
            P.v(lambda v, k=k: v.tensor_scalar(out=xq[:, k * T:(k + 1) * T], in0=xqs[:, :],
                                               scalar1=-128.0, scalar2=None if False else 0.0,
                                               op0=ALU.max, op1=ALU.add))
        FFN_RT = FFN_I // 128
        ev_par = [0, 0]
        st_par = [0, 0]
        hp_tok = 0
        for rt in range(FFN_RT):
            for wsrc, wdst, slab, scl in ((wg_h, wq_g, wsl_g, scg), (wu_h, wq_u, wsl_u, scu)):
                P.dma(lambda q, wsrc=wsrc, rt=rt: q.dma_start(
                    out=wr_f[:, :], in_=wsrc[rt * 128:(rt + 1) * 128, :]),
                    waits=[("dv", P.cnt["dv"])])
                P.v(lambda v: v.reduce_max(out=wstat[:, 5:6], in_=wr_f[:, :], axis=AX.X,
                                           apply_absolute_value=True), waits=["ld"])
                P.v(lambda v: v.reciprocal(out=wstat[:, 6:7], in_=wstat[:, 5:6]))
                P.v(lambda v: v.tensor_scalar_mul(out=wstat[:, 6:7], in0=wstat[:, 6:7], scalar1=127.0))
                P.v(lambda v, scl=scl: v.tensor_scalar(out=scl[:, :], in0=wstat[:, 5:6],
                                                       scalar1=1.0 / 127.0, scalar2=scol[:, 0:1],
                                                       op0=ALU.mult, op1=ALU.mult))
                P.v(lambda v: v.tensor_scalar(out=xqs[:, 0:D], in0=wr_f[:, :], scalar1=wstat[:, 6:7],
                                              scalar2=MAGIC, op0=ALU.mult, op1=ALU.add))
                P.v(lambda v: v.tensor_scalar(out=xqs[:, 0:D], in0=xqs[:, 0:D], scalar1=MAGIC,
                                              scalar2=127.0, op0=ALU.subtract, op1=ALU.min))
                P.v(lambda v, wdst=wdst: v.tensor_scalar(out=wdst[:, :], in0=xqs[:, 0:D],
                                                         scalar1=-128.0, scalar2=0.0,
                                                         op0=ALU.max, op1=ALU.add))
                for g8 in range(1):
                    P.wait("tensor", "dv", P.cnt["dv"])
                    for u in range(8):
                        P.t(lambda p, u=u, wdst=wdst: p.transpose(out=p_t[:, u * 128:(u + 1) * 128],
                                                                  in_=wdst[:, u * 128:(u + 1) * 128],
                                                                  identity=identb[:, :]), sem="pe")
                    P.v(lambda v, slab=slab: v.tensor_copy(out=slab[:, :], in_=p_t[:, :]), waits=["pe"])
            for tb in range(8):
                bp = tb % 2
                pg = p_g0 if bp == 0 else p_g1
                pu = p_u0 if bp == 0 else p_u1
                hp = hp0 if bp == 0 else hp1
                P.wait("tensor", "dv", P.cnt["dv"])
                P.wait("tensor", "ac", ev_par[bp])
                for k in range(8):
                    P.t(_mm(pg[:, :], wsl_g[:, k * 128:(k + 1) * 128],
                            xq[:, k * T + tb * 512:k * T + (tb + 1) * 512], k == 0, k == 7), sem="pe")
                for k in range(8):
                    P.t(_mm(pu[:, :], wsl_u[:, k * 128:(k + 1) * 128],
                            xq[:, k * T + tb * 512:k * T + (tb + 1) * 512], k == 0, k == 7), sem="pe")
                ev_par[bp] = P.s(lambda s, pg=pg: s.activation(out=gs[:, :], in_=pg[:, :], func=AF.Silu,
                                                               scale=scg[:, 0:1]),
                                 waits=["pe", ("dv", hp_tok)])
                P.v(lambda v, pu=pu: v.tensor_scalar_mul(out=us[:, :], in0=pu[:, :], scalar1=scu[:, 0:1]),
                    waits=["pe"])
                hp_tok = P.v(lambda v, hp=hp: v.tensor_tensor(out=hp[:, :], in0=gs[:, :], in1=us[:, :],
                                                               op=ALU.mult),
                             waits=["ac", ("so", st_par[bp])])
                P.v(lambda v, hp=hp: v.reduce_max(out=hmc[:, 1:2], in_=hp[:, :], axis=AX.X,
                                                  apply_absolute_value=True))
                P.v(lambda v: v.tensor_tensor(out=hmc[:, 0:1], in0=hmc[:, 0:1], in1=hmc[:, 1:2], op=ALU.max))
                st_par[bp] = P.dma(lambda q, hp=hp, rt=rt, tb=tb: q.dma_start(
                    out=hpre_h[rt * 128:(rt + 1) * 128, tb * 512:(tb + 1) * 512], in_=hp[:, :]),
                    waits=["dv"], q="gpsimd", sem="so")
        P.g(lambda g: g.reduce_max(hm1[0:1, :], hmc[:, 0:1], axis=AX.C), waits=["dv"])
        P.dma(lambda q: q.dma_start(out=hmax_h, in_=hm1[0:1, 0:1]), waits=["po"], q="gpsimd", sem="so")
        P.op("sync", lambda q: q.wait_ge(P.sem["so"], P.cnt["so"]))
        P.op("sync", lambda q: q.wait_ge(P.sem["ld"], P.cnt["ld"]))
        P.finalize(blk)
    return nc




def build_launch3(debug=False, cut=None):
    nc = bass.Bass("TRN2", target_bir_lowering=False, detect_race_conditions=False)
    x0_h = nc.dram_tensor("x0", [T, D], F32, kind="ExternalInput").ap()
    s0_h = nc.dram_tensor("s0_fm", [D, T], BF16, kind="ExternalInput").ap()
    hpre_h = nc.dram_tensor("hpre_fm", [FFN_I, T], BF16, kind="ExternalInput").ap()
    hmax8_h = nc.dram_tensor("hmax8", [1, 8], F32, kind="ExternalInput").ap()
    clipd_h = nc.dram_tensor("clip_down", [1, 1], F32, kind="ExternalInput").ap()
    wdown_h = nc.dram_tensor("w_down", [D, FFN_I], F32, kind="ExternalInput").ap()
    lng_h = nc.dram_tensor("ln_gamma", [1, D], F32, kind="ExternalInput").ap()
    lnb_h = nc.dram_tensor("ln_beta", [1, D], F32, kind="ExternalInput").ap()
    inw_h = {d: nc.dram_tensor(f"in_w_{d}", [2 * L1_DI, D], F32, kind="ExternalInput").ap() for d in "fb"}
    cw_h = {d: nc.dram_tensor(f"conv_w_{d}", [L1_DI, 4], F32, kind="ExternalInput").ap() for d in "fb"}
    cb_h = {d: nc.dram_tensor(f"conv_b_{d}", [L1_DI], F32, kind="ExternalInput").ap() for d in "fb"}
    xpw_h = {d: nc.dram_tensor(f"xp_w_{d}", [96, L1_DI], F32, kind="ExternalInput").ap() for d in "fb"}
    dtpw_h = {d: nc.dram_tensor(f"dtp_w_{d}", [L1_DI, L1_DTR], F32, kind="ExternalInput").ap() for d in "fb"}
    dtpb_h = {d: nc.dram_tensor(f"dtp_b_{d}", [L1_DI], F32, kind="ExternalInput").ap() for d in "fb"}
    alog_h = {d: nc.dram_tensor(f"a_log_{d}", [L1_DI, L1_NST], F32, kind="ExternalInput").ap() for d in "fb"}
    dsk_h = {d: nc.dram_tensor(f"d_{d}", [L1_DI], F32, kind="ExternalInput").ap() for d in "fb"}
    outw_h = {d: nc.dram_tensor(f"out_w_{d}", [D, L1_DI], F32, kind="ExternalInput").ap() for d in "fb"}
    projw_h = nc.dram_tensor("proj_w", [D, 2 * D], F32, kind="ExternalInput").ap()
    projb_h = nc.dram_tensor("proj_b", [1, D], F32, kind="ExternalInput").ap()
    fw_h = nc.dram_tensor("ffn_w", [D, D], F32, kind="ExternalInput").ap()
    fb_h = nc.dram_tensor("ffn_b", [1, D], F32, kind="ExternalInput").ap()
    dw_h = nc.dram_tensor("delta_w", [D, 32], F32, kind="ExternalInput").ap()
    da_h = nc.dram_tensor("delta_act", [1, 1], F32, kind="ExternalInput").ap()
    x2_h = nc.dram_tensor("x2", [T, D], F32, kind="ExternalOutput").ap()

    x1_sp = nc.dram_tensor("x1_sp", [T, D], F32, kind="Internal").ap()
    xn1t_sp = nc.dram_tensor("xn1t_sp", [T, D], BF16, kind="Internal").ap()
    xn1f_sp = nc.dram_tensor("xn1f_sp", [D, T], BF16, kind="Internal").ap()
    z_sp = {d: nc.dram_tensor(f"zsp_{d}", [L1_DI, T], BF16, kind="Internal").ap() for d in "fb"}
    xcp_sp = {d: nc.dram_tensor(f"xcpsp_{d}", [L1_DI, T], BF16, kind="Internal").ap() for d in "fb"}
    xc_sp = {d: nc.dram_tensor(f"xcsp_{d}", [L1_DI, T], BF16, kind="Internal").ap() for d in "fb"}
    bc_sp = {d: nc.dram_tensor(f"bcsp_{d}", [32, T], BF16, kind="Internal").ap() for d in "fb"}
    yd_sp = {d: nc.dram_tensor(f"ydsp_{d}", [L1_DI, T], BF16, kind="Internal").ap() for d in "fb"}
    yp_sp = {d: nc.dram_tensor(f"ypsp_{d}", [D, T], BF16, kind="Internal").ap() for d in "fb"}
    yat_sp = nc.dram_tensor("yat_sp", [D, T], BF16, kind="Internal").ap()
    dbg = {}
    if debug:
        for nm, shp, dt in (("x1", [T, D], F32), ("xln", [D, T], BF16), ("xc_f", [L1_DI, T], BF16),
                            ("dtb_f", [96, T], F32), ("y_f", [L1_DI, T], BF16), ("yp_f", [D, T], BF16),
                            ("yat", [D, T], BF16), ("g1", [D, T], BF16), ("xc_b", [L1_DI, T], BF16),
                            ("y_b", [L1_DI, T], BF16)):
            dbg[nm] = nc.dram_tensor("dbg_" + nm, shp, dt, kind="ExternalOutput").ap()

    with ExitStack() as st:
        e = st.enter_context
        P = Prog(nc, st, cut=cut)
        P.setup_consts(st, [1e-5, 1.0])
        identb = e(nc.sbuf_tensor("identb", [128, 128], BF16))
        identf = e(nc.sbuf_tensor("identf", [128, 128], F32))
        ones1 = e(nc.sbuf_tensor("ones1", [1, 128], F32))
        ones1b = e(nc.sbuf_tensor("ones1b", [1, 128], BF16))
        sc = e(nc.sbuf_tensor("sc", [1, 8], F32))
        st8 = e(nc.sbuf_tensor("st8", [128, 8], F32))
        invh = e(nc.sbuf_tensor("invh", [128, 1], F32))
        sh = e(nc.sbuf_tensor("sh", [128, 1], F32))
        pbcols = e(nc.sbuf_tensor("pbcols", [128, 8], F32))
        fbcols = e(nc.sbuf_tensor("fbcols", [128, 8], F32))
        dacol = e(nc.sbuf_tensor("dacol", [128, 1], F32))
        blk = e(nc.Block())
        xst = ExitStack()
        gbig = xst.enter_context(nc.sbuf_tensor("gbig", [128, D], F32))
        bbig = xst.enter_context(nc.sbuf_tensor("bbig", [128, D], F32))

        P.g(lambda g: g.memset(identb[:, :], 0.0))
        P.g(lambda g: g.affine_select(out=identb[:, :], in_=identb[:, :], compare_op=ALU.not_equal,
                                      fill=1.0, base=0, pattern=[[-1, 128]], channel_multiplier=1))
        P.g(lambda g: g.memset(identf[:, :], 0.0))
        P.g(lambda g: g.affine_select(out=identf[:, :], in_=identf[:, :], compare_op=ALU.not_equal,
                                      fill=1.0, base=0, pattern=[[-1, 128]], channel_multiplier=1))
        P.g(lambda g: g.memset(ones1[:, :], 1.0))
        P.g(lambda g: g.memset(ones1b[:, :], 1.0))
        P.dma(lambda q: q.dma_start(out=sc[0:1, 0:8], in_=hmax8_h))
        P.dma(lambda q: q.dma_start(out=st8[0:1, 0:1], in_=clipd_h))
        P.dma(lambda q: q.dma_start(out=st8[0:1, 1:2], in_=da_h))
        P.v(lambda v: v.reduce_max(out=st8[0:1, 2:3], in_=sc[0:1, 0:8], axis=AX.X), waits=["ld"])
        P.s(lambda s: s.activation(out=st8[0:1, 3:4], in_=st8[0:1, 0:1], func=AF.Sigmoid), waits=["ld", "dv"])
        P.s(lambda s: s.activation(out=st8[0:1, 4:5], in_=st8[0:1, 1:2], func=AF.Tanh, scale=4.0))
        P.v(lambda v: v.tensor_scalar_mul(out=st8[0:1, 4:5], in0=st8[0:1, 4:5], scalar1=0.5), waits=["ac"])
        P.v(lambda v: v.tensor_tensor(out=st8[0:1, 5:6], in0=st8[0:1, 2:3], in1=st8[0:1, 3:4], op=ALU.mult))
        P.v(lambda v: v.tensor_scalar_mul(out=st8[0:1, 5:6], in0=st8[0:1, 5:6], scalar1=1.0 / 127.0))
        P.v(lambda v: v.reciprocal(out=st8[0:1, 6:7], in_=st8[0:1, 5:6]))
        with ExitStack() as s0s:
            p_b = s0s.enter_context(nc.psum_tensor("p_bc3", [128, 16], F32))
            P.t(_mm(p_b[:, 0:3], ones1[0:1, :], st8[0:1, 4:7], True, True), waits=["dv", "po"], sem="pe")
            P.v(lambda v: v.tensor_copy(out=dacol[:, :], in_=p_b[:, 0:1]), waits=["pe"])
            P.v(lambda v: v.tensor_copy(out=sh[:, :], in_=p_b[:, 1:2]))
            P.v(lambda v: v.tensor_copy(out=invh[:, :], in_=p_b[:, 2:3]))
            gb = s0s.enter_context(nc.sbuf_tensor("gbrow", [1, D], F32))
            bb = s0s.enter_context(nc.sbuf_tensor("bbrow", [1, D], F32))
            P.dma(lambda q: q.dma_start(out=gb[0:1, :], in_=lng_h))
            P.dma(lambda q: q.dma_start(out=bb[0:1, :], in_=lnb_h))
            with nc.psum_tensor("p_gb", [128, 512], F32) as p_gb:
                for j in range(2):
                    P.t(_mm(p_gb[:, :], ones1[0:1, :], gb[0:1, j * 512:(j + 1) * 512], True, True),
                        waits=["ld", ("dv", P.cnt["dv"])], sem="pe")
                    P.v(lambda v, j=j: v.tensor_copy(out=gbig[:, j * 512:(j + 1) * 512], in_=p_gb[:, :]),
                        waits=["pe"])
                    P.t(_mm(p_gb[:, :], ones1[0:1, :], bb[0:1, j * 512:(j + 1) * 512], True, True),
                        waits=[("dv", P.cnt["dv"])], sem="pe")
                    P.v(lambda v, j=j: v.tensor_copy(out=bbig[:, j * 512:(j + 1) * 512], in_=p_gb[:, :]),
                        waits=["pe"])
                P.dma(lambda q: q.dma_start(out=pbcols[:, :], in_=projb_h.rearrange("a (o p) -> p (a o)", p=128),
                                            allow_slow_non_contiguous=True))
                P.dma(lambda q: q.dma_start(out=fbcols[:, :], in_=fb_h.rearrange("a (o p) -> p (a o)", p=128),
                                            allow_slow_non_contiguous=True))

        xln_fm = xst.enter_context(nc.sbuf_tensor("xln_fm", [128, 8 * T], BF16))
        with ExitStack() as s1s:
            e1 = s1s.enter_context
            wd_f0 = e1(nc.sbuf_tensor("wd_f0", [128, FFN_I], F32))
            wd_f1 = wd_f0
            wdq = e1(nc.sbuf_tensor("wdq", [128, FFN_I], BF16))
            wdall = e1(nc.sbuf_tensor("wdall", [128, 8 * FFN_I], BF16))
            scd = e1(nc.sbuf_tensor("scd", [128, 8], F32))
            wst = e1(nc.sbuf_tensor("wst", [128, 4], F32))
            hq = e1(nc.sbuf_tensor("hq", [128, 22 * 512], BF16))
            hqf = e1(nc.sbuf_tensor("hqf", [128, 2 * 512], F32))
            g0tb = e1(nc.sbuf_tensor("g0tb", [128, 8 * 512], BF16))
            s0tb = e1(nc.sbuf_tensor("s0tb", [128, 8 * 512], BF16))
            inctm = e1(nc.sbuf_tensor("inctm", [128, 4 * D], BF16))
            x0t = e1(nc.sbuf_tensor("x0t", [128, 4 * D], F32))
            x1t = x0t
            lst = e1(nc.sbuf_tensor("lst", [128, 8], F32))
            xn1t = e1(nc.sbuf_tensor("xn1t", [128, D], BF16))
            xrw = hqf
            sqs3 = g0tb.ap().bitcast(F32)[:, 0:D]
            xlnt = e1(nc.sbuf_tensor("xlnt", [128, D], BF16))
            p_m0 = e1(nc.psum_tensor("p3_m0", [128, 512], F32))
            p_m1 = e1(nc.psum_tensor("p3_m1", [128, 512], F32))
            p_t3 = e1(nc.psum_tensor("p3_t", [128, 1024], BF16))

            wdrd = [0, 0]
            for rt in range(8):
                wd_f = wd_f0 if rt % 2 == 0 else wd_f1
                P.dma(lambda q, rt=rt, wd_f=wd_f: q.dma_start(
                    out=wd_f[:, :], in_=wdown_h[rt * 128:(rt + 1) * 128, :]),
                    waits=[("dv", wdrd[(rt + 1) % 2])])
                P.v(lambda v, wd_f=wd_f: v.reduce_max(out=wst[:, 0:1], in_=wd_f[:, :], axis=AX.X,
                                                      apply_absolute_value=True), waits=["ld"])
                P.v(lambda v: v.reciprocal(out=wst[:, 1:2], in_=wst[:, 0:1]))
                P.v(lambda v: v.tensor_scalar_mul(out=wst[:, 1:2], in0=wst[:, 1:2], scalar1=127.0))
                P.v(lambda v, rt=rt: v.tensor_scalar(out=scd[:, rt:rt + 1], in0=wst[:, 0:1],
                                                     scalar1=1.0 / 127.0, scalar2=sh[:, 0:1],
                                                     op0=ALU.mult, op1=ALU.mult))
                P.v(lambda v, wd_f=wd_f: v.tensor_scalar(out=wd_f[:, :], in0=wd_f[:, :], scalar1=wst[:, 1:2],
                                              scalar2=MAGIC, op0=ALU.mult, op1=ALU.add))
                P.v(lambda v, wd_f=wd_f: v.tensor_scalar(out=wd_f[:, :], in0=wd_f[:, :], scalar1=MAGIC, scalar2=127.0,
                                              op0=ALU.subtract, op1=ALU.min))
                wdrd[rt % 2] = P.v(lambda v, wd_f=wd_f: v.tensor_scalar(out=wdq[:, :], in0=wd_f[:, :], scalar1=-128.0, scalar2=0.0,
                                              op0=ALU.max, op1=ALU.add),
                                   waits=[("pe", P.cnt["pe"])])
                for g8 in range(3):
                    nu = 8 if g8 < 2 else 6
                    P.wait("tensor", "dv", P.cnt["dv"])
                    for u in range(nu):
                        kk = g8 * 8 + u
                        P.t(lambda p, u=u, kk=kk: p.transpose(out=p_t3[:, u * 128:(u + 1) * 128],
                                                              in_=wdq[:, kk * 128:(kk + 1) * 128],
                                                              identity=identb[:, :]), sem="pe")
                    P.v(lambda v, rt=rt, g8=g8, nu=nu: v.tensor_copy(
                        out=wdall[:, rt * FFN_I + g8 * 1024:rt * FFN_I + g8 * 1024 + nu * 128],
                        in_=p_t3[:, 0:nu * 128]), waits=["pe"])

            for tb in range(8):
                pe_war = P.cnt["pe"]
                for kc in range(22):
                    P.dma(lambda q, kc=kc, tb=tb: q.dma_start(
                        out=hq[:, kc * 512:(kc + 1) * 512],
                        in_=hpre_h[kc * 128:(kc + 1) * 128, tb * 512:(tb + 1) * 512]),
                        waits=[("pe", pe_war)])
                    if kc % 2 == 1 or kc == 21:
                        g0 = (kc // 2) * 2
                        w = (kc - g0 + 1) * 512
                        P.v(lambda v, g0=g0, w=w: v.tensor_scalar(
                            out=hqf[:, 0:w], in0=hq[:, g0 * 512:g0 * 512 + w],
                            scalar1=invh[:, 0:1], scalar2=MAGIC,
                            op0=ALU.mult, op1=ALU.add), waits=["ld"])
                        P.v(lambda v, w=w: v.tensor_scalar(out=hqf[:, 0:w], in0=hqf[:, 0:w],
                                                           scalar1=MAGIC, scalar2=127.0,
                                                           op0=ALU.subtract, op1=ALU.min))
                        P.v(lambda v, g0=g0, w=w: v.tensor_scalar(
                            out=hq[:, g0 * 512:g0 * 512 + w], in0=hqf[:, 0:w],
                            scalar1=-128.0, scalar2=0.0, op0=ALU.max, op1=ALU.add))
                for ot in range(8):
                    pm = p_m0 if ot % 2 == 0 else p_m1
                    P.wait("tensor", "dv", P.cnt["dv"])
                    P.wait("tensor", "ac", P.cnt["ac"])
                    for k in range(22):
                        P.t(_mm(pm[:, :], wdall[:, ot * FFN_I + k * 128:ot * FFN_I + (k + 1) * 128],
                                hq[:, k * 512:(k + 1) * 512], k == 0, k == 21), sem="pe")
                    P.s(lambda s, pm=pm, ot=ot: s.activation(out=g0tb[:, ot * 512:(ot + 1) * 512],
                                                             in_=pm[:, :], func=AF.Silu,
                                                             scale=scd[:, ot:ot + 1]),
                        waits=["pe", ("dv", P.cnt["dv"])])
                P.dma(lambda q, tb=tb: q.dma_start(
                    out=s0tb[:, :].rearrange("p (k t) -> p k t", k=8),
                    in_=s0_h[:, tb * 512:(tb + 1) * 512].rearrange("(k p) t -> p k t", p=128)),
                    waits=[("pe", P.cnt["pe"])])
                P.v(lambda v: v.tensor_tensor(out=s0tb[:, :], in0=s0tb[:, :], in1=g0tb[:, :], op=ALU.mult),
                    waits=["ld", "ac"])
                for g8 in range(4):
                    P.wait("tensor", "dv", P.cnt["dv"])
                    for u in range(8):
                        kk = g8 + u * 4
                        pass
                    for k in range(8):
                        P.t(lambda p, k=k, g8=g8: p.transpose(
                            out=p_t3[:, k * 128:(k + 1) * 128],
                            in_=s0tb[:, k * 512 + g8 * 128:k * 512 + (g8 + 1) * 128],
                            identity=identb[:, :]), sem="pe")
                    P.v(lambda v, g8=g8: v.tensor_copy(out=inctm[:, g8 * D:(g8 + 1) * D], in_=p_t3[:, :]),
                        waits=["pe"])
                P.dma(lambda q, tb=tb: q.dma_start(out=x0t[:, :].rearrange("p (s d) -> p s d", s=4),
                                                   in_=x0_h[tb * 512:(tb + 1) * 512, :]
                                                   .rearrange("(s p) d -> p s d", p=128)),
                      waits=[("ac", P.cnt["ac"]), ("so", P.cnt["so"])])
                P.v(lambda v: v.tensor_tensor(out=x1t[:, :], in0=x0t[:, :], in1=inctm[:, :], op=ALU.add),
                    waits=["ld", "dv"])
                P.dma(lambda q, tb=tb: q.dma_start(out=x1_sp[tb * 512:(tb + 1) * 512, :]
                                                   .rearrange("(s p) d -> p s d", p=128),
                                                   in_=x1t[:, :].rearrange("p (s d) -> p s d", s=4)),
                      waits=["dv"], q="gpsimd", sem="so")
                for sb in range(4):
                    xs_ = x1t.ap()[:, sb * D:(sb + 1) * D]
                    P.v(lambda v, xs_=xs_: v.reduce_sum(out=lst[:, 0:1], in_=xs_, axis=AX.X))
                    P.s(lambda s, xs_=xs_: s.activation(out=sqs3[:, :], in_=xs_, func=AF.Square,
                                                        accum_out=lst[:, 1:2]),
                        waits=[("dv", P.cnt["dv"])])
                    P.v(lambda v: v.tensor_scalar_mul(out=lst[:, 2:3], in0=lst[:, 0:1], scalar1=1.0 / D))
                    P.v(lambda v: v.tensor_tensor(out=lst[:, 3:4], in0=lst[:, 2:3], in1=lst[:, 2:3],
                                                  op=ALU.mult))
                    P.v(lambda v: v.tensor_scalar_mul(out=lst[:, 4:5], in0=lst[:, 1:2], scalar1=1.0 / D),
                        waits=["ac"])
                    P.v(lambda v: v.tensor_tensor(out=lst[:, 5:6], in0=lst[:, 4:5], in1=lst[:, 3:4],
                                                  op=ALU.subtract))
                    P.s(lambda s: s.activation(out=lst[:, 6:7], in_=lst[:, 5:6], func=AF.Sqrt, bias=1e-5),
                        waits=["dv"])
                    P.v(lambda v: v.reciprocal(out=lst[:, 7:8], in_=lst[:, 6:7]), waits=["ac"])
                    P.v(lambda v, xs_=xs_: v.tensor_scalar(out=xn1t[:, :], in0=xs_, scalar1=lst[:, 2:3],
                                                           scalar2=lst[:, 7:8],
                                                           op0=ALU.subtract, op1=ALU.mult))
                    P.dma(lambda q, tb=tb, sb=sb: q.dma_start(
                        out=xn1t_sp[tb * 512 + sb * 128:tb * 512 + (sb + 1) * 128, :], in_=xn1t[:, :]),
                        waits=["dv"], q="gpsimd", sem="so")
                    P.v(lambda v: v.reduce_sum(out=lst[:, 0:1], in_=xn1t[:, :], axis=AX.X))
                    P.s(lambda s: s.activation(out=sqs3[:, :], in_=xn1t[:, :], func=AF.Square,
                                               accum_out=lst[:, 1:2]), waits=[("dv", P.cnt["dv"])])
                    P.v(lambda v: v.tensor_scalar_mul(out=lst[:, 2:3], in0=lst[:, 0:1], scalar1=1.0 / D))
                    P.v(lambda v: v.tensor_tensor(out=lst[:, 3:4], in0=lst[:, 2:3], in1=lst[:, 2:3],
                                                  op=ALU.mult))
                    P.v(lambda v: v.tensor_scalar_mul(out=lst[:, 4:5], in0=lst[:, 1:2], scalar1=1.0 / D),
                        waits=["ac"])
                    P.v(lambda v: v.tensor_tensor(out=lst[:, 5:6], in0=lst[:, 4:5], in1=lst[:, 3:4],
                                                  op=ALU.subtract))
                    P.s(lambda s: s.activation(out=lst[:, 6:7], in_=lst[:, 5:6], func=AF.Sqrt, bias=1e-5),
                        waits=["dv"])
                    P.v(lambda v: v.reciprocal(out=lst[:, 7:8], in_=lst[:, 6:7]), waits=["ac"])
                    P.v(lambda v: v.tensor_scalar(out=xrw[:, :], in0=xn1t[:, :], scalar1=lst[:, 2:3],
                                                  scalar2=lst[:, 7:8], op0=ALU.subtract, op1=ALU.mult))
                    P.v(lambda v: v.tensor_tensor(out=xrw[:, :], in0=xrw[:, :], in1=gbig[:, :], op=ALU.mult))
                    P.v(lambda v: v.tensor_tensor(out=xlnt[:, :], in0=xrw[:, :], in1=bbig[:, :], op=ALU.add))
                    for which, tile in (("xn1", xn1t), ("xln", xlnt)):
                        P.wait("tensor", "dv", P.cnt["dv"])
                        for u in range(8):
                            P.t(lambda p, u=u, tile=tile: p.transpose(out=p_t3[:, u * 128:(u + 1) * 128],
                                                                      in_=tile[:, u * 128:(u + 1) * 128],
                                                                      identity=identb[:, :]), sem="pe")
                        if which == "xn1":
                            P.v(lambda v: v.tensor_copy(out=inctm[:, 0:1024], in_=p_t3[:, :]),
                                waits=["pe", ("so", P.cnt["so"])])
                            P.dma(lambda q, tb=tb, sb=sb: q.dma_start(
                                out=xn1f_sp[:, tb * 512 + sb * 128:tb * 512 + (sb + 1) * 128]
                                .rearrange("(k p) t -> p k t", p=128),
                                in_=inctm[:, 0:1024].rearrange("p (k t) -> p k t", k=8)),
                                waits=["dv"], q="gpsimd", sem="so")
                        else:
                            tt_ = tb * 512 + sb * 128
                            xlv = xln_fm.ap().rearrange("p (k t) -> p k t", k=8)
                            P.v(lambda v, xlv=xlv, tt_=tt_: v.tensor_copy(
                                out=xlv[:, :, tt_:tt_ + 128],
                                in_=p_t3[:, :].rearrange("p (k t) -> p k t", k=8)), waits=["pe"])
        if debug:
            P.dma(lambda q: q.dma_start(out=dbg["x1"], in_=x1_sp), waits=[("so", P.cnt["so"])],
                  q="gpsimd", sem="so")
            P.dma(lambda q: q.dma_start(out=dbg["xln"].rearrange("(k p) t -> p k t", p=128),
                                        in_=xln_fm[:, :].rearrange("p (k t) -> p k t", k=8)),
                  waits=[("dv", P.cnt["dv"])], q="gpsimd", sem="so")
        P.barrier()
        nc_dirs = _emit_l1_dirs(nc, st, e, P, dict(
            inw_h=inw_h, cw_h=cw_h, cb_h=cb_h, xpw_h=xpw_h, dtpw_h=dtpw_h, dtpb_h=dtpb_h,
            alog_h=alog_h, dsk_h=dsk_h, outw_h=outw_h, z_sp=z_sp, xcp_sp=xcp_sp, xc_sp=xc_sp,
            bc_sp=bc_sp, yd_sp=yd_sp, yp_sp=yp_sp, xln_fm=xln_fm, xln_stack=xst,
            identb=identb, identf=identf, ones1=ones1, ones1b=ones1b, dbg=dbg, debug=debug))
        P.barrier()
        _emit_l3_tail(nc, st, e, P, dict(
            projw_h=projw_h, fw_h=fw_h, dw_h=dw_h, yp_sp=yp_sp, yat_sp=yat_sp, xn1t_sp=xn1t_sp,
            xn1f_sp=xn1f_sp, x1_sp=x1_sp, x2_h=x2_h, pbcols=pbcols, fbcols=fbcols, dacol=dacol,
            identb=identb, identf=identf, ones1=ones1, dbg=dbg, debug=debug))
        P.op("sync", lambda q: q.wait_ge(P.sem["so"], P.cnt["so"]))
        P.op("sync", lambda q: q.wait_ge(P.sem["ld"], P.cnt["ld"]))
        P.finalize(blk)
    return nc



def _emit_l1_dirs(nc, st, e, P, env):
    g = type("E", (), env)
    xln_fm = env["xln_fm"]
    identb, identf, ones1 = env["identb"], env["identf"], env["ones1"]
    ones1b = env["ones1b"]
    debug, dbg = env["debug"], env["dbg"]

    with ExitStack() as s2:
        e2 = s2.enter_context
        wsl_f = e2(nc.sbuf_tensor("l1w_f", [128, D], F32))
        wslq = e2(nc.sbuf_tensor("l1w_q", [128, D], BF16))
        slab = e2(nc.sbuf_tensor("l1slab", [128, 8 * 128], BF16))
        ev0 = e2(nc.sbuf_tensor("l1ev0", [128, 512], BF16))
        ev1 = e2(nc.sbuf_tensor("l1ev1", [128, 512], BF16))
        p_a = e2(nc.psum_tensor("l1p_a", [128, 512], F32))
        p_b = e2(nc.psum_tensor("l1p_b", [128, 512], F32))
        p_t = e2(nc.psum_tensor("l1p_t", [128, 1024], BF16))
        for d in "fb":
            ev_tok = [("dv", 0), ("dv", 0)]
            st_tok = [0, 0]
            for rt in range(32):
                P.dma(lambda q, d=d, rt=rt: q.dma_start(out=wslq[:, :],
                                                        in_=env["inw_h"][d][rt * 128:(rt + 1) * 128, :]),
                      waits=[("pe", P.cnt["pe"])], q="gpsimd", sem="so")
                P.wait("tensor", "so", P.cnt["so"])
                P.wait("tensor", "dv", P.cnt["dv"])
                for u in range(8):
                    P.t(lambda p, u=u: p.transpose(out=p_t[:, u * 128:(u + 1) * 128],
                                                   in_=wslq[:, u * 128:(u + 1) * 128],
                                                   identity=identb[:, :]), sem="pe")
                P.v(lambda v: v.tensor_copy(out=slab[:, :], in_=p_t[:, :]), waits=["pe"])
                P.wait("tensor", "dv", P.cnt["dv"])
                for tb in range(8):
                    bp = tb % 2
                    pm = p_a if bp == 0 else p_b
                    ev = ev0 if bp == 0 else ev1
                    P.wait("tensor", ev_tok[bp][0], ev_tok[bp][1])
                    for k in range(8):
                        P.t(_mm(pm[:, :], slab[:, k * 128:(k + 1) * 128],
                                xln_fm[:, k * T + tb * 512:k * T + (tb + 1) * 512], k == 0, k == 7),
                            sem="pe")
                    if rt < 16:
                        ev_tok[bp] = ("dv", P.v(lambda v, ev=ev, pm=pm: v.tensor_copy(out=ev[:, :], in_=pm[:, :]),
                                                waits=["pe", ("so", st_tok[bp])]))
                        wsem = "dv"
                    else:
                        P.wait("scalar", "so", st_tok[bp])
                        ev_tok[bp] = ("ac", P.s(lambda s, ev=ev, pm=pm: s.activation(out=ev[:, :], in_=pm[:, :],
                                                                                     func=AF.Silu), waits=["pe"]))
                        wsem = "ac"
                    dst = env["xcp_sp"][d] if rt < 16 else env["z_sp"][d]
                    row = rt * 128 if rt < 16 else (rt - 16) * 128
                    st_tok[bp] = P.dma(lambda q, ev=ev, dst=dst, row=row, tb=tb: q.dma_start(
                        out=dst[row:row + 128, tb * 512:(tb + 1) * 512], in_=ev[:, :]),
                        waits=[wsem], q="gpsimd", sem="so")
    env["xln_stack"].close()
    P.barrier()

    def _one_dir(d):
        rev = d == "b"
        with ExitStack() as s3:
            e3 = s3.enter_context
            convw1 = e3(nc.sbuf_tensor("cw1" + d, [128, 16 * 4], F32))
            convb1 = e3(nc.sbuf_tensor("cb1" + d, [128, 16], F32))
            dtb = e3(nc.sbuf_tensor("dtb" + d, [64, T], BF16))
            dtpT = e3(nc.sbuf_tensor("dtpT" + d, [64, 16 * 128], BF16))
            dtp_q = e3(nc.sbuf_tensor("dtp_q" + d, [128, 64], BF16))
            ndtpb = e3(nc.sbuf_tensor("ndtpb" + d, [128, 16], F32))
            aneg = e3(nc.sbuf_tensor("aneg" + d, [128, 256], F32))
            dcol = e3(nc.sbuf_tensor("dcol" + d, [128, 16], F32))
            p_c = e3(nc.psum_tensor("p_c" + d, [128, 512], F32))
            p_d = e3(nc.psum_tensor("p_d" + d, [128, 512], F32))
            p_t2 = e3(nc.psum_tensor("p_t2" + d, [128, 1024], BF16))
            s3a = ExitStack()
            e3a = s3a.enter_context
            convx = e3a(nc.sbuf_tensor("cvx" + d, [128, T + 3], BF16))
            convacc = e3a(nc.sbuf_tensor("cva" + d, [128, T], F32))
            convo = e3a(nc.sbuf_tensor("cvo" + d, [128, T], BF16))
            xpT = e3a(nc.sbuf_tensor("xpT" + d, [128, 16 * 96], BF16))
            xp_f = e3a(nc.sbuf_tensor("xp_f" + d, [96, L1_DI], F32))
            xp_q = e3a(nc.sbuf_tensor("xp_q" + d, [96, L1_DI], BF16))
            xdbl = e3a(nc.sbuf_tensor("xdbl" + d, [96, T], F32))

            P.dma(lambda q, d=d: q.dma_start(out=convw1[:, :].rearrange("p (s k) -> p s k", s=16),
                                             in_=env["cw_h"][d].rearrange("(s p) k -> p s k", p=128)))
            P.dma(lambda q, d=d: q.dma_start(out=convb1[:, :], in_=env["cb_h"][d]
                                             .rearrange("(s p) -> p s", p=128),
                                             allow_slow_non_contiguous=True))
            P.dma(lambda q, d=d: q.dma_start(out=ndtpb[:, :], in_=env["dtpb_h"][d]
                                             .rearrange("(s p) -> p s", p=128),
                                             allow_slow_non_contiguous=True))
            P.dma(lambda q, d=d: q.dma_start(out=dcol[:, :], in_=env["dsk_h"][d]
                                             .rearrange("(s p) -> p s", p=128),
                                             allow_slow_non_contiguous=True))
            P.dma(lambda q, d=d: q.dma_start(out=aneg[:, :].rearrange("p (s n) -> p s n", s=16),
                                             in_=env["alog_h"][d].rearrange("(s p) n -> p s n", p=128)))
            P.s(lambda s: s.activation(out=aneg[:, :], in_=aneg[:, :], func=AF.Exp), waits=["ld"])
            P.v(lambda v: v.tensor_scalar_mul(out=aneg[:, :], in0=aneg[:, :], scalar1=-1.0), waits=["ac"])
            P.v(lambda v: v.tensor_scalar_mul(out=ndtpb[:, :], in0=ndtpb[:, :], scalar1=-1.0), waits=["ld"])

            P.g(lambda g_: g_.memset(convx[:, :], 0.0))
            off = 3 if not rev else 0
            cv_tok = 0
            for ds in range(16):
                P.dma(lambda q, d=d, ds=ds, off=off: q.dma_start(
                    out=convx[:, off:off + T], in_=env["xcp_sp"][d][ds * 128:(ds + 1) * 128, :]),
                    waits=[("ac", cv_tok), ("po", P.cnt["po"])])
                for k in range(4):
                    tap = k if not rev else 3 - k
                    if k == 0:
                        P.v(lambda v, ds=ds, tap=tap, k=k: v.tensor_scalar_mul(
                            out=convacc[:, :], in0=convx[:, k:k + T],
                            scalar1=convw1[:, ds * 4 + tap:ds * 4 + tap + 1]), waits=["ld", ("ac", cv_tok)])
                    else:
                        P.v(lambda v, ds=ds, tap=tap, k=k: v.scalar_tensor_tensor(
                            out=convacc[:, :], in0=convx[:, k:k + T],
                            scalar=convw1[:, ds * 4 + tap:ds * 4 + tap + 1],
                            in1=convacc[:, :], op0=ALU.mult, op1=ALU.add))
                cv_tok = P.s(lambda s, ds=ds: s.activation(out=convo[:, :], in_=convacc[:, :],
                                                           func=AF.Silu, bias=convb1[:, ds:ds + 1]),
                             waits=["dv", ("so", P.cnt["so"])])
                P.dma(lambda q, d=d, ds=ds: q.dma_start(out=env["xc_sp"][d][ds * 128:(ds + 1) * 128, :],
                                                        in_=convo[:, :]), waits=["ac"], q="gpsimd", sem="so")
            if debug and ("xc_" + d) in dbg:
                P.dma(lambda q, d=d: q.dma_start(out=dbg["xc_" + d], in_=env["xc_sp"][d]),
                      waits=[("so", P.cnt["so"])], q="gpsimd", sem="so")

            P.dma(lambda q, d=d: q.dma_start(out=xp_q[:, :], in_=env["xpw_h"][d]),
                  waits=[("pe", P.cnt["pe"])], q="gpsimd", sem="so")
            P.wait("tensor", "so", P.cnt["so"])
            for u in range(16):
                P.wait("tensor", "ac", P.cnt["ac"])
                P.wait("tensor", "dv", P.cnt["dv"])
                P.t(lambda p, u=u: p.transpose(out=p_t2[:, 0:96], in_=xp_q[:, u * 128:(u + 1) * 128],
                                               identity=identb[0:96, 0:96]), sem="pe")
                P.v(lambda v, u=u: v.tensor_copy(out=xpT[:, u * 96:(u + 1) * 96], in_=p_t2[:, 0:96]),
                    waits=["pe"])
            for ds in range(16):
                P.dma(lambda q, d=d, ds=ds: q.dma_start(out=convo[:, :],
                                                        in_=env["xc_sp"][d][ds * 128:(ds + 1) * 128, :]),
                      waits=[("pe", P.cnt["pe"]), ("ac", P.cnt["ac"]), ("so", P.cnt["so"])])
                for tb in range(8):
                    pm = p_c if tb % 2 == 0 else p_d
                    P.wait("tensor", "ld", P.cnt["ld"])
                    P.wait("tensor", "dv", P.cnt["dv"])
                    P.t(_mm(pm[0:96, :], xpT[:, ds * 96:(ds + 1) * 96],
                            convo[:, tb * 512:(tb + 1) * 512], True, True), sem="pe")
                    if ds == 0:
                        P.v(lambda v, pm=pm, tb=tb: v.tensor_copy(out=xdbl[:, tb * 512:(tb + 1) * 512],
                                                                  in_=pm[0:96, :]), waits=["pe"])
                    else:
                        P.v(lambda v, pm=pm, tb=tb: v.tensor_tensor(out=xdbl[:, tb * 512:(tb + 1) * 512],
                                                                    in0=xdbl[:, tb * 512:(tb + 1) * 512],
                                                                    in1=pm[0:96, :], op=ALU.add),
                            waits=["pe"])
            P.dma(lambda q, d=d: q.dma_start(out=env["bc_sp"][d], in_=xdbl[64:96, :]),
                  waits=["dv"], q="gpsimd", sem="so")
            P.s(lambda s: s.activation(out=dtb[:, :], in_=xdbl[0:64, :], func=AF.Copy), waits=["dv"])
            if debug and ("dtb_" + d) in dbg:
                P.dma(lambda q: q.dma_start(out=dbg["dtb_" + d], in_=xdbl[:, :]),
                      waits=["dv"], q="gpsimd", sem="so")
            s3a.close()
            P.barrier()
            for u in range(16):
                P.dma(lambda q, d=d, u=u: q.dma_start(out=dtp_q[:, :],
                                                      in_=env["dtpw_h"][d][u * 128:(u + 1) * 128, :]),
                      waits=[("pe", P.cnt["pe"])], q="gpsimd", sem="so")
                P.wait("tensor", "so", P.cnt["so"])
                P.wait("tensor", "dv", P.cnt["dv"])
                P.t(lambda p: p.transpose(out=p_t2[0:64, 0:128], in_=dtp_q[:, :],
                                          identity=identb[:, :]), sem="pe")
                P.v(lambda v, u=u: v.tensor_copy(out=dtpT[:, u * 128:(u + 1) * 128], in_=p_t2[0:64, 0:128]),
                    waits=["pe"])

            with ExitStack() as s4:
                e4 = s4.enter_context
                Ball = e4(nc.sbuf_tensor("Ball" + d, [128, 16 * 1024], BF16))
                Call = e4(nc.sbuf_tensor("Call" + d, [128, 16 * 1024], BF16))
                brow = e4(nc.sbuf_tensor("brow" + d, [1, 512], BF16))
                brow2 = e4(nc.sbuf_tensor("brow2" + d, [1, 512], BF16))
                delta = e4(nc.sbuf_tensor("delta" + d, [128, 1024], F32))
                delta2 = e4(nc.sbuf_tensor("delta2" + d, [128, 1024], F32))
                dbufs = [delta, delta2]

                xcc0 = e4(nc.sbuf_tensor("xcc0" + d, [128, 1024], BF16))
                xcc1 = e4(nc.sbuf_tensor("xcc1" + d, [128, 1024], BF16))
                zcc = e4(nc.sbuf_tensor("zcc" + d, [128, 1024], BF16))
                Ebuf0 = e4(nc.sbuf_tensor("Ebuf0" + d, [128, 1024], F32))
                Ebuf1 = e4(nc.sbuf_tensor("Ebuf1" + d, [128, 1024], F32))
                hball = e4(nc.sbuf_tensor("hball" + d, [128, 16 * 1024], BF16))
                w_all = e4(nc.sbuf_tensor("w_all" + d, [128, 16 * 1024], BF16))
                wacc = e4(nc.sbuf_tensor("wacc" + d, [128, 16 * 1024], BF16))
                states = e4(nc.sbuf_tensor("states" + d, [128, 256], F32))
                ytq = e4(nc.sbuf_tensor("ytq" + d, [128, 1024], BF16))
                y2q = e4(nc.sbuf_tensor("y2q" + d, [128, 1024], BF16))
                y3q = e4(nc.sbuf_tensor("y3q" + d, [128, 1024], BF16))
                db = y2q

                P.v(lambda v: v.memset(states[:, :], 0.0))
                tree_tok = 0
                y3_tok = 0
                stt_tok = 0
                sp_tok = 0
                xcc_rd = [0, 0]
                row_mm = [0, 0]
                rp_i = 0
                tsmin_tok = 0
                ln_tok = 0
                btoks = [0] * 16
                nxt_b_tok = 0
                nxt_c_tok = 0
                tqs = list(range(4)) if not rev else list(range(3, -1, -1))
                for tqi, tq in enumerate(tqs):
                    t0 = tq * 1024
                    def _bcast_one(bc, n, t0b, rd_tok):
                        nonlocal row_mm, rp_i
                        dest = Ball if bc == 0 else Call
                        for half in range(2):
                            rp = rp_i % 2
                            rp_i += 1
                            row = brow if rp == 0 else brow2
                            P.dma(lambda q, d=d, n=n, t0b=t0b, bc=bc, row=row, half=half: q.dma_start(
                                out=row[0:1, :],
                                in_=env["bc_sp"][d][bc * 16 + n:bc * 16 + n + 1,
                                                    t0b + half * 512:t0b + (half + 1) * 512]),
                                waits=[("pe", row_mm[rp])])
                            pm = p_c if half == 0 else p_d
                            P.wait("tensor", "ld", P.cnt["ld"])
                            P.wait("tensor", "ac", P.cnt["ac"])
                            row_mm[rp] = P.t(_mm(pm[:, :], ones1b[0:1, :], row[0:1, :],
                                                 True, True), sem="pe")
                            P.s(lambda s, n=n, half=half, pm=pm, dest=dest: s.activation(
                                out=dest[:, n * 1024 + half * 512:n * 1024 + (half + 1) * 512],
                                in_=pm[:, :], func=AF.Copy),
                                waits=["pe"] + ([("dv", rd_tok)] if half == 0 else []))

                    if tqi == 0:
                        rd0 = P.cnt["dv"]
                        for bc in range(2):
                            for n in range(16):
                                _bcast_one(bc, n, t0, rd0 if (bc == 0 and n == 0) else 0)
                            if bc == 0:
                                bc_tok = P.cnt["ac"]
                        cc_tok = P.cnt["ac"]
                    else:
                        bc_tok, cc_tok = nxt_b_tok, nxt_c_tok

                    def _gate(j, t0=t0):
                        nonlocal stt_tok, y3_tok, sp_tok
                        xcj = xcc0 if j % 2 == 0 else xcc1
                        P.dma(lambda q, d=d, j=j, t0=t0: q.dma_start(
                            out=zcc[:, :], in_=env["z_sp"][d][j * 128:(j + 1) * 128, t0:t0 + 1024]),
                            waits=[("dv", y3_tok)])
                        stt_tok = P.v(lambda v, j=j, xcj=xcj: v.scalar_tensor_tensor(
                            out=y2q[:, :], in0=xcj[:, :], scalar=dcol[:, j:j + 1],
                            in1=ytq[:, :], op0=ALU.mult, op1=ALU.add))
                        xcc_rd[j % 2] = stt_tok
                        y3_tok = P.v(lambda v: v.tensor_tensor(out=y3q[:, :], in0=y2q[:, :],
                                                               in1=zcc[:, :], op=ALU.mult),
                                     waits=["ld", ("so", sp_tok)])
                        sp_tok = P.dma(lambda q, d=d, j=j, t0=t0: q.dma_start(
                            out=env["yd_sp"][d][j * 128:(j + 1) * 128, t0:t0 + 1024], in_=y3q[:, :]),
                            waits=["dv"], q="gpsimd", sem="so")

                    def _delta(j, inline):
                        nonlocal tsmin_tok
                        dj = dbufs[j % 2]
                        for half in range(2):
                            pm = p_c if half == 0 else p_d
                            if inline:
                                P.wait("tensor", "dv", P.cnt["dv"])
                                P.wait("tensor", "ac", P.cnt["ac"])
                            else:
                                P.wait("tensor", "dv", tsmin_tok)
                            P.t(_mm(pm[:, :], dtpT[:, j * 128:(j + 1) * 128],
                                    dtb[:, t0 + half * 512:t0 + (half + 1) * 512], True, True), sem="pe")
                            tsmin_tok = P.v(lambda v, pm=pm, j=j, half=half, dj=dj: v.tensor_scalar(
                                out=dj[:, half * 512:(half + 1) * 512], in0=pm[:, :],
                                scalar1=ndtpb[:, j:j + 1], scalar2=88.0,
                                op0=ALU.subtract, op1=ALU.min), waits=["pe"])
                        P.s(lambda s, dj=dj: s.activation(out=dj[:, :], in_=dj[:, :], func=AF.Exp),
                            waits=[("dv", tsmin_tok)])
                        return P.s(lambda s, dj=dj: s.activation(out=dj[:, :], in_=dj[:, :],
                                                                 func=AF.Ln, bias=1.0))

                    scan_tok = [0, 0]
                    exp_tok2 = [0, 0]
                    for ds in range(16):
                        xcd = xcc0 if ds % 2 == 0 else xcc1
                        delta = dbufs[ds % 2]
                        if ds == 0:
                            ln_tok = _delta(0, True)
                        P.dma(lambda q, d=d, ds=ds, t0=t0, xcd=xcd: q.dma_start(
                            out=xcd[:, :], in_=env["xc_sp"][d][ds * 128:(ds + 1) * 128, t0:t0 + 1024]),
                            waits=[("dv", xcc_rd[ds % 2])])
                        P.v(lambda v, xcd=xcd, delta=delta: v.tensor_tensor(
                            out=db[:, :], in0=delta[:, :], in1=xcd[:, :], op=ALU.mult),
                            waits=["ld", ("ac", ln_tok)])
                        for n in range(16):
                            par = n % 2
                            Eb = Ebuf0 if par == 0 else Ebuf1
                            scol_ = states.ap()[:, ds * 16 + n:ds * 16 + n + 1]
                            exp_tok2[par] = P.s(lambda s, ds=ds, n=n, Eb=Eb, delta=delta: s.activation(
                                out=Eb[:, :], in_=delta[:, :], func=AF.Exp,
                                scale=aneg[:, ds * 16 + n:ds * 16 + n + 1]),
                                waits=[("dv", scan_tok[par])])
                            btoks[n] = P.v(lambda v, n=n: v.tensor_tensor(
                                out=w_all[:, n * 1024:(n + 1) * 1024], in0=db[:, :],
                                in1=Ball[:, n * 1024:(n + 1) * 1024], op=ALU.mult),
                                waits=([("ac", bc_tok)] if ds == 0 and n == 0 else []))
                            wv = w_all.ap()[:, n * 1024:(n + 1) * 1024]
                            hv = hball.ap()[:, n * 1024:(n + 1) * 1024]
                            if not rev:
                                ein, bin_, hout = Eb[:, :], wv, hv
                            else:
                                ein = Eb.ap()[:, ::-1]
                                bin_ = wv[:, ::-1]
                                hout = hv[:, ::-1]
                            init = 0.0 if tqi == 0 else scol_
                            scan_tok[par] = P.v(
                                lambda v, ein=ein, bin_=bin_, hout=hout, init=init:
                                v.tensor_tensor_scan(out=hout, data0=ein, data1=bin_, initial=init,
                                                     op0=ALU.mult, op1=ALU.add),
                                waits=[("ac", exp_tok2[par])])
                        if ds < 15:
                            ln_tok = _delta(ds + 1, False)
                        if ds == 15 and tqi < 3:
                            t0n = tqs[tqi + 1] * 1024
                            for n in range(16):
                                _bcast_one(0, n, t0n, btoks[n])
                            nxt_b_tok = P.cnt["ac"]
                        if ds > 0:
                            _gate(ds - 1)
                        lastc = (1023 if not rev else 0)
                        P.v(lambda v, ds=ds, lastc=lastc: v.tensor_copy(
                            out=states[:, ds * 16:(ds + 1) * 16].unsqueeze(2),
                            in_=hball[:, :].rearrange("p (n t) -> p n t", n=16)[:, :, lastc:lastc + 1]))
                        wtok = P.v(lambda v: v.tensor_tensor(out=wacc[:, :], in0=hball[:, :],
                                                             in1=Call[:, :], op=ALU.mult),
                                   waits=([("ac", cc_tok)] if ds == 0 else []))
                        if ds == 15 and tqi < 3:
                            t0n = tqs[tqi + 1] * 1024
                            for n in range(16):
                                _bcast_one(1, n, t0n, wtok if n == 0 else 0)
                            nxt_c_tok = P.cnt["ac"]
                        for width in (8, 4, 2):
                            P.v(lambda v, width=width: v.tensor_tensor(
                                out=wacc[:, 0:width * 1024], in0=wacc[:, 0:width * 1024],
                                in1=wacc[:, width * 1024:2 * width * 1024], op=ALU.add))
                        tree_tok = P.v(lambda v: v.tensor_tensor(out=ytq[:, :], in0=wacc[:, 0:1024],
                                                                 in1=wacc[:, 1024:2048], op=ALU.add))
                    _gate(15)
            P.barrier()
            if debug and ("y_" + d) in dbg:
                P.dma(lambda q, d=d: q.dma_start(out=dbg["y_" + d], in_=env["yd_sp"][d]),
                      waits=[("so", P.cnt["so"])], q="gpsimd", sem="so")

            with ExitStack() as s5:
                e5 = s5.enter_context
                woT = e5(nc.sbuf_tensor("woT" + d, [128, 8 * L1_DI], BF16))
                wrow = e5(nc.sbuf_tensor("wrow" + d, [128, L1_DI], F32))
                wrq = e5(nc.sbuf_tensor("wrq" + d, [128, L1_DI], BF16))
                yin = e5(nc.sbuf_tensor("yin" + d, [128, 16 * 512], BF16))
                evo0 = e5(nc.sbuf_tensor("evo0" + d, [128, 512], BF16))
                evo1 = e5(nc.sbuf_tensor("evo1" + d, [128, 512], BF16))
                for rt in range(8):
                    P.dma(lambda q, d=d, rt=rt: q.dma_start(out=wrq[:, :],
                                                            in_=env["outw_h"][d][rt * 128:(rt + 1) * 128, :]),
                          waits=[("pe", P.cnt["pe"])], q="gpsimd", sem="so")
                    for g8 in range(2):
                        P.wait("tensor", "so", P.cnt["so"])
                        P.wait("tensor", "dv", P.cnt["dv"])
                        for u in range(8):
                            kk = g8 * 8 + u
                            P.t(lambda p, u=u, kk=kk: p.transpose(out=p_t2[:, u * 128:(u + 1) * 128],
                                                                  in_=wrq[:, kk * 128:(kk + 1) * 128],
                                                                  identity=identb[:, :]), sem="pe")
                        P.v(lambda v, rt=rt, g8=g8: v.tensor_copy(
                            out=woT[:, rt * L1_DI + g8 * 1024:rt * L1_DI + (g8 + 1) * 1024],
                            in_=p_t2[:, :]), waits=["pe"])
                ev_tok = [0, 0]
                st_tok = [0, 0]
                for tb in range(8):
                    P.dma(lambda q, d=d, tb=tb: q.dma_start(
                        out=yin[:, :].rearrange("p (k t) -> p k t", k=16),
                        in_=env["yd_sp"][d][:, tb * 512:(tb + 1) * 512].rearrange("(k p) t -> p k t", p=128)),
                        waits=[("pe", P.cnt["pe"]), ("so", P.cnt["so"])])
                    for ot in range(8):
                        bp = ot % 2
                        pm = p_c if bp == 0 else p_d
                        ev = evo0 if bp == 0 else evo1
                        P.wait("tensor", "ld", P.cnt["ld"])
                        P.wait("tensor", "dv", ev_tok[bp])
                        for k in range(16):
                            P.t(_mm(pm[:, :], woT[:, ot * L1_DI + k * 128:ot * L1_DI + (k + 1) * 128],
                                    yin[:, k * 512:(k + 1) * 512], k == 0, k == 15), sem="pe")
                        ev_tok[bp] = P.v(lambda v, ev=ev, pm=pm: v.tensor_copy(out=ev[:, :], in_=pm[:, :]),
                                         waits=["pe", ("so", st_tok[bp])])
                        st_tok[bp] = P.dma(lambda q, d=d, ev=ev, ot=ot, tb=tb: q.dma_start(
                            out=env["yp_sp"][d][ot * 128:(ot + 1) * 128, tb * 512:(tb + 1) * 512],
                            in_=ev[:, :]), waits=["dv"], q="gpsimd", sem="so")
            if debug and ("yp_" + d) in dbg:
                P.dma(lambda q, d=d: q.dma_start(out=dbg["yp_" + d], in_=env["yp_sp"][d]),
                      waits=[("so", P.cnt["so"])], q="gpsimd", sem="so")
        P.barrier()

    for d in "fb":
        _one_dir(d)


def _emit_l3_tail(nc, st, e, P, env):
    identb, ones1 = env["identb"], env["ones1"]
    debug, dbg = env["debug"], env["dbg"]
    pbcols, fbcols, dacol = env["pbcols"], env["fbcols"], env["dacol"]

    def _proj():
      with ExitStack() as s6:
        e6 = s6.enter_context
        pwT = e6(nc.sbuf_tensor("pwT", [128, 8 * 2048], BF16))
        wrow = e6(nc.sbuf_tensor("pw_row", [128, 2048], F32))
        wrq = e6(nc.sbuf_tensor("pw_rq", [128, 2048], BF16))
        yin = e6(nc.sbuf_tensor("pyin", [128, 16 * 512], BF16))
        evo0 = e6(nc.sbuf_tensor("pevo0", [128, 512], BF16))
        evo1 = e6(nc.sbuf_tensor("pevo1", [128, 512], BF16))
        p_a = e6(nc.psum_tensor("p6_a", [128, 512], F32))
        p_b = e6(nc.psum_tensor("p6_b", [128, 512], F32))
        p_t = e6(nc.psum_tensor("p6_t", [128, 1024], BF16))
        for rt in range(8):
            P.dma(lambda q, rt=rt: q.dma_start(out=wrq[:, :],
                                               in_=env["projw_h"][rt * 128:(rt + 1) * 128, :]),
                  waits=[("pe", P.cnt["pe"])], q="gpsimd", sem="so")
            for g8 in range(2):
                P.wait("tensor", "so", P.cnt["so"])
                P.wait("tensor", "dv", P.cnt["dv"])
                for u in range(8):
                    kk = g8 * 8 + u
                    P.t(lambda p, u=u, kk=kk: p.transpose(out=p_t[:, u * 128:(u + 1) * 128],
                                                          in_=wrq[:, kk * 128:(kk + 1) * 128],
                                                          identity=identb[:, :]), sem="pe")
                P.v(lambda v, rt=rt, g8=g8: v.tensor_copy(
                    out=pwT[:, rt * 2048 + g8 * 1024:rt * 2048 + (g8 + 1) * 1024], in_=p_t[:, :]),
                    waits=["pe"])
        ev_tok = [0, 0]
        st_tok = [0, 0]
        for tb in range(8):
            P.dma(lambda q, tb=tb: q.dma_start(
                out=yin[:, 0:8 * 512].rearrange("p (k t) -> p k t", k=8),
                in_=env["yp_sp"]["f"][:, tb * 512:(tb + 1) * 512].rearrange("(k p) t -> p k t", p=128)),
                waits=[("pe", P.cnt["pe"])])
            P.dma(lambda q, tb=tb: q.dma_start(
                out=yin[:, 8 * 512:16 * 512].rearrange("p (k t) -> p k t", k=8),
                in_=env["yp_sp"]["b"][:, tb * 512:(tb + 1) * 512].rearrange("(k p) t -> p k t", p=128)))
            for ot in range(8):
                bp = ot % 2
                pm = p_a if bp == 0 else p_b
                ev = evo0 if bp == 0 else evo1
                P.wait("tensor", "ld", P.cnt["ld"])
                P.wait("tensor", "dv", ev_tok[bp])
                for k in range(16):
                    P.t(_mm(pm[:, :], pwT[:, ot * 2048 + k * 128:ot * 2048 + (k + 1) * 128],
                            yin[:, k * 512:(k + 1) * 512], k == 0, k == 15), sem="pe")
                ev_tok[bp] = P.v(lambda v, pm=pm, ev=ev, ot=ot: v.tensor_scalar(
                    out=ev[:, :], in0=pm[:, :], scalar1=pbcols[:, ot:ot + 1], scalar2=0.0,
                    op0=ALU.add, op1=ALU.add), waits=["pe", ("so", st_tok[bp])], sem="dv")
                st_tok[bp] = P.dma(lambda q, ev=ev, ot=ot, tb=tb: q.dma_start(
                    out=env["yat_sp"][ot * 128:(ot + 1) * 128, tb * 512:(tb + 1) * 512], in_=ev[:, :]),
                    waits=["dv"], q="gpsimd", sem="so")
    _proj()
    if debug:
        P.dma(lambda q: q.dma_start(out=dbg["yat"], in_=env["yat_sp"]),
              waits=[("so", P.cnt["so"])], q="gpsimd", sem="so")
    P.barrier()

    def _ffn():
      with ExitStack() as s7:
        e7 = s7.enter_context
        fwT = e7(nc.sbuf_tensor("fwT", [128, 8 * 1024], BF16))
        xq1 = e7(nc.sbuf_tensor("xq1", [128, 8 * T], BF16))
        wrow = e7(nc.sbuf_tensor("fw_row", [128, 1024], F32))
        dwrow = e7(nc.sbuf_tensor("dw_row", [128, 32], F32))
        q_bf = e7(nc.sbuf_tensor("q_bf", [128, 1024], BF16))
        wa = e7(nc.sbuf_tensor("wa", [128, 1024], F32))
        xsb = e7(nc.sbuf_tensor("xsb", [128, 1024], F32))
        stp = e7(nc.sbuf_tensor("stp", [128, 1024], I32))
        stpf = e7(nc.sbuf_tensor("stpf", [128, 1024], F32))
        amax = e7(nc.sbuf_tensor("amax", [128, 32], F32))
        ascl = e7(nc.sbuf_tensor("ascl", [128, 32], F32))
        asci = e7(nc.sbuf_tensor("asci", [128, 32], F32))
        ei32 = e7(nc.sbuf_tensor("ei32", [128, 32], I32))
        xnin = e7(nc.sbuf_tensor("xnin", [128, 1024], BF16))
        g1tb = e7(nc.sbuf_tensor("g1tb", [128, 8 * 512], BF16))
        yatb = e7(nc.sbuf_tensor("yatb", [128, 8 * 512], BF16))
        xnfb = e7(nc.sbuf_tensor("xnfb", [128, 8 * 512], BF16))
        s1b = e7(nc.sbuf_tensor("s1b", [128, 8 * 512], F32))
        s1g = e7(nc.sbuf_tensor("s1g", [128, 8 * 512], BF16))
        s1gt = e7(nc.sbuf_tensor("s1gt", [128, 4 * D], BF16))
        x1b = e7(nc.sbuf_tensor("x1b", [128, 4 * D], F32))
        x2b = e7(nc.sbuf_tensor("x2b", [128, 4 * D], F32))
        p_a = e7(nc.psum_tensor("p7_a", [128, 512], F32))
        p_b = e7(nc.psum_tensor("p7_b", [128, 512], F32))
        p_t = e7(nc.psum_tensor("p7_t", [128, 1024], BF16))

        def mxfp4(src_ap, dcol_ap, out_ap, blocks):
            P.s(lambda s: s.activation(out=wa[:, :], in_=src_ap, func=AF.Abs),
                waits=["ld", ("dv", P.cnt["dv"])])
            wv = wa.ap().rearrange("p (b i) -> p b i", b=32)
            for wdt in (16, 8, 4, 2, 1):
                P.v(lambda v, wv=wv, wdt=wdt: v.tensor_tensor(
                    out=wv[:, :, 0:wdt], in0=wv[:, :, 0:wdt], in1=wv[:, :, wdt:2 * wdt], op=ALU.max),
                    waits=["ac"])
            P.v(lambda v, wv=wv: v.tensor_copy(out=amax[:, :], in_=wv[:, :, 0].squeeze(2)
                                               if False else wv[:, :, 0:1].squeeze(2)))
            P.v(lambda v: v.tensor_scalar(out=ei32[:, :], in0=amax[:, :].bitcast(I32), scalar1=1,
                                          scalar2=24, op0=ALU.logical_shift_left,
                                          op1=ALU.logical_shift_right))
            P.v(lambda v: v.tensor_scalar(out=ascl[:, :], in0=ei32[:, :], scalar1=129.0, scalar2=-124.0,
                                          op0=ALU.subtract, op1=ALU.max))
            if dcol_ap is not None:
                P.v(lambda v: v.tensor_scalar(out=ascl[:, :], in0=ascl[:, :], scalar1=dcol_ap,
                                              scalar2=None if False else 0.0, op0=ALU.add, op1=ALU.add))
            else:
                P.v(lambda v: v.tensor_tensor(out=ascl[:, :], in0=ascl[:, :], in1=dwrow[:, :], op=ALU.add))
            P.s(lambda s: s.activation(out=ascl[:, :], in_=ascl[:, :], func=AF.Exp,
                                       scale=0.6931471805599453), waits=["dv"])
            P.v(lambda v: v.reciprocal(out=asci[:, :], in_=ascl[:, :]), waits=["ac"])
            av = asci.ap().unsqueeze(2).broadcast_to([128, 32, 32])
            P.v(lambda v, src_ap=src_ap, av=av: v.tensor_tensor(
                out=xsb[:, :].rearrange("p (b i) -> p b i", b=32),
                in0=src_ap.rearrange("p (b i) -> p b i", b=32), in1=av, op=ALU.mult))
            P.v(lambda v: v.tensor_scalar(out=xsb[:, :], in0=xsb[:, :], scalar1=6.0, scalar2=-6.0,
                                          op0=ALU.min, op1=ALU.max))
            P.v(lambda v: v.tensor_scalar(out=stp[:, :], in0=xsb[:, :].bitcast(I32), scalar1=1,
                                          scalar2=24, op0=ALU.logical_shift_left,
                                          op1=ALU.logical_shift_right))
            P.v(lambda v: v.tensor_scalar(out=stp[:, :], in0=stp[:, :], scalar1=127, scalar2=0,
                                          op0=ALU.subtract, op1=ALU.max))
            P.v(lambda v: v.tensor_scalar(out=stp[:, :], in0=stp[:, :], scalar1=2, scalar2=0,
                                          op0=ALU.min, op1=ALU.max))
            P.v(lambda v: v.tensor_scalar(out=ei32[:, 0:32], in0=stp[:, 0:32], scalar1=0, scalar2=0,
                                          op0=ALU.add, op1=ALU.add)) if False else None
            P.v(lambda v: v.tensor_scalar(out=stpf[:, :].bitcast(I32), in0=stp[:, :], scalar1=-1,
                                          scalar2=128, op0=ALU.mult, op1=ALU.add))
            P.v(lambda v: v.tensor_scalar(out=stpf[:, :].bitcast(I32), in0=stpf[:, :].bitcast(I32),
                                          scalar1=23, scalar2=0, op0=ALU.logical_shift_left,
                                          op1=ALU.logical_shift_left))
            P.v(lambda v: v.tensor_tensor(out=xsb[:, :], in0=xsb[:, :], in1=stpf[:, :], op=ALU.mult))
            P.v(lambda v: v.tensor_scalar(out=xsb[:, :], in0=xsb[:, :], scalar1=MAGIC, scalar2=MAGIC,
                                          op0=ALU.add, op1=ALU.subtract))
            P.v(lambda v: v.tensor_scalar(out=stp[:, :], in0=stp[:, :], scalar1=126, scalar2=23,
                                          op0=ALU.add, op1=ALU.arith_shift_left)
                if False else v.tensor_scalar(out=stp[:, :], in0=stp[:, :], scalar1=126, scalar2=0,
                                              op0=ALU.add, op1=ALU.add))
            P.v(lambda v: v.tensor_scalar(out=stp[:, :], in0=stp[:, :], scalar1=23, scalar2=0,
                                          op0=ALU.logical_shift_left, op1=ALU.logical_shift_left))
            P.v(lambda v: v.tensor_tensor(out=xsb[:, :], in0=xsb[:, :], in1=stp[:, :].bitcast(F32),
                                          op=ALU.mult))
            sv = ascl.ap().unsqueeze(2).broadcast_to([128, 32, 32])
            P.v(lambda v, sv=sv, out_ap=out_ap: v.tensor_tensor(
                out=out_ap.rearrange("p (b i) -> p b i", b=32),
                in0=xsb[:, :].rearrange("p (b i) -> p b i", b=32), in1=sv, op=ALU.mult))

        for rt in range(8):
            P.dma(lambda q, rt=rt: q.dma_start(out=wrow[:, :], in_=env["fw_h"][rt * 128:(rt + 1) * 128, :]),
                  waits=[("dv", P.cnt["dv"]), ("pe", P.cnt["pe"])])
            P.dma(lambda q, rt=rt: q.dma_start(out=dwrow[:, :], in_=env["dw_h"][rt * 128:(rt + 1) * 128, :]))
            P.s(lambda s: s.activation(out=dwrow[:, :], in_=dwrow[:, :], func=AF.Tanh, scale=4.0),
                waits=["ld"])
            P.v(lambda v: v.tensor_scalar(out=dwrow[:, :], in0=dwrow[:, :], scalar1=0.5, scalar2=0.0,
                                          op0=ALU.mult, op1=ALU.add), waits=["ac"])
            mxfp4(wrow.ap()[:, :], None, q_bf.ap()[:, :], 32)
            P.wait("tensor", "dv", P.cnt["dv"])
            for u in range(8):
                P.t(lambda p, u=u: p.transpose(out=p_t[:, u * 128:(u + 1) * 128],
                                               in_=q_bf[:, u * 128:(u + 1) * 128],
                                               identity=identb[:, :]), sem="pe")
            P.v(lambda v, rt=rt: v.tensor_copy(out=fwT[:, rt * 1024:(rt + 1) * 1024], in_=p_t[:, :]),
                waits=["pe"])
        for tt in range(32):
            P.dma(lambda q, tt=tt: q.dma_start(out=xnin[:, :],
                                               in_=env["xn1t_sp"][tt * 128:(tt + 1) * 128, :]),
                  waits=[("dv", P.cnt["dv"]), ("pe", P.cnt["pe"])])
            mxfp4(xnin.ap()[:, :], dacol.ap()[:, 0:1], q_bf.ap()[:, :], 32)
            P.wait("tensor", "dv", P.cnt["dv"])
            for u in range(8):
                P.t(lambda p, u=u: p.transpose(out=p_t[:, u * 128:(u + 1) * 128],
                                               in_=q_bf[:, u * 128:(u + 1) * 128],
                                               identity=identb[:, :]), sem="pe")
            xqv = xq1.ap().rearrange("p (k t) -> p k t", k=8)
            P.v(lambda v, tt=tt, xqv=xqv: v.tensor_copy(
                out=xqv[:, :, tt * 128:(tt + 1) * 128],
                in_=p_t[:, :].rearrange("p (k t) -> p k t", k=8)), waits=["pe"])
        s1_tok = 0
        x2_tok = 0
        x2dma_tok = 0
        for tb in range(8):
            for ot in range(8):
                pm = p_a if ot % 2 == 0 else p_b
                P.wait("tensor", "dv", P.cnt["dv"])
                P.wait("tensor", "ac", P.cnt["ac"])
                for k in range(8):
                    P.t(_mm(pm[:, :], fwT[:, ot * 1024 + k * 128:ot * 1024 + (k + 1) * 128],
                            xq1[:, k * T + tb * 512:k * T + (tb + 1) * 512], k == 0, k == 7), sem="pe")
                P.s(lambda s, pm=pm, ot=ot: s.activation(out=g1tb[:, ot * 512:(ot + 1) * 512],
                                                         in_=pm[:, :], func=AF.Silu,
                                                         bias=fbcols[:, ot:ot + 1]), waits=["pe"])
            if debug:
                P.dma(lambda q, tb=tb: q.dma_start(out=dbg["g1"][:, tb * 512:(tb + 1) * 512]
                                                   .rearrange("(k p) t -> p k t", p=128),
                                                   in_=g1tb[:, :].rearrange("p (k t) -> p k t", k=8)),
                      waits=["ac"], q="gpsimd", sem="so")
            P.dma(lambda q, tb=tb: q.dma_start(
                out=yatb[:, :].rearrange("p (k t) -> p k t", k=8),
                in_=env["yat_sp"][:, tb * 512:(tb + 1) * 512].rearrange("(k p) t -> p k t", p=128)),
                waits=[("dv", s1_tok)])
            P.dma(lambda q, tb=tb: q.dma_start(
                out=xnfb[:, :].rearrange("p (k t) -> p k t", k=8),
                in_=env["xn1f_sp"][:, tb * 512:(tb + 1) * 512].rearrange("(k p) t -> p k t", p=128)))
            s1_tok = P.v(lambda v: v.tensor_tensor(out=s1b[:, :], in0=xnfb[:, :], in1=yatb[:, :],
                                                   op=ALU.add), waits=["ld", "ac"])
            P.v(lambda v: v.tensor_tensor(out=s1g[:, :], in0=s1b[:, :], in1=g1tb[:, :], op=ALU.mult))
            for g8 in range(4):
                P.wait("tensor", "dv", P.cnt["dv"])
                for k in range(8):
                    P.t(lambda p, k=k, g8=g8: p.transpose(
                        out=p_t[:, k * 128:(k + 1) * 128],
                        in_=s1g[:, k * 512 + g8 * 128:k * 512 + (g8 + 1) * 128],
                        identity=identb[:, :]), sem="pe")
                P.v(lambda v, g8=g8: v.tensor_copy(out=s1gt[:, g8 * D:(g8 + 1) * D], in_=p_t[:, :]),
                    waits=["pe"])
            P.dma(lambda q, tb=tb: q.dma_start(out=x1b[:, :].rearrange("p (s d) -> p s d", s=4),
                                               in_=env["x1_sp"][tb * 512:(tb + 1) * 512, :]
                                               .rearrange("(s p) d -> p s d", p=128)),
                  waits=[("dv", x2_tok)])
            x2_tok = P.v(lambda v: v.tensor_tensor(out=x2b[:, :], in0=x1b[:, :], in1=s1gt[:, :],
                                                   op=ALU.add),
                         waits=["ld", "dv", ("so", x2dma_tok)])
            x2dma_tok = P.dma(lambda q, tb=tb: q.dma_start(out=env["x2_h"][tb * 512:(tb + 1) * 512, :]
                                               .rearrange("(s p) d -> p s d", p=128),
                                               in_=x2b[:, :].rearrange("p (s d) -> p s d", s=4)),
                  waits=["dv"], q="gpsimd", sem="so")
    _ffn()



_NC_CACHE = {}


def _get_ncs():
    if "ncs" not in _NC_CACHE:
        _NC_CACHE["ncs"] = (build_launch1(), build_launch2(), build_launch3())
    return _NC_CACHE["ncs"]


def kernel(**inp):
    B = 8
    nc1, nc2, nc3 = _get_ncs()
    f32 = lambda a: np.ascontiguousarray(np.asarray(a), dtype=np.float32)

    x0 = f32(inp["L0_attn_hidden_states"])
    im1 = []
    for b in range(B):
        im1.append(dict(
            x0=x0[b],
            w_in=f32(inp["L0_attn_W_in"]),
            conv_w=f32(inp["L0_attn_conv_w"]),
            conv_b=f32(inp["L0_attn_conv_b"]),
            dt_bias=f32(inp["L0_attn_dt_bias"]),
            a_log=f32(inp["L0_attn_A_log"]),
            d_skip=f32(inp["L0_attn_D"]).reshape(1, -1),
            norm_w=f32(inp["L0_attn_norm_w"]).reshape(1, -1),
            w_out=f32(inp["L0_attn_W_out"]),
        ))
    r1 = run_bass_kernel_spmd(nc1, im1, core_ids=list(range(B))).results

    xmax8 = np.concatenate([np.asarray(r["xmax"]).reshape(1, 1) for r in r1], axis=1)
    im2 = []
    for b in range(B):
        im2.append(dict(
            xn_fm=np.asarray(r1[b]["xn_fm"]),
            xmax8=f32(xmax8),
            clip_x=f32(inp["L0_ffn_clip_x"]).reshape(1, 1),
            w_gate=f32(inp["L0_ffn_w_gate"]),
            w_up=f32(inp["L0_ffn_w_up"]),
        ))
    r2 = run_bass_kernel_spmd(nc2, im2, core_ids=list(range(B))).results

    hmax8 = np.concatenate([np.asarray(r["hmax"]).reshape(1, 1) for r in r2], axis=1)
    im3 = []
    for b in range(B):
        m = dict(
            x0=x0[b],
            s0_fm=np.asarray(r1[b]["s0_fm"]),
            hpre_fm=np.asarray(r2[b]["hpre_fm"]),
            hmax8=f32(hmax8),
            clip_down=f32(inp["L0_ffn_clip_down"]).reshape(1, 1),
            w_down=f32(inp["L0_ffn_w_down"]),
            ln_gamma=f32(inp["L1_attn_ln_gamma"]).reshape(1, -1),
            ln_beta=f32(inp["L1_attn_ln_beta"]).reshape(1, -1),
            proj_w=f32(inp["L1_attn_proj_w"]),
            proj_b=f32(inp["L1_attn_proj_b"]).reshape(1, -1),
            ffn_w=f32(inp["L1_ffn_weight"]),
            ffn_b=f32(inp["L1_ffn_bias"]).reshape(1, -1),
            delta_w=f32(inp["L1_ffn_delta_w"]),
            delta_act=f32(inp["L1_ffn_delta_act"]).reshape(1, 1),
        )
        for d in "fb":
            m[f"in_w_{d}"] = f32(inp[f"L1_attn_in_w_{d}"])
            m[f"conv_w_{d}"] = f32(inp[f"L1_attn_conv_w_{d}"]).reshape(L1_DI, 4)
            m[f"conv_b_{d}"] = f32(inp[f"L1_attn_conv_b_{d}"])
            m[f"xp_w_{d}"] = f32(inp[f"L1_attn_xp_w_{d}"])
            m[f"dtp_w_{d}"] = f32(inp[f"L1_attn_dtp_w_{d}"])
            m[f"dtp_b_{d}"] = f32(inp[f"L1_attn_dtp_b_{d}"])
            m[f"a_log_{d}"] = f32(inp[f"L1_attn_A_log_{d}"])
            m[f"d_{d}"] = f32(inp[f"L1_attn_D_{d}"])
            m[f"out_w_{d}"] = f32(inp[f"L1_attn_out_w_{d}"])
        im3.append(m)
    r3 = run_bass_kernel_spmd(nc3, im3, core_ids=list(range(B))).results

    out = np.stack([np.asarray(r["x2"], np.float32) for r in r3], axis=0)
    return out
